# version 02
# baseline (speedup 1.0000x reference)
import numpy as np
import ml_dtypes
import concourse.bass as bass
import concourse.mybir as mybir

F32 = mybir.dt.float32
BF16 = mybir.dt.bfloat16
I32 = mybir.dt.int32
AF = mybir.ActivationFunctionType
ALU = mybir.AluOpType
AXL = mybir.AxisListType

ENGINES = ("sync", "scalar", "vector", "tensor", "gpsimd")


class Prog:
    def __init__(self):
        self.nc = bass.Bass("TRN2", target_bir_lowering=False)
        self.ops = {e: [] for e in ENGINES}
        self.sems = {}
        self.cnt = {}
        self._const_cols = {}
        self._finalized = False

    def sem(self, name):
        if name not in self.sems:
            self.sems[name] = self.nc.alloc_semaphore(name)
            self.cnt[name] = 0
        return self.sems[name]

    def bump(self, name, k):
        self.sem(name)
        self.cnt[name] += k
        return self.cnt[name]

    def _inc_of(self, sem, k):
        if sem is None:
            return None, None
        tgt = self.bump(sem, k)
        return self.sems[sem], tgt

    def raw(self, eng, fn):
        self.ops[eng].append(fn)

    def wait(self, eng, sem, val):
        if val is None or val <= 0:
            return
        s = self.sem(eng) if False else self.sem(sem)
        self.ops[eng].append(lambda q, _s=s, _v=val: q.wait_ge(_s, _v))

    def waits(self, eng, wl):
        for (sem, val) in wl:
            self.wait(eng, sem, val)

    NSLOT = 6

    def dma(self, eng, out, in_, stream, waits=()):
        key = ("dmaslot", stream)
        idx = self._dma_idx = getattr(self, "_dma_idx", {})
        i = idx.get(key, 0)
        idx[key] = i + 1
        sname = f"D{stream}{i % self.NSLOT}"
        self.sem(sname)
        prev = self.cnt[sname]
        wl = list(waits)
        if prev:
            wl.append((sname, prev))
        sh, tgt = self._inc_of(sname, 16)
        self.waits(eng, wl)
        self.ops[eng].append(
            lambda q, _o=out, _i=in_, _s=sh: q.dma_start(out=_o, in_=_i).then_inc(_s, 16))
        return (sname, tgt)

    def stream_tokens(self, stream):
        toks = []
        for s in range(self.NSLOT):
            n = f"D{stream}{s}"
            if self.cnt.get(n, 0):
                toks.append((n, self.cnt[n]))
        return toks

    def wait_stream(self, eng, stream):
        self.waits(eng, self.stream_tokens(stream))

    def mm(self, psum, lhsT, rhs, start, stop, sem=None, waits=()):
        self.waits("tensor", waits)
        sh, tgt = self._inc_of(sem, 1)
        if sh is None:
            self.ops["tensor"].append(
                lambda q, _p=psum, _l=lhsT, _r=rhs, _a=start, _b=stop:
                q.matmul(_p, _l, _r, start=_a, stop=_b))
        else:
            self.ops["tensor"].append(
                lambda q, _p=psum, _l=lhsT, _r=rhs, _a=start, _b=stop, _s=sh:
                q.matmul(_p, _l, _r, start=_a, stop=_b).then_inc(_s, 1))
        return tgt

    def transpose(self, psum, in_, ident, sem=None, waits=()):
        self.waits("tensor", waits)
        sh, tgt = self._inc_of(sem, 1)
        if sh is None:
            self.ops["tensor"].append(
                lambda q, _p=psum, _i=in_, _d=ident: q.transpose(_p, _i, _d))
        else:
            self.ops["tensor"].append(
                lambda q, _p=psum, _i=in_, _d=ident, _s=sh:
                q.transpose(_p, _i, _d).then_inc(_s, 1))
        return tgt

    def _eop(self, eng, build, sem=None, waits=(), inc=1):
        self.waits(eng, waits)
        sh, tgt = self._inc_of(sem, inc)
        if sh is None:
            self.ops[eng].append(build)
        else:
            self.ops[eng].append(lambda q, _b=build, _s=sh, _i=inc: _b(q).then_inc(_s, _i))
        return tgt

    def tt(self, out, a, b, alu, sem=None, waits=(), eng="vector"):
        return self._eop(eng, lambda q, _o=out, _a=a, _b=b, _u=alu:
                         q.tensor_tensor(_o, _a, _b, _u), sem, waits)

    def tsc(self, out, a, s1, s2, op0, op1=None, sem=None, waits=(), eng="vector"):
        if op1 is None:
            return self._eop(eng, lambda q, _o=out, _a=a, _s=s1, _u=op0:
                             q.tensor_scalar(_o, _a, _s, None, _u), sem, waits)
        return self._eop(eng, lambda q, _o=out, _a=a, _s=s1, _t=s2, _u=op0, _v=op1:
                         q.tensor_scalar(_o, _a, _s, _t, _u, _v), sem, waits)

    def stt(self, out, in0, scalar, in1, op0, op1, sem=None, waits=(), eng="vector"):
        return self._eop(eng, lambda q, _o=out, _a=in0, _s=scalar, _b=in1, _u=op0, _v=op1:
                         q.scalar_tensor_tensor(_o, _a, _s, _b, _u, _v), sem, waits)

    def copy(self, out, a, sem=None, waits=(), eng="vector"):
        return self._eop(eng, lambda q, _o=out, _a=a: q.tensor_copy(_o, _a), sem, waits)

    def memset(self, out, v, sem=None, waits=(), eng="vector"):
        return self._eop(eng, lambda q, _o=out, _v=v: q.memset(_o, _v), sem, waits)

    def reduce(self, out, a, alu, negate=False, sem=None, waits=(), eng="vector"):
        return self._eop(eng, lambda q, _o=out, _a=a, _u=alu, _n=negate:
                         q.tensor_reduce(_o, _a, AXL.X, _u, negate=_n), sem, waits)

    def recip(self, out, a, sem=None, waits=()):
        return self._eop("vector", lambda q, _o=out, _a=a: q.reciprocal(_o, _a), sem, waits)

    def select(self, out, mask, t_, f_, sem=None, waits=()):
        return self._eop("vector", lambda q, _o=out, _m=mask, _t=t_, _f=f_:
                         q.select(_o, _m, _t, _f), sem, waits)

    def scan(self, out, d0, d1, init, op0, op1, sem=None, waits=()):
        return self._eop("vector", lambda q, _o=out, _a=d0, _b=d1, _i=init, _u=op0, _v=op1:
                         q.tensor_tensor_scan(_o, _a, _b, _i, _u, _v), sem, waits)

    def act(self, out, a, fn, bias=0.0, scale=1.0, sem=None, waits=(), accum_out=None):
        if isinstance(bias, float):
            self.fconst(bias)
        if isinstance(scale, float):
            self.fconst(scale)
        if accum_out is None:
            return self._eop("scalar", lambda q, _o=out, _a=a, _f=fn, _b=bias, _s=scale:
                             q.activation(_o, _a, _f, bias=_b, scale=_s), sem, waits)
        return self._eop("scalar", lambda q, _o=out, _a=a, _f=fn, _b=bias, _s=scale, _x=accum_out:
                         q.activation(_o, _a, _f, bias=_b, scale=_s, accum_out=_x), sem, waits)

    def fconst(self, value):
        if value not in self._const_cols:
            assert not self._finalized
            self._const_cols[value] = len(self._const_cols)
        return value

    def init_consts(self, *values):
        for v in values:
            self.fconst(v)

    def finalize(self):
        assert not self._finalized
        self._finalized = True
        nc = self.nc
        pre = {e: [] for e in ENGINES}
        if self._const_cols:
            vals = sorted(self._const_cols, key=self._const_cols.get)
            arr = np.tile(np.array([vals], dtype=np.float32), (128, 1))
            th = nc.inline_tensor(arr.astype(np.float32), name="constvals")
            tile = nc.alloc_sbuf_tensor("constsb", [128, len(vals)], F32)
            for v in vals:
                c = self._const_cols[v]
                nc.const_aps.register(F32, float(v), tile[:, c:c + 1])
            cs = self.sem("cinit")
            pre["sync"].append(lambda q, _t=tile, _a=th.ap(), _s=cs:
                               q.dma_start(out=_t[:, :], in_=_a).then_inc(_s, 16))
            for e in ("scalar",):
                pre[e].append(lambda q, _s=cs: q.wait_ge(_s, 16))
        with nc.Block() as block:
            for eng in ENGINES:
                lst = pre[eng] + self.ops[eng]
                if lst:
                    getattr(block, eng)(lambda q, _l=tuple(lst): [f(q) for f in _l])
        return nc


def tsl(i, n=128):
    return slice(i * n, (i + 1) * n)


"""Section emitters for the megakernel (dev copies; final versions get pasted
into kernel.py)."""

T = 1024
D = 2048
DK = D // 128


def glu_env(P, *, icmax=2, w2slot=6144, hmax=16, acc=None, xnb=None):
    nc = P.nc
    env = {}
    env["xnb"] = xnb if xnb is not None else nc.alloc_sbuf_tensor("xnb", [128, DK * T], BF16)
    env["ICMAX"] = icmax
    env["W1SLOT"] = 16 * icmax * 128
    env["W2SLOT"] = w2slot
    env["w1b"] = nc.alloc_sbuf_tensor("w1b", [128, 2 * env["W1SLOT"]], BF16)
    env["w3b"] = nc.alloc_sbuf_tensor("w3b", [128, 2 * env["W1SLOT"]], BF16)
    env["w2b"] = nc.alloc_sbuf_tensor("w2b", [128, 2 * env["W2SLOT"]], BF16)
    env["h"] = nc.alloc_sbuf_tensor("hbuf", [128, hmax * T], BF16)
    env["sil"] = nc.alloc_sbuf_tensor("sil", [128, 2 * 512], F32)
    env["tmp"] = nc.alloc_sbuf_tensor("glutmp", [128, 2 * 512], F32)
    env["acc"] = acc if acc is not None else nc.alloc_sbuf_tensor("accb", [128, DK * T], F32)
    env["psum"] = [nc.alloc_psum_tensor(f"pb{i}", [128, 512], F32) for i in range(6)]
    env["slotuse"] = {}
    env["bankfree"] = {}
    env["silfree"] = {}
    return env


def emit_glu_ffn(P, env, *, w1_ap, w3_ap, w2_ap, n_i, c_col=None, tag,
                 i_chunk=2, d_chunk=4, first_into_acc=False):
    xnb = env["xnb"]; w1b = env["w1b"]; w3b = env["w3b"]; w2b = env["w2b"]
    hbuf = env["h"]; sil = env["sil"]; tmp = env["tmp"]; acc = env["acc"]
    pb = env["psum"]; slotuse = env["slotuse"]; bankfree = env["bankfree"]
    silfree = env["silfree"]
    W1SLOT = env["W1SLOT"]; W2SLOT = env["W2SLOT"]
    assert i_chunk <= env["ICMAX"] and n_i * T <= hbuf.shape[1]

    chunks = [list(range(c0, min(n_i, c0 + i_chunk))) for c0 in range(0, n_i, i_chunk)]
    dchunks = [list(range(c0, min(DK, c0 + d_chunk))) for c0 in range(0, DK, d_chunk)]
    assert n_i * d_chunk * 128 <= W2SLOT

    gidx = 0
    for ci, chunk in enumerate(chunks):
        cs = len(chunk)
        slot = ci % 2
        dcnt = {}
        for name, wb, src in (("w1", w1b, w1_ap), ("w3", w3b, w3_ap)):
            key = ("A" + name, slot)
            wl = []
            if key in slotuse:
                wl.append(("pe", slotuse[key]))
            dcnt[name] = P.dma(
                "gpsimd",
                wb[:, slot * W1SLOT: slot * W1SLOT + 16 * cs * 128].rearrange(
                    "p (k c) -> p k c", k=16),
                src[:, chunk[0] * 128:(chunk[0] + cs) * 128].rearrange(
                    "(k p) c -> p k c", p=128),
                "w", waits=wl)
        for li, gi in enumerate(chunk):
            for half in range(2):
                par = gidx % 2
                bA, bB = pb[2 * par], pb[2 * par + 1]
                iA, iB = 2 * par, 2 * par + 1
                gidx += 1
                wl = [dcnt["w1"], dcnt["w3"]]
                for ib in (iA, iB):
                    if ib in bankfree:
                        wl.append(bankfree.pop(ib))
                first = True
                for k in range(DK):
                    off = slot * W1SLOT + k * cs * 128 + li * 128
                    rhs = xnb[:, k * T + half * 512: k * T + (half + 1) * 512]
                    P.mm(bA[:, :], w1b[:, off:off + 128], rhs,
                         start=(k == 0), stop=(k == DK - 1),
                         waits=wl if first else ())
                    first = False
                    P.mm(bB[:, :], w3b[:, off:off + 128], rhs,
                         start=(k == 0), stop=(k == DK - 1),
                         sem="pe" if k == DK - 1 else None)
                pecnt = P.cnt["pe"]
                slotuse[("Aw1", slot)] = pecnt
                slotuse[("Aw3", slot)] = pecnt
                ssl = gidx % 2
                swl = [("pe", pecnt)]
                if ssl in silfree:
                    swl.append(silfree.pop(ssl))
                acnt = P.act(sil[:, ssl * 512:(ssl + 1) * 512], bA[:, :], AF.Silu,
                             sem="act", waits=swl)
                hdst = hbuf[:, gi * T + half * 512: gi * T + (half + 1) * 512]
                hg = [env["h_guard"]] if "h_guard" in env else []
                if c_col is None:
                    dcnt2 = P.tt(hdst, sil[:, ssl * 512:(ssl + 1) * 512], bB[:, :],
                                 ALU.mult, sem="dve", waits=[("act", acnt)] + hg)
                else:
                    t1 = P.tt(tmp[:, ssl * 512:(ssl + 1) * 512], bB[:, :],
                              c_col[:, half * 512:(half + 1) * 512], ALU.mult,
                              sem="dve", waits=[("pe", pecnt)])
                    dcnt2 = P.tt(hdst, sil[:, ssl * 512:(ssl + 1) * 512],
                                 tmp[:, ssl * 512:(ssl + 1) * 512],
                                 ALU.mult, sem="dve",
                                 waits=[("act", acnt), ("dve", t1)] + hg)
                bankfree[iA] = ("act", acnt)
                bankfree[iB] = ("dve", dcnt2)
                silfree[ssl] = ("dve", dcnt2)
    env.pop("h_guard", None)
    h_ready = P.cnt["dve"]

    emit_phaseB(P, env, w2_ap=w2_ap, n_i=n_i, rhs_buf=hbuf,
                d_chunk=d_chunk, first_into_acc=first_into_acc,
                hready=("dve", h_ready))


def emit_moe_router(P, env, *, wg_ap, call_tile, ident_sb, tag="rt"):
    nc = P.nc
    xnb = env["xnb"]
    pb = env["psum"]
    wgt = nc.alloc_sbuf_tensor(f"wgt{tag}", [128, DK * 16], BF16)
    sc = nc.alloc_sbuf_tensor(f"sc{tag}", [128, 16], F32)
    msk = nc.alloc_sbuf_tensor(f"msk{tag}", [128, 16], F32)
    comb = nc.alloc_sbuf_tensor(f"comb{tag}", [128, 16], F32)
    eqt = nc.alloc_sbuf_tensor(f"eq{tag}", [128, 16], F32)
    gs = nc.alloc_sbuf_tensor(f"gs{tag}", [128, 4], F32)
    gmb = nc.alloc_sbuf_tensor(f"gmb{tag}", [128, 4], F32)
    red = nc.alloc_sbuf_tensor(f"red{tag}", [128, 6], F32)

    wcnt = P.dma("gpsimd", wgt[:, :].rearrange("p (k c) -> p k c", k=DK),
                 wg_ap.rearrange("(k p) c -> p k c", p=128), "w")
    for tt in range(8):
        wl = [wcnt]
        if "rtbank0" in env:
            wl.append(env.pop("rtbank0"))
        for k in range(DK):
            P.mm(pb[0][:, 0:16], xnb[:, k * T + tt * 128: k * T + (tt + 1) * 128],
                 wgt[:, k * 16:(k + 1) * 16],
                 start=(k == 0), stop=(k == DK - 1),
                 waits=wl if k == 0 else (), sem="pe" if k == DK - 1 else None)
        pc = P.cnt["pe"]
        a1 = P.act(sc[:, :], pb[0][:, 0:16], AF.Exp, sem="act", waits=[("pe", pc)])
        env["rtbank0"] = ("act", a1)
        d1 = P.reduce(red[:, 0:1], sc[:, :], ALU.add, sem="dve", waits=[("act", a1)])
        d2 = P.recip(red[:, 1:2], red[:, 0:1], sem="dve", waits=[("dve", d1)])
        for g in range(4):
            P.reduce(gs[:, g:g + 1], sc[:, 4 * g:4 * (g + 1)], ALU.max,
                     sem="dve", waits=[("act", a1)])
        d3 = P.cnt["dve"]
        d4 = P.reduce(red[:, 2:3], gs[:, :], ALU.max, sem="dve", waits=[("dve", d3)])
        e1 = P.tsc(gmb[:, :], gs[:, :], red[:, 2:3], None, ALU.is_equal,
                   sem="dve", waits=[("dve", d4)])
        f1 = P.stt(gs[:, :], gmb[:, :], -1e30, gs[:, :], ALU.mult, ALU.add,
                   sem="dve", waits=[("dve", e1)])
        f2 = P.reduce(red[:, 3:4], gs[:, :], ALU.max, sem="dve", waits=[("dve", f1)])
        f3 = P.tsc(gs[:, :], gs[:, :], red[:, 3:4], None, ALU.is_equal,
                   sem="dve", waits=[("dve", f2)])
        f4 = P.tt(gmb[:, :], gmb[:, :], gs[:, :], ALU.add, sem="dve", waits=[("dve", f3)])
        f5 = P.tsc(gmb[:, :], gmb[:, :], 1e30, -1e30, ALU.mult, ALU.add,
                   sem="dve", waits=[("dve", f4)])
        for g in range(4):
            P.tsc(msk[:, 4 * g:4 * (g + 1)], sc[:, 4 * g:4 * (g + 1)],
                  gmb[:, g:g + 1], None, ALU.add, sem="dve", waits=[("dve", f5)])
        for r in range(4):
            r1 = P.reduce(red[:, 4:5], msk[:, :], ALU.max, sem="dve",
                          waits=[("dve", P.cnt["dve"])])
            r3 = P.tsc(eqt[:, :], msk[:, :], red[:, 4:5], None, ALU.is_equal,
                       sem="dve", waits=[("dve", r1)])
            if r == 0:
                r4 = P.tsc(comb[:, :], eqt[:, :], red[:, 4:5], None, ALU.mult,
                           sem="dve", waits=[("dve", r3)])
            else:
                r4 = P.stt(comb[:, :], eqt[:, :], red[:, 4:5], comb[:, :],
                           ALU.mult, ALU.add, sem="dve", waits=[("dve", r3)])
            P.stt(msk[:, :], eqt[:, :], -1e30, msk[:, :],
                  ALU.mult, ALU.add, sem="dve", waits=[("dve", r4)])
        s1 = P.tsc(comb[:, :], comb[:, :], red[:, 1:2], None, ALU.mult,
                   sem="dve", waits=[("dve", P.cnt["dve"])])
        wl = [("dve", s1)]
        if "rtbank1" in env:
            wl.append(env.pop("rtbank1"))
        tcnt = P.transpose(pb[1][0:16, 0:128], comb[:, 0:16], ident_sb[:, :],
                           sem="pe", waits=wl)
        c1 = P.copy(call_tile[0:16, tt * 128:(tt + 1) * 128], pb[1][0:16, 0:128],
                    sem="dve", waits=[("pe", tcnt)])
        env["rtbank1"] = ("dve", c1)
    return P.cnt["dve"]


def emit_launchA(P, *, xs_ap, wf_ap, wi_ap, wv_ap, xt_out, xn_out, f_out, g_out,
                 ap_out, wvt_ap=None, rw_ap=None, waux_ap=None, wvr_out=None):
    nc = P.nc
    xnat = nc.alloc_sbuf_tensor("xnat", [128, 4 * D], F32)
    xntok = nc.alloc_sbuf_tensor("xntok", [128, 4 * D], BF16)
    sqt = nc.alloc_sbuf_tensor("sqt", [128, D], F32)
    stat = nc.alloc_sbuf_tensor("stat", [128, 4], F32)
    xTa = nc.alloc_sbuf_tensor("xTa", [128, DK * 512], F32)
    xnb = nc.alloc_sbuf_tensor("xnbA", [128, DK * T], BF16)
    wfb = nc.alloc_sbuf_tensor("wfb", [128, 2 * 16 * 2 * 128], BF16)
    wib = nc.alloc_sbuf_tensor("wib", [128, 2 * 16 * 2 * 128], BF16)
    wvb = nc.alloc_sbuf_tensor("wvb", [128, 2 * 16 * 2 * 128], BF16)
    ftile = nc.alloc_sbuf_tensor("ftile", [128, 2 * T], F32)
    gtile = nc.alloc_sbuf_tensor("gtile", [128, 2 * T], BF16)
    sg1 = nc.alloc_sbuf_tensor("sg1", [128, 2 * 512], F32)
    sg2 = nc.alloc_sbuf_tensor("sg2", [128, 2 * 512], F32)
    t1b = nc.alloc_sbuf_tensor("t1b", [128, 2 * 512], F32)
    scr = nc.alloc_sbuf_tensor("scr", [128, 512], F32)
    zt = nc.alloc_sbuf_tensor("zt", [128, 512], F32)
    apb = nc.alloc_sbuf_tensor("apb", [128, 64], F32)
    identnp = nc.inline_tensor(np.eye(128, dtype=np.float32), name="ident128")
    ident = nc.alloc_sbuf_tensor("ident", [128, 128], F32)
    identbnp = nc.inline_tensor(np.eye(128, dtype=np.float32).astype(ml_dtypes.bfloat16), name="identb128")
    identb = nc.alloc_sbuf_tensor("identb", [128, 128], BF16)
    pb = [nc.alloc_psum_tensor(f"pa{i}", [128, 512], F32) for i in range(6)]
    pbb = [nc.alloc_psum_tensor(f"pab{i}", [128, 128], BF16) for i in range(2)]

    hcnt = P.dma("sync", ident[:, :], identnp.ap(), "hx")
    hcnt2 = P.dma("sync", identb[:, :], identbnp.ap(), "hx")
    P.memset(zt[:, :], 0.0, sem="dve")
    ztc = P.cnt["dve"]

    bankuse = {}
    xtstore = []
    for rnd in range(2):
        lcnt = []
        for j in range(4):
            tt = rnd * 4 + j
            b, sr = tt // 4, (tt % 4) * 128
            lw = []
            if rnd == 1:
                lw = xtstore
            lcnt.append(P.dma("sync", xnat[:, j * D:(j + 1) * D],
                              xs_ap[b, sr:sr + 128, :], "hx", waits=lw))
        if rnd == 1:
            xtstore = []
        for j in range(4):
            tt = rnd * 4 + j
            wl = [lcnt[j]]
            s0 = P.tt(sqt[:, :], xnat[:, j * D:(j + 1) * D],
                      xnat[:, j * D:(j + 1) * D], ALU.mult, sem="dve", waits=wl)
            s1 = P.reduce(stat[:, 0:1], xnat[:, j * D:(j + 1) * D], ALU.add,
                          sem="dve", waits=[("dve", s0)])
            s2 = P.reduce(stat[:, 2:3], sqt[:, :], ALU.add, sem="dve",
                          waits=[("dve", s0)])
            m1 = P.tsc(stat[:, 0:1], stat[:, 0:1], 1.0 / D, None, ALU.mult,
                       sem="dve", waits=[("dve", s1)])
            m2 = P.tsc(stat[:, 2:3], stat[:, 2:3], 1.0 / D, None, ALU.mult,
                       sem="dve", waits=[("dve", s2)])
            m3 = P.stt(stat[:, 3:4], stat[:, 0:1], stat[:, 0:1], stat[:, 2:3],
                       ALU.mult, ALU.subtract, sem="dve", waits=[("dve", m2)])
            a1 = P.act(stat[:, 1:2], stat[:, 3:4], AF.Ln, bias=1e-5, scale=-1.0,
                       sem="act", waits=[("dve", m3)])
            a2 = P.act(stat[:, 1:2], stat[:, 1:2], AF.Exp, scale=-0.5,
                       sem="act", waits=[("act", a1)])
            x1 = P.tsc(xntok[:, j * D:(j + 1) * D], xnat[:, j * D:(j + 1) * D],
                       stat[:, 0:1], stat[:, 1:2], ALU.subtract, ALU.mult,
                       sem="dve", waits=[("act", a2)])
            P.wait("tensor", "dve", x1)
            for dk in range(DK):
                bk = dk % 2
                wl = [hcnt]
                if bk in bankuse:
                    wl.append(bankuse.pop(bk))
                tc = P.transpose(pb[bk][:, 0:128],
                                 xnat[:, j * D + dk * 128: j * D + (dk + 1) * 128],
                                 ident[:, :], sem="pe", waits=wl)
                cc = P.copy(xTa[:, dk * 512 + j * 128: dk * 512 + (j + 1) * 128],
                            pb[bk][:, 0:128], sem="dve", waits=[("pe", tc)])
                bankuse[bk] = ("dve", cc)
                bk2 = 2 + dk % 2
                wl = [hcnt2]
                if bk2 in bankuse:
                    wl.append(bankuse.pop(bk2))
                tc2 = P.transpose(pbb[dk % 2][:, 0:128],
                                  xntok[:, j * D + dk * 128: j * D + (dk + 1) * 128],
                                  identb[:, :], sem="pe", waits=wl)
                cc2 = P.copy(xnb[:, dk * T + rnd * 512 + j * 128: dk * T + rnd * 512 + (j + 1) * 128],
                             pbb[dk % 2][:, 0:128], sem="dve", waits=[("pe", tc2)])
                bankuse[bk2] = ("dve", cc2)
        xc = P.cnt["dve"]
        xtstore.append(P.dma("sync",
                             xt_out[:, :, rnd * 512:(rnd + 1) * 512].rearrange(
                                 "k p c -> p k c"),
                             xTa[:, :].rearrange("p (k c) -> p k c", k=DK), "st",
                             waits=[("dve", xc)]))
    xnc = P.cnt["dve"]
    P.dma("sync", xn_out.rearrange("k p c -> p k c"),
          xnb[:, :].rearrange("p (k c) -> p k c", k=DK), "st", waits=[("dve", xnc)])

    WSLOT = 16 * 2 * 128
    slotuse = {}
    bankuse2 = {}
    sttok = {}
    for ci in range(8):
        chunk = [2 * ci, 2 * ci + 1]
        slot = ci % 2
        dc = {}
        for nm, wb, src in (("wf", wfb, wf_ap), ("wi", wib, wi_ap), ("wv", wvb, wv_ap)):
            wl = [("pe", slotuse[(nm, slot)])] if (nm, slot) in slotuse else []
            dc[nm] = P.dma("gpsimd",
                           wb[:, slot * WSLOT: slot * WSLOT + 16 * 2 * 128].rearrange(
                               "p (k c) -> p k c", k=16),
                           src[:, chunk[0] * 128:(chunk[0] + 2) * 128].rearrange(
                               "(k p) c -> p k c", p=128),
                           "w", waits=wl)
        for li, rt in enumerate(chunk):
            par = rt % 2
            bF, bI, bV = pb[3 * par], pb[3 * par + 1], pb[3 * par + 2]
            for half in range(2):
                wl = [dc["wf"], dc["wi"], dc["wv"], ("dve", xnc)]
                for ib in (3 * par, 3 * par + 1, 3 * par + 2):
                    if ib in bankuse2:
                        wl.append(bankuse2.pop(ib))
                first = True
                for k in range(DK):
                    off = slot * WSLOT + k * 2 * 128 + li * 128
                    rhs = xnb[:, k * T + half * 512: k * T + (half + 1) * 512]
                    P.mm(bF[:, :], wfb[:, off:off + 128], rhs, start=(k == 0),
                         stop=(k == DK - 1), waits=wl if first else ())
                    first = False
                    P.mm(bI[:, :], wib[:, off:off + 128], rhs, start=(k == 0),
                         stop=(k == DK - 1))
                    P.mm(bV[:, :], wvb[:, off:off + 128], rhs, start=(k == 0),
                         stop=(k == DK - 1), sem="pe" if k == DK - 1 else None)
                pc = P.cnt["pe"]
                for nm in ("wf", "wi", "wv"):
                    slotuse[(nm, slot)] = pc
                ssl = half
                fwl = [("pe", pc)]
                if ("ft", par, half) in sttok:
                    fwl.extend(sttok.pop(("ft", par, half)))
                af = P.act(ftile[:, par * T + half * 512: par * T + (half + 1) * 512],
                           bF[:, :], AF.Tanh, sem="act", waits=fwl)
                ai = P.act(sg1[:, ssl * 512:(ssl + 1) * 512], bI[:, :], AF.Sigmoid,
                           sem="act", waits=[("pe", pc)])
                av = P.act(sg2[:, ssl * 512:(ssl + 1) * 512], bV[:, :], AF.Sigmoid,
                           sem="act", waits=[("pe", pc)])
                v1 = P.tt(t1b[:, ssl * 512:(ssl + 1) * 512], bV[:, :],
                          sg2[:, ssl * 512:(ssl + 1) * 512], ALU.mult,
                          sem="dve", waits=[("act", av)])
                gwl = [("act", ai), ("dve", v1)]
                if ("gt", par, half) in sttok:
                    gwl.extend(sttok.pop(("gt", par, half)))
                v2 = P.tt(gtile[:, par * T + half * 512: par * T + (half + 1) * 512],
                          sg1[:, ssl * 512:(ssl + 1) * 512],
                          t1b[:, ssl * 512:(ssl + 1) * 512], ALU.mult,
                          sem="dve", waits=gwl)
                bankuse2[3 * par] = ("act", af)
                bankuse2[3 * par + 1] = ("act", ai)
                bankuse2[3 * par + 2] = ("dve", v2)
            fv = P.cnt["act"]
            gv = P.cnt["dve"]
            for b in range(2):
                s1 = P.scan(scr[:, :], ftile[:, par * T + b * 512: par * T + (b + 1) * 512],
                            gtile[:, par * T + b * 512: par * T + (b + 1) * 512],
                            0.0, ALU.mult, ALU.add, sem="dve",
                            waits=[("act", fv), ("dve", P.cnt["dve"])])
                P.copy(apb[:, rt * 2 + b: rt * 2 + b + 1], scr[:, 511:512],
                       sem="dve", waits=[("dve", s1)])
                s2 = P.scan(scr[:, :], ftile[:, par * T + b * 512: par * T + (b + 1) * 512],
                            zt[:, :], 1.0, ALU.mult, ALU.add, sem="dve",
                            waits=[("dve", P.cnt["dve"]), ("dve", ztc)])
                P.copy(apb[:, 32 + rt * 2 + b: 32 + rt * 2 + b + 1], scr[:, 511:512],
                       sem="dve", waits=[("dve", s2)])
            sc = P.cnt["dve"]
            stf = P.dma("sync", f_out[rt], ftile[:, par * T:(par + 1) * T], "st",
                        waits=[("act", fv)])
            stg = P.dma("sync", g_out[rt], gtile[:, par * T:(par + 1) * T], "st",
                        waits=[("dve", sc)])
            sttok[("ft", par, 0)] = [stf, ("dve", sc)]
            sttok[("ft", par, 1)] = [stf, ("dve", sc)]
            sttok[("gt", par, 0)] = [stg, ("dve", sc)]
            sttok[("gt", par, 1)] = [stg, ("dve", sc)]
    apc = P.cnt["dve"]
    P.dma("sync", ap_out, apb[:, :], "st", waits=[("dve", apc)])

    if wvt_ap is not None:
        slabw = nc.alloc_sbuf_tensor("slabw", [128, 2 * 1024], F32)
        rwA = nc.alloc_sbuf_tensor("rwA", [128, 32], F32)
        wvrA = nc.alloc_sbuf_tensor("wvrA", [128, 1024], BF16)
        lrwA = P.dma("sync", rwA[:, :], rw_ap, "hx")
        mzA = P.memset(wvrA[:, :], 0.0, sem="dve")
        slab_free = {}
        wvb_free = {}
        for h in range(16):
            for sl in range(2):
                b = (h * 2 + sl) % 2
                wl = []
                if b in slab_free:
                    wl.append(slab_free.pop(b))
                ls = P.dma("scalar", slabw[:, b * 1024:(b + 1) * 1024],
                           wvt_ap[128 * h:128 * (h + 1),
                                  sl * 1024:(sl + 1) * 1024],
                           "wvt", waits=wl)
                for j in range(8):
                    dcx = sl * 8 + j
                    pbk = 4 + (h * DK + dcx) % 2
                    mlw = [ls, lrwA]
                    if pbk in bankuse2:
                        mlw.append(bankuse2.pop(pbk))
                    if ("wv", pbk) in wvb_free:
                        mlw.append(wvb_free.pop(("wv", pbk)))
                    mc = P.mm(pb[pbk][:, 0:2],
                              slabw[:, b * 1024 + j * 128: b * 1024 + (j + 1) * 128],
                              rwA[:, 2 * h:2 * h + 2], start=True, stop=True,
                              sem="pe", waits=mlw)
                    cc = P.copy(wvrA[:, dcx * 64 + h: dcx * 64 + h + 33: 32],
                                pb[pbk][:, 0:2], sem="dve",
                                waits=[("pe", mc), ("dve", mzA)])
                    wvb_free[("wv", pbk)] = ("dve", cc)
                slab_free[b] = ("pe", P.cnt["pe"])
        P.dma("sync", wvr_out, wvrA[:, :], "st", waits=[("dve", P.cnt["dve"])])
    if waux_ap is not None:
        P.dma("sync", xnat[0:1, 0:D], waux_ap, "hx", waits=[("dve", apc)])
    P.wait_stream("sync", "st")
    P.wait_stream("sync", "hx")
    P.wait_stream("scalar", "wvt")


def emit_phaseB(P, env, *, w2_ap, n_i, rhs_buf, d_chunk, first_into_acc,
                hready):
    w2b = env["w2b"]; acc = env["acc"]; pb = env["psum"]
    slotuse = env["slotuse"]; bankfree = env["bankfree"]
    W2SLOT = env["W2SLOT"]
    dchunks = [list(range(c0, min(DK, c0 + d_chunk))) for c0 in range(0, DK, d_chunk)]
    assert n_i * d_chunk * 128 <= W2SLOT
    bidx = env.get("bidx", 0)
    for dci, dchunk in enumerate(dchunks):
        dcs = len(dchunk)
        slot = dci % 2
        key = ("Bw2", slot)
        wl = []
        if key in slotuse:
            wl.append(("pe", slotuse[key]))
        w2cnt = P.dma(
            "gpsimd",
            w2b[:, slot * W2SLOT: slot * W2SLOT + n_i * dcs * 128].rearrange(
                "p (k c) -> p k c", k=n_i),
            w2_ap[:, dchunk[0] * 128:(dchunk[0] + dcs) * 128].rearrange(
                "(k p) c -> p k c", p=128),
            "w", waits=wl)
        for mi, m2 in enumerate(dchunk):
            for half in range(2):
                par = bidx % 2
                bY = pb[4 + par]; iY = 4 + par
                bidx += 1
                wl = [w2cnt, hready]
                if iY in bankfree:
                    wl.append(bankfree.pop(iY))
                for k2 in range(n_i):
                    off = slot * W2SLOT + k2 * dcs * 128 + mi * 128
                    if isinstance(rhs_buf, list):
                        ent = rhs_buf[k2]
                        if isinstance(ent, list):
                            rhs = ent[half]
                        else:
                            rhs = ent[:, half * 512:(half + 1) * 512]
                    else:
                        rhs = rhs_buf[:, k2 * T + half * 512: k2 * T + (half + 1) * 512]
                    P.mm(bY[:, :], w2b[:, off:off + 128], rhs,
                         start=(k2 == 0), stop=(k2 == n_i - 1),
                         waits=wl if k2 == 0 else (),
                         sem="pe" if k2 == n_i - 1 else None)
                pecnt = P.cnt["pe"]
                slotuse[("Bw2", slot)] = pecnt
                adst = acc[:, m2 * T + half * 512: m2 * T + (half + 1) * 512]
                if first_into_acc:
                    dc = P.copy(adst, bY[:, :], sem="dve", waits=[("pe", pecnt)])
                else:
                    dc = P.tt(adst, adst, bY[:, :], ALU.add, sem="dve",
                              waits=[("pe", pecnt)])
                bankfree[iY] = ("dve", dc)
    env["bidx"] = bidx


def emit_projA(P, env, *, w_ap, n_m, evac_cb, tagkey, chunk=2, banks=(0, 1),
               rhs_ready=None, wbuf=None, wslot=None):
    xnb = env["xnb"]; pb = env["psum"]
    slotuse = env["slotuse"]; bankfree = env["bankfree"]
    wbuf = wbuf if wbuf is not None else env["w1b"]
    WSLOT = wslot if wslot is not None else env["W1SLOT"]
    chunks = [list(range(c0, min(n_m, c0 + chunk))) for c0 in range(0, n_m, chunk)]
    gidx = env.get(("gidx", tagkey), 0)
    for ci, ch in enumerate(chunks):
        cs = len(ch)
        slot = ci % 2
        key = (tagkey, slot)
        wl = []
        if key in slotuse:
            wl.append(("pe", slotuse[key]))
        dcnt = P.dma("gpsimd",
                     wbuf[:, slot * WSLOT: slot * WSLOT + 16 * cs * 128].rearrange(
                         "p (k c) -> p k c", k=16),
                     w_ap[:, ch[0] * 128:(ch[0] + cs) * 128].rearrange(
                         "(k p) c -> p k c", p=128),
                     "w", waits=wl)
        for li, mt in enumerate(ch):
            for half in range(2):
                par = gidx % 2
                bk = banks[par]
                gidx += 1
                wl = [dcnt]
                if rhs_ready is not None:
                    wl.append(rhs_ready)
                if bk in bankfree:
                    wl.append(bankfree.pop(bk))
                for k in range(DK):
                    off = slot * WSLOT + k * cs * 128 + li * 128
                    rhs = xnb[:, k * T + half * 512: k * T + (half + 1) * 512]
                    P.mm(pb[bk][:, :], wbuf[:, off:off + 128], rhs,
                         start=(k == 0), stop=(k == DK - 1),
                         waits=wl if k == 0 else (),
                         sem="pe" if k == DK - 1 else None)
                pecnt = P.cnt["pe"]
                slotuse[(tagkey, slot)] = pecnt
                tok = evac_cb(mt, half, pb[bk], pecnt)
                bankfree[bk] = tok


def emit_launchB0(P, aps):
    nc = P.nc
    env = glu_env(P, icmax=1, w2slot=2048, hmax=16)
    xTb = env["acc"]
    xnb = env["xnb"]
    hbuf = env["h"]

    carry = nc.alloc_sbuf_tensor("carry", [128, 32], F32)
    apsb = nc.alloc_sbuf_tensor("apsb", [128, 8 * 64], F32)
    initsb = nc.alloc_sbuf_tensor("initsb", [128, 16], F32)
    selb = nc.alloc_sbuf_tensor("selb", [128, 8], F32)
    selrow = nc.alloc_sbuf_tensor("selrow", [1, 8], BF16)
    scr8 = nc.alloc_sbuf_tensor("scr8", [128, 8], F32)
    cat8 = nc.alloc_sbuf_tensor("cat8", [128, 8], F32)
    ftl = nc.alloc_sbuf_tensor("ftl", [128, 2 * T], F32)
    gtl = nc.alloc_sbuf_tensor("gtl", [128, 2 * T], BF16)
    stt_ = nc.alloc_sbuf_tensor("sttl", [128, 2 * 512], F32)
    tq = env["sil"]
    sgq = env["tmp"]
    callb = nc.alloc_sbuf_tensor("callb", [16, T], BF16)
    selnp = np.zeros((16, 16 * 128), np.float32)
    for hh in range(16):
        selnp[hh, 128 * hh:128 * (hh + 1)] = 1.0
    sel16c = nc.inline_tensor(selnp.astype(ml_dtypes.bfloat16), name="sel16c")
    sel16 = nc.alloc_sbuf_tensor("sel16", [16, 16 * 128], BF16)
    identnp = nc.inline_tensor(np.eye(128, dtype=np.float32), name="identB128")
    ident = nc.alloc_sbuf_tensor("identB", [128, 128], F32)
    ones1 = nc.inline_tensor(np.ones((1, 128), np.float32).astype(ml_dtypes.bfloat16),
                             name="ones1B")
    ones_sb = nc.alloc_sbuf_tensor("ones_sbB", [1, 128], BF16)
    env["ident"] = ident
    env["ones_sb"] = ones_sb
    env["sel16"] = sel16
    env["callb"] = callb

    lxt = P.dma("sync", xTb[:, :].rearrange("p (k c) -> p k c", k=DK),
                aps["xt_in"].rearrange("k p c -> p k c"), "hx")
    lxn = P.dma("sync", xnb[:, :].rearrange("p (k c) -> p k c", k=DK),
                aps["xn_in"].rearrange("k p c -> p k c"), "hx")
    lap = P.dma("sync", apsb[:, :].rearrange("p (c k) -> p c k", c=8),
                aps["apall"], "hx")
    lin = P.dma("sync", initsb[:, :], aps["initsb_in"], "hx")
    lid = P.dma("sync", ident[:, :], identnp.ap(), "hx")
    lsel = P.dma("gpsimd", sel16[:, :], sel16c.ap(), "x")
    lone = P.dma("gpsimd", ones_sb[:, :], ones1.ap(), "x")
    lsv = P.dma("gpsimd", selrow[0:1, 0:8], aps["selvec"], "x")
    bc = P.mm(env["psum"][0][:, 0:8], ones_sb[0:1, :], selrow[0:1, 0:8],
              start=True, stop=True, sem="pe", waits=[lone, lsv])
    c0 = P.copy(selb[:, :], env["psum"][0][:, 0:8], sem="dve", waits=[("pe", bc)])
    env["bankfree"][0] = ("dve", c0)

    for rt in range(DK):
        for b in range(2):
            col = rt * 2 + b
            s1 = P.scan(scr8[:, :], apsb[:, :].rearrange("p (c k) -> p c k", c=8)[:, :, 32 + col],
                        apsb[:, :].rearrange("p (c k) -> p c k", c=8)[:, :, col],
                        initsb[:, rt:rt + 1], ALU.mult, ALU.add, sem="dve",
                        waits=[lap, lin, ("dve", P.cnt["dve"])])
            c1 = P.copy(cat8[:, 0:1], initsb[:, rt:rt + 1], sem="dve",
                        waits=[("dve", s1)])
            c2 = P.copy(cat8[:, 1:8], scr8[:, 0:7], sem="dve", waits=[("dve", c1)])
            c3 = P.tt(cat8[:, :], cat8[:, :], selb[:, :], ALU.mult, sem="dve",
                      waits=[("dve", c2)])
            P.reduce(carry[:, col:col + 1], cat8[:, :], ALU.add, sem="dve",
                     waits=[("dve", c3)])
    carry_done = P.cnt["dve"]

    fgtok = {}
    def q_evac(rt, b, bank, pecnt):
        swl = [("pe", pecnt), fgtok[("f", rt % 2)], fgtok[("g", rt % 2)],
               ("dve", carry_done), ("dve", P.cnt["dve"])]
        ssl = b
        s1 = P.scan(stt_[:, ssl * 512:(ssl + 1) * 512],
                    ftl[:, (rt % 2) * T + b * 512: (rt % 2) * T + (b + 1) * 512],
                    gtl[:, (rt % 2) * T + b * 512: (rt % 2) * T + (b + 1) * 512],
                    carry[:, rt * 2 + b: rt * 2 + b + 1], ALU.mult, ALU.add,
                    sem="dve", waits=swl)
        t1 = P.tt(tq[:, ssl * 512:(ssl + 1) * 512], bank[:, :],
                  stt_[:, ssl * 512:(ssl + 1) * 512], ALU.mult, sem="dve",
                  waits=[("dve", s1)])
        a1 = P.act(sgq[:, ssl * 512:(ssl + 1) * 512],
                   tq[:, ssl * 512:(ssl + 1) * 512], AF.Sigmoid, sem="act",
                   waits=[("dve", t1)])
        r1 = P.tt(hbuf[:, rt * T + b * 512: rt * T + (b + 1) * 512],
                  tq[:, ssl * 512:(ssl + 1) * 512],
                  sgq[:, ssl * 512:(ssl + 1) * 512], ALU.mult, sem="dve",
                  waits=[("act", a1)])
        return ("dve", r1)

    def fg_load(rt):
        wl = []
        if ("fw", rt % 2) in fgtok:
            wl = fgtok.pop(("fw", rt % 2))
        f1 = P.dma("sync", ftl[:, (rt % 2) * T:(rt % 2 + 1) * T], aps["f_in"][rt],
                   "hx", waits=wl)
        g1 = P.dma("sync", gtl[:, (rt % 2) * T:(rt % 2 + 1) * T], aps["g_in"][rt],
                   "hx", waits=wl)
        fgtok[("f", rt % 2)] = f1
        fgtok[("g", rt % 2)] = g1

    fg_load(0)
    fg_load(1)
    def q_evac_outer(mt, half, bank, pecnt):
        tok = q_evac(mt, half, bank, pecnt)
        if half == 1:
            if mt + 2 < DK:
                fgtok[("fw", mt % 2)] = [tok]
                fg_load(mt + 2)
        return tok

    emit_projA(P, env, w_ap=aps["wq"], n_m=DK, evac_cb=q_evac_outer, tagkey="wqs",
               rhs_ready=lxn, banks=(0, 1), chunk=1)
    ro_ready = P.cnt["dve"]

    emit_phaseB(P, env, w2_ap=aps["wro"], n_i=DK, rhs_buf=hbuf, d_chunk=1,
                first_into_acc=False, hready=("dve", ro_ready))
    ro_used = P.cnt["pe"]

    env["h_guard"] = ("pe", ro_used)
    emit_glu_ffn(P, env, w1_ap=aps["wgate"], w3_ap=aps["wup"], w2_ap=aps["wdown"],
                 n_i=16, c_col=None, tag="lg", i_chunk=1, d_chunk=1)

    emit_moe_router(P, env, wg_ap=aps["wgmoe"], call_tile=callb, ident_sb=ident)
    env["bankfree"][0] = env.pop("rtbank0")
    env["bankfree"][1] = env.pop("rtbank1")
    cb = nc.alloc_sbuf_tensor("cbB", [128, T], BF16)
    pbc = env["psum"][2]
    for e in range(16):
        for half in range(2):
            wl = [("dve", P.cnt["dve"])]
            if e == 0 and half == 0:
                wl.append(lsel)
            if 2 in env["bankfree"]:
                wl.append(env["bankfree"].pop(2))
            pe1 = P.mm(pbc[:, 0:512], sel16[:, e * 128:(e + 1) * 128],
                       callb[:, half * 512:(half + 1) * 512],
                       start=True, stop=True, sem="pe", waits=wl)
            cc = P.copy(cb[:, half * 512:(half + 1) * 512], pbc[:, 0:512],
                        sem="dve", waits=[("pe", pe1)])
            env["bankfree"][2] = ("dve", cc)
        emit_glu_ffn(P, env, w1_ap=aps["w1t"][e], w3_ap=aps["w3t"][e],
                     w2_ap=aps["w2t"][e], n_i=11, c_col=cb[:, :], tag=f"e{e}",
                     i_chunk=1, d_chunk=1)
    for sh in range(2):
        emit_glu_ffn(P, env, w1_ap=aps["ws1t"][:, sh * 1408:(sh + 1) * 1408],
                     w3_ap=aps["ws3t"][:, sh * 1408:(sh + 1) * 1408],
                     w2_ap=aps["ws2t"][sh * 1408:(sh + 1) * 1408, :],
                     n_i=11, c_col=None, tag=f"sh{sh}", i_chunk=1, d_chunk=1)
    env["xn0_lastuse"] = ("pe", P.cnt["pe"])
    env["apsb"] = apsb
    env["ftl"] = ftl
    env["gtl"] = gtl
    env["sttl"] = stt_
    env["cbB"] = cb
    env["sel16_load"] = lsel
    env["ftl_free"] = ("dve", ro_ready)
    env["gtl_free"] = ("dve", ro_ready)
    return env


def emit_LN1(P, env):
    nc = P.nc
    xTb = env["acc"]; xnb = env["xnb"]; pb = env["psum"]
    sq = env["sil"]
    ftl = env["ftl"]; gtl = env["gtl"]
    meanB = ftl[:, 0:T]; rstdB = ftl[:, T:2 * T]
    m16 = gtl[0:1, 0:T]; r16 = gtl[0:1, T:2 * T]
    onescnp = nc.inline_tensor(np.ones((128, 1), np.float32), name="onescol")
    onesc = nc.alloc_sbuf_tensor("onescol_sb", [128, 1], F32)
    env["onescol_sb"] = onesc
    oc = P.dma("sync", onesc[:, :], onescnp.ap(), "hx")
    x1done = env["x1done"]
    mrow = env["sil"][32:33, 0:T]
    vrow = env["sil"][64:65, 0:T]
    bf_ = env["bankfree"]

    wl0 = [oc, x1done]
    for bkk in range(4):
        if bkk in bf_:
            wl0.append(bf_.pop(bkk))
    sqfree = {}
    for dk in range(DK):
        for half in range(2):
            sq_sl = sq[:, half * 512: half * 512 + 512]
            swl = [x1done]
            if half in sqfree:
                swl.append(sqfree.pop(half))
            s_ = P.tt(sq_sl, xTb[:, dk * T + half * 512: dk * T + (half + 1) * 512],
                      xTb[:, dk * T + half * 512: dk * T + (half + 1) * 512],
                      ALU.mult, sem="dve", waits=swl)
            P.mm(pb[half][0:1, :], onesc[:, :],
                 xTb[:, dk * T + half * 512: dk * T + (half + 1) * 512],
                 start=(dk == 0), stop=(dk == DK - 1),
                 waits=(wl0 if dk == 0 and half == 0 else ()))
            mq_ = P.mm(pb[2 + half][0:1, :], onesc[:, :], sq_sl,
                       start=(dk == 0), stop=(dk == DK - 1),
                       waits=[("dve", s_)], sem="pe")
            sqfree[half] = ("pe", mq_)
    pc = P.cnt["pe"]
    c1 = P.act(mrow[:, 0:512], pb[0][0:1, :], AF.Copy, scale=1.0 / D, sem="act",
               waits=[("pe", pc)])
    c2 = P.act(mrow[:, 512:T], pb[1][0:1, :], AF.Copy, scale=1.0 / D, sem="act",
               waits=[("pe", pc)])
    c3 = P.act(vrow[:, 0:512], pb[2][0:1, :], AF.Copy, scale=1.0 / D, sem="act",
               waits=[("pe", pc)])
    c4 = P.act(vrow[:, 512:T], pb[3][0:1, :], AF.Copy, scale=1.0 / D, sem="act",
               waits=[("pe", pc)])
    for bkk in range(4):
        bf_[bkk] = ("act", [c1, c2, c3, c4][bkk])
    scr_row = env["tmp"][64:65, 0:T]
    d0 = P.tt(scr_row, mrow[:, :], mrow[:, :], ALU.mult, sem="dve",
              waits=[("act", c4)])
    d1 = P.tt(vrow[:, :], scr_row, vrow[:, :], ALU.subtract, sem="dve",
              waits=[("dve", d0)])
    a1 = P.act(vrow[:, :], vrow[:, :], AF.Ln, bias=1e-5, scale=-1.0, sem="act",
               waits=[("dve", d1)])
    a2 = P.act(vrow[:, :], vrow[:, :], AF.Exp, scale=-0.5, sem="act",
               waits=[("act", a1)])
    c5 = P.copy(m16[:, :], mrow[:, :], sem="dve",
                waits=[("act", a2), env["gtl_free"]])
    c6 = P.copy(r16[:, :], vrow[:, :], sem="dve", waits=[("act", a2), ("dve", c5)])
    for half in range(2):
        for j, (dst, srow) in enumerate(((meanB, m16), (rstdB, r16))):
            bk = half * 2 + j
            wl = [("dve", c6)]
            if half == 0 and j == 0:
                wl.append(env["ftl_free"])
            if bk in bf_:
                wl.append(bf_.pop(bk))
            mc = P.mm(pb[bk][:, :], env["ones_sb"][0:1, :],
                      srow[:, half * 512:(half + 1) * 512],
                      start=True, stop=True, sem="pe", waits=wl)
            cc = P.copy(dst[:, half * 512:(half + 1) * 512], pb[bk][:, :],
                        sem="dve", waits=[("pe", mc)])
            bf_[bk] = ("dve", cc)
    bc = P.cnt["dve"]
    for dk in range(DK):
        u1 = P.tt(env["tmp"][:, :], xTb[:, dk * T:(dk + 1) * T], meanB[:, :],
                  ALU.subtract, sem="dve",
                  waits=[("dve", bc), env["xn0_lastuse"], ("dve", P.cnt["dve"])])
        P.tt(xnb[:, dk * T:(dk + 1) * T], env["tmp"][:, :], rstdB[:, :], ALU.mult,
             sem="dve", waits=[("dve", u1)])
    env["xn1_ready"] = ("dve", P.cnt["dve"])


def emit_l1ffn(P, env, aps):
    nc = P.nc
    xnb = env["xnb"]; pb = env["psum"]; bf_ = env["bankfree"]
    slotuse = env["slotuse"]
    xc = env["h"]
    sil = env["sil"]; tmp = env["tmp"]; sttl = env["sttl"]
    ftl = env["ftl"]; gtl = env["gtl"]
    sel16 = env["sel16"]; onesb = env["ones_sb"]
    onesc = env["onescol_sb"]
    W1SLOT = env["W1SLOT"]
    convsb = nc.alloc_sbuf_tensor("convsb", [128, 32], F32)
    rotsb = nc.alloc_sbuf_tensor("rotsb", [128, 2], F32)
    qtl = nc.alloc_sbuf_tensor("qtl", [128, T], BF16)
    ktl = nc.alloc_sbuf_tensor("ktl", [128, T], BF16)
    prodb = nc.alloc_sbuf_tensor("prodb", [128, 512], BF16)
    vgwb = nc.alloc_sbuf_tensor("vgwb", [128, 16 * 128], BF16)
    vtile = nc.alloc_sbuf_tensor("vtile", [128, T], BF16)
    qkrow = nc.alloc_sbuf_tensor("qkrow", [16, 3 * 512], BF16)
    qkb = nc.alloc_sbuf_tensor("qkb", [128, 3 * 512], BF16)
    extf = nc.alloc_sbuf_tensor("extf", [128, T], BF16)
    protb = vgwb[:, 0:128]
    bd64b = vgwb[:, 128:256]
    wvrb = qtl
    rtrD = ftl[0:16, T:2 * T]
    pb6 = nc.alloc_psum_tensor("pb6", [128, 512], F32)
    pb7 = nc.alloc_psum_tensor("pb7", [128, 512], F32)
    onesfnp = nc.inline_tensor(np.ones((1, 128), np.float32), name="onesfrow")
    onesf_row = nc.alloc_sbuf_tensor("onesf_row", [1, 128], F32)
    onesc = env["onescol_sb"]
    outf_tiles = ([env["w1b"][:, i * T:(i + 1) * T] for i in range(4)] +
                  [env["w3b"][:, i * T:(i + 1) * T] for i in range(4)] +
                  [gtl[:, 0:T], gtl[:, T:2 * T], env["cbB"][:, :]] +
                  [qtl[:, :], ktl[:, :]] +
                  [extf[:, 0:T], qkb[:, 0:T],
                   [prodb[:, 0:512], qkb[:, 1024:1536]]])
    rtrA = sil[0:16, :]; rtrB = tmp[0:16, :]; rtrC = sttl[0:16, :]
    mrow = sttl[0:1, 0:T]; vrow = tmp[0:1, 0:T]
    o_t = ftl[:, 0:T]

    lcv = P.dma("sync", convsb[:, :], aps["convw"], "hx")
    lof = P.dma("sync", onesf_row[:, :], onesfnp.ap(), "hx")
    lrt = P.dma("sync", rotsb[:, :], aps["rotcs"], "hx")
    lpr = P.dma("gpsimd", protb[:, :], aps["protc"], "x")
    lbd = P.dma("gpsimd", bd64b[:, :], aps["bd64c"], "x")

    hguard = env["h_guard2"]
    xnr = env["xn1_ready"]
    xc_tok = []
    cscr_war = {}
    for dk in range(DK):
        cw0 = convsb[:, 2 * dk:2 * dk + 1]
        cw1 = convsb[:, 2 * dk + 1:2 * dk + 2]
        S = sttl if dk % 2 == 0 else tmp
        w0 = [xnr, lcv] if dk == 0 else []
        if (dk % 2, 0) in cscr_war:
            w0.append(cscr_war.pop((dk % 2, 0)))
        t0 = P.tsc(S[:, 0:512], xnb[:, dk * T:dk * T + 512], cw1, None,
                   ALU.mult, sem="dve", waits=w0)
        w1 = []
        if (dk % 2, 1) in cscr_war:
            w1.append(cscr_war.pop((dk % 2, 1)))
        t1 = P.tsc(S[:, 512:T], xnb[:, dk * T + 512:(dk + 1) * T], cw1, None,
                   ALU.mult, sem="dve", waits=w1)
        t2 = P.stt(S[:, 512:T], xnb[:, dk * T:dk * T + 512], cw0,
                   S[:, 512:T], ALU.mult, ALU.add, sem="dve",
                   waits=[("dve", t1)])
        aw = [hguard] if dk < 2 else []
        a0 = P.act(xc[:, dk * T:dk * T + 512], S[:, 0:512], AF.Silu, sem="act",
                   waits=aw + [("dve", t0)])
        a1 = P.act(xc[:, dk * T + 512:(dk + 1) * T], S[:, 512:T], AF.Silu,
                   sem="act", waits=[("dve", t2)])
        cscr_war[(dk % 2, 0)] = ("act", a0)
        cscr_war[(dk % 2, 1)] = ("act", a1)
        xc_tok.append(("act", a1))
    xc_ready = xc_tok[-1]

    dpsum = [pb[3][0:16, :], pb[4][0:16, :], pb[5][0:16, :]]
    for bb in (2, 3, 4, 5):
        if bb in bf_:
            pass
    qk_free = {}
    for tt in range(8):
        dc = {}
        for nm, wsrc in (("q", aps["wqf"]), ("k", aps["wkf"])):
            slot = tt % 2
            key = ("l1" + nm, slot)
            wl = [("pe", slotuse[key])] if key in slotuse else []
            wb = env["w1b"] if nm == "q" else env["w3b"]
            dc[nm] = P.dma("gpsimd",
                           wb[:, slot * W1SLOT: slot * W1SLOT + 16 * 128].rearrange(
                               "p (k c) -> p k c", k=16),
                           wsrc[:, tt * 128:(tt + 1) * 128].rearrange(
                               "(k p) c -> p k c", p=128),
                           "w", waits=wl)
        for nm, bk in (("q", 0), ("k", 1)):
            wb = env["w1b"] if nm == "q" else env["w3b"]
            slot = tt % 2
            for half in range(2):
                wl = [dc[nm], xc_ready]
                if (nm, "bank") in qk_free:
                    wl.append(qk_free.pop((nm, "bank")))
                if tt == 0 and half == 0 and bk in bf_:
                    wl.append(bf_.pop(bk))
                for k in range(DK):
                    off = slot * W1SLOT + k * 128
                    P.mm(pb[bk][:, :], wb[:, off:off + 128],
                         xc[:, k * T + half * 512: k * T + (half + 1) * 512],
                         start=(k == 0), stop=(k == DK - 1),
                         waits=wl if k == 0 else (),
                         sem="pe" if k == DK - 1 else None)
                pc = P.cnt["pe"]
                slotuse[("l1" + nm, slot)] = pc
                dst = qtl if nm == "q" else ktl
                dwl = [("pe", pc)]
                if (nm, "tile", half) in qk_free:
                    dwl.append(qk_free.pop((nm, "tile", half)))
                if nm == "q":
                    aq = P.act(dst[:, half * 512:(half + 1) * 512], pb[bk][:, :],
                               AF.Silu, sem="act", waits=dwl)
                    qk_free[(nm, "bank")] = ("act", aq)
                else:
                    dq = P.copy(dst[:, half * 512:(half + 1) * 512], pb[bk][:, :],
                                sem="dve", waits=dwl)
                    qk_free[(nm, "bank")] = ("dve", dq)
            evac_tok = qk_free[(nm, "bank")]
            if nm == "q":
                q_act = evac_tok
            wl2 = [("dve", P.cnt["dve"]), lpr, lrt, evac_tok]
            if ("rot", "bank") in qk_free:
                wl2.append(qk_free.pop(("rot", "bank")))
            rm = P.mm(pb[2][:, :], protb[:, :], dst[:, 512:T], start=True,
                      stop=True, sem="pe", waits=wl2)
            rt1 = P.tsc(sttl[:, 0:512], dst[:, 512:T], rotsb[:, 0:1], None,
                        ALU.mult, sem="dve",
                        waits=[("dve", P.cnt["dve"]), lrt, evac_tok] +
                              ([xc_ready] if tt == 0 else []))
            rt2 = P.stt(dst[:, 512:T], pb[2][:, :], rotsb[:, 1:2],
                        sttl[:, 0:512], ALU.mult, ALU.add, sem="dve",
                        waits=[("pe", rm), ("dve", rt1)])
            qk_free[("rot", "bank")] = ("dve", rt2)
            qk_free[(nm, "tile", 0)] = ("dve", rt2)
            qk_free[(nm, "tile", 1)] = ("dve", rt2)
        qr = ("dve", P.cnt["dve"])
        for pi, (lh, mh) in enumerate(((0, 0), (1, 0), (1, 1))):
            pw = [qr, ("dve", P.cnt["dve"]), lbd, q_act]
            if ("prod",) in qk_free:
                pw.append(qk_free.pop(("prod",)))
            pd = P.tt(prodb[:, :], qtl[:, lh * 512:(lh + 1) * 512],
                      ktl[:, mh * 512:(mh + 1) * 512], ALU.mult, sem="dve",
                      waits=pw)
            dmw = [("dve", pd)]
            if tt == 0 and 3 + pi in bf_:
                dmw.append(bf_.pop(3 + pi))
            dm = P.mm(dpsum[pi], bd64b[:, tt * 16:(tt + 1) * 16], prodb[:, :],
                      start=(tt == 0), stop=(tt == 7), sem="pe", waits=dmw)
            qk_free[("prod",)] = ("pe", dm)
        qk_free[("q", "tile", 0)] = ("pe", P.cnt["pe"])
        qk_free[("q", "tile", 1)] = qk_free[("q", "tile", 0)]
        qk_free[("k", "tile", 0)] = qk_free[("q", "tile", 0)]
        qk_free[("k", "tile", 1)] = qk_free[("q", "tile", 0)]
    dots_pe = P.cnt["pe"]
    if ("rot", "bank") in qk_free:
        bf_[2] = qk_free.pop(("rot", "bank"))

    wvr_ready = P.dma("sync", wvrb[:, :], aps["wvr_in"], "hx",
                      waits=[("pe", dots_pe), ("dve", P.cnt["dve"])])

    lg_pe = []
    for half in range(2):
        bk = half
        wl = [wvr_ready, xc_ready]
        if bk in bf_:
            wl.append(bf_.pop(bk))
        if (("q", "bank") if bk == 0 else ("k", "bank")) in qk_free:
            wl.append(qk_free.pop(("q", "bank") if bk == 0 else ("k", "bank")))
        for dk in range(DK):
            P.mm(pb[bk][0:64, :], wvrb[:, dk * 64:(dk + 1) * 64],
                 xc[:, dk * T + half * 512: dk * T + (half + 1) * 512],
                 start=(dk == 0), stop=(dk == DK - 1),
                 waits=wl if dk == 0 else (),
                 sem="pe" if dk == DK - 1 else None)
        lg_pe.append(P.cnt["pe"])

    e0a = P.act(rtrA[:, 0:512], pb[0][0:16, :], AF.Exp, sem="act",
                waits=[("pe", lg_pe[0])])
    e0b = P.act(rtrA[:, 512:T], pb[1][0:16, :], AF.Exp, sem="act",
                waits=[("pe", lg_pe[1])])
    e1a = P.act(rtrB[:, 0:512], pb[0][32:48, :], AF.Exp, sem="act",
                waits=[("pe", lg_pe[0])])
    e1b = P.act(rtrB[:, 512:T], pb[1][32:48, :], AF.Exp, sem="act",
                waits=[("pe", lg_pe[1])])
    bf_[0] = ("act", e1a)
    bf_[1] = ("act", e1b)
    r1 = P.tt(rtrC[:, :], rtrA[:, :], rtrB[:, :], ALU.add, sem="dve",
              waits=[("act", e1b)])
    r2 = P.recip(rtrC[:, :], rtrC[:, :], sem="dve", waits=[("dve", r1)])
    r3 = P.tt(rtrD[:, :], rtrA[:, :], rtrB[:, :], ALU.is_ge, sem="dve",
              waits=[("dve", r2)])
    r4 = P.tt(rtrA[:, :], rtrA[:, :], rtrB[:, :], ALU.max, sem="dve",
              waits=[("dve", r3)])
    r5 = P.tt(rtrA[:, :], rtrA[:, :], rtrC[:, :], ALU.mult, sem="dve",
              waits=[("dve", r4)])
    r6 = P.tt(rtrB[:, 0:512], rtrD[:, 0:512], rtrD[:, 512:T], ALU.is_equal,
              sem="dve", waits=[("dve", r5)])
    r7 = P.tt(rtrC[:, 0:512], rtrB[:, 0:512], rtrA[:, 512:T], ALU.mult,
              sem="dve", waits=[("dve", r6)])
    r8 = P.tsc(rtrB[:, 512:T], rtrB[:, 0:512], 1.0, None, ALU.add, sem="dve",
               waits=[("dve", r7)])
    r8r = P.recip(rtrB[:, 512:T], rtrB[:, 512:T], sem="dve", waits=[("dve", r8)])
    r9 = P.tt(rtrC[:, 512:T], rtrA[:, 512:T], rtrB[:, 512:T], ALU.mult,
              sem="dve", waits=[("dve", r8r)])
    q1 = P.tt(rtrD[:, 0:512], rtrA[:, 0:512], dpsum[0], ALU.mult, sem="dve",
              waits=[("dve", r9), ("pe", dots_pe)])
    q2 = P.tsc(qkrow[:, 0:512], rtrD[:, 0:512], 0.125, None, ALU.mult,
               sem="dve", waits=[("dve", q1)])
    q3 = P.tt(rtrD[:, 512:T], rtrC[:, 0:512], dpsum[1], ALU.mult, sem="dve",
              waits=[("dve", q2)])
    q4 = P.tsc(qkrow[:, 512:1024], rtrD[:, 512:T], 0.0625, None, ALU.mult,
               sem="dve", waits=[("dve", q3)])
    q5 = P.tt(rtrD[:, 0:512], rtrC[:, 512:T], dpsum[2], ALU.mult, sem="dve",
              waits=[("dve", q4)])
    q6 = P.tsc(qkrow[:, 1024:1536], rtrD[:, 0:512], 0.125, None, ALU.mult,
               sem="dve", waits=[("dve", q5)])
    for bb in (3, 4, 5):
        bf_[bb] = ("dve", q6)
    qk_ready = ("dve", q6)

    vg_free = {"wdma": ("pe", P.cnt["pe"])}
    outf_done = []
    for h in range(16):
        dl = [vg_free.pop("wdma")] if "wdma" in vg_free else []
        ldv = P.dma("gpsimd", vgwb[:, :].rearrange("p (k c) -> p k c", k=16),
                    aps["wvf"][:, 128 * h:128 * (h + 1)].rearrange(
                        "(k p) c -> p k c", p=128), "w", waits=dl)
        for half in range(2):
            bk = half
            wl = [ldv, xc_ready]
            if bk in bf_:
                wl.append(bf_.pop(bk))
            for k in range(DK):
                P.mm(pb[bk][:, :], vgwb[:, k * 128:(k + 1) * 128],
                     xc[:, k * T + half * 512: k * T + (half + 1) * 512],
                     start=(k == 0), stop=(k == DK - 1),
                     waits=wl if k == 0 else (),
                     sem="pe" if k == DK - 1 else None)
            pc = P.cnt["pe"]
            vw = [("pe", pc)]
            if ("vtile", half) in vg_free:
                vw.append(vg_free.pop(("vtile", half)))
            cv = P.copy(vtile[:, half * 512:(half + 1) * 512], pb[bk][:, :],
                        sem="dve", waits=vw)
            bf_[bk] = ("dve", cv)
        v_ready = P.cnt["dve"]
        vmm_done = ("pe", P.cnt["pe"])
        odst = [(o_t[:, 0:512], vtile[:, 0:512]),
                (sttl[:, 0:512], vtile[:, 0:512]),
                (tmp[:, 512:T], vtile[:, 512:T])]
        otoks = []
        for j in range(3):
            bk = 2 + j % 2
            wl = [qk_ready, ("dve", v_ready)]
            if h == 0 and j == 0 and "sel16_load" in env:
                wl.append(env["sel16_load"])
            if bk in bf_:
                wl.append(bf_.pop(bk))
            bm = P.mm(pb[bk][:, :], sel16[:, h * 128:(h + 1) * 128],
                      qkrow[:, j * 512:(j + 1) * 512], start=True, stop=True,
                      sem="pe", waits=wl)
            oj = P.tt(odst[j][0], pb[bk][:, :], odst[j][1], ALU.mult,
                      sem="dve", waits=[("pe", bm), ("dve", P.cnt["dve"])])
            otoks.append(oj)
            bf_[bk] = ("dve", oj)
        o4 = P.tt(o_t[:, 512:T], sttl[:, 0:512], tmp[:, 512:T], ALU.add,
                  sem="dve", waits=[("dve", otoks[2])])
        scr_row = ftl[0:1, T:2 * T]
        for half in range(2):
            base = 0 if half == 0 else 32
            w5 = [("dve", o4)]
            if half == 0:
                for bb in (6, 7):
                    if bb in bf_:
                        w5.append(bf_.pop(bb))
            sq1 = P.tt(sttl[:, half * 512:(half + 1) * 512],
                       o_t[:, half * 512:(half + 1) * 512],
                       o_t[:, half * 512:(half + 1) * 512], ALU.mult,
                       sem="dve", waits=[("dve", o4), ("dve", P.cnt["dve"])])
            P.mm(pb6[base:base + 1, :], onesc[:, :],
                 o_t[:, half * 512:(half + 1) * 512], start=True, stop=True,
                 waits=w5)
            P.mm(pb7[base:base + 1, :], onesc[:, :],
                 sttl[:, half * 512:(half + 1) * 512], start=True, stop=True,
                 waits=[("dve", sq1)], sem="pe")
        lnpc = P.cnt["pe"]
        rw0 = [("pe", lnpc), ("dve", P.cnt["dve"])]
        x1_ = P.act(mrow[:, 0:512], pb6[0:1, :], AF.Copy, scale=1.0 / 128,
                    sem="act", waits=rw0)
        x2_ = P.act(mrow[:, 512:T], pb6[32:33, :], AF.Copy, scale=1.0 / 128,
                    sem="act")
        x3_ = P.act(vrow[:, 0:512], pb7[0:1, :], AF.Copy, scale=1.0 / 128,
                    sem="act")
        x4_ = P.act(vrow[:, 512:T], pb7[32:33, :], AF.Copy, scale=1.0 / 128,
                    sem="act")
        bf_[6] = ("act", x4_)
        bf_[7] = ("act", x4_)
        v1_ = P.tt(scr_row, mrow[:, :], mrow[:, :], ALU.mult, sem="dve",
                   waits=[("act", x4_)])
        v2_ = P.tt(vrow[:, :], scr_row, vrow[:, :], ALU.subtract, sem="dve",
                   waits=[("dve", v1_)])
        a1_ = P.act(vrow[:, :], vrow[:, :], AF.Ln, bias=1e-5, scale=-1.0,
                    sem="act", waits=[("dve", v2_)])
        a2_ = P.act(vrow[:, :], vrow[:, :], AF.Exp, scale=-0.5, sem="act",
                    waits=[("act", a1_)])
        bdone = []
        for j, (srow, dstb) in enumerate(((mrow, sttl), (vrow, tmp))):
            for half in range(2):
                bk = 2 + (2 * j + half) % 2
                wl = [("act", a2_), ("dve", P.cnt["dve"])]
                if bk in bf_:
                    wl.append(bf_.pop(bk))
                if h == 0:
                    wl.append(lof)
                bm = P.mm(pb[bk][:, :], onesf_row[0:1, :],
                          srow[:, half * 512:(half + 1) * 512], start=True,
                          stop=True, sem="pe", waits=wl)
                bc2 = P.copy(dstb[:, half * 512:(half + 1) * 512], pb[bk][:, :],
                             sem="dve", waits=[("pe", bm)])
                bf_[bk] = ("dve", bc2)
                bdone.append(bc2)
        l1_ = P.tt(o_t[:, 0:512], o_t[:, 0:512], sttl[:, 0:512], ALU.subtract,
                   sem="dve", waits=[("dve", bdone[-1])])
        l2_ = P.tt(o_t[:, 512:T], o_t[:, 512:T], sttl[:, 512:T], ALU.subtract,
                   sem="dve", waits=[("dve", l1_)])
        l3_ = P.tt(o_t[:, 0:512], o_t[:, 0:512], tmp[:, 0:512], ALU.mult,
                   sem="dve", waits=[("dve", l2_)])
        l4_ = P.tt(o_t[:, 512:T], o_t[:, 512:T], tmp[:, 512:T], ALU.mult,
                   sem="dve", waits=[("dve", l3_)])
        ldg = P.dma("gpsimd", vgwb[:, :].rearrange("p (k c) -> p k c", k=16),
                    aps["wgf"][:, 128 * h:128 * (h + 1)].rearrange(
                        "(k p) c -> p k c", p=128), "w",
                    waits=[vmm_done])
        for half in range(2):
            bk = half
            wl = [ldg]
            if bk in bf_:
                wl.append(bf_.pop(bk))
            for k in range(DK):
                P.mm(pb[bk][:, :], vgwb[:, k * 128:(k + 1) * 128],
                     xc[:, k * T + half * 512: k * T + (half + 1) * 512],
                     start=(k == 0), stop=(k == DK - 1),
                     waits=wl if k == 0 else (),
                     sem="pe" if k == DK - 1 else None)
            gpc = P.cnt["pe"]
            ag = P.act(sttl[:, half * 512:(half + 1) * 512], pb[bk][:, :],
                       AF.Silu, sem="act", waits=[("pe", gpc),
                                                  ("dve", l4_ if half == 0 else P.cnt["dve"])])
            tile = outf_tiles[h]
            if isinstance(tile, list):
                tdst = tile[half]
            else:
                tdst = tile[:, half * 512:(half + 1) * 512]
            ow = [("act", ag), ("dve", l4_)]
            if h < 8:
                g_ = max(slotuse.get(("l1q", 0), 0), slotuse.get(("l1q", 1), 0),
                         slotuse.get(("l1k", 0), 0), slotuse.get(("l1k", 1), 0))
                ow.append(("pe", g_))
            elif h in (11, 12, 13, 14, 15):
                ow.append(("pe", dots_pe))
            ot = P.tt(tdst,
                      sttl[:, half * 512:(half + 1) * 512],
                      o_t[:, half * 512:(half + 1) * 512], ALU.mult,
                      sem="dve", waits=ow)
            bf_[bk] = ("act", ag)
        vg_free["wdma"] = ("pe", P.cnt["pe"])
        vg_free[("vtile", 0)] = ("dve", P.cnt["dve"])
        vg_free[("vtile", 1)] = vg_free[("vtile", 0)]
        outf_done.append(P.cnt["dve"])
    outf_ready = ("dve", P.cnt["dve"])

    emit_phaseB(P, env, w2_ap=aps["woutf"], n_i=DK, rhs_buf=outf_tiles,
                d_chunk=1, first_into_acc=False, hready=outf_ready)
    env["l1ffn_done"] = ("pe", P.cnt["pe"])
    env["l1ffn_bufs"] = {"qtl": qtl, "ktl": ktl, "extf": extf, "vgwb": vgwb,
                         "convsb": convsb,
                         "wvrb": wvrb, "vtile": vtile, "qkb": qkb,
                         "prodb": prodb, "protb": protb, "bd64b": bd64b,
                         "onesf_row": onesf_row, "onesf_load": lof,
                         "pb6": pb6, "pb7": pb7}


def emit_l1attn(P, env, aps):
    nc = P.nc
    EPS = 1e-8
    xnb = env["xnb"]; pb = env["psum"]; bf_ = env["bankfree"]
    slotuse = env["slotuse"]
    hbuf = env["h"]; sil = env["sil"]; tmp = env["tmp"]; sttl = env["sttl"]
    ftl = env["ftl"]; gtl = env["gtl"]
    w1b = env["w1b"]; w3b = env["w3b"]; w2b = env["w2b"]
    onesb = env["ones_sb"]; onesc = env["onescol_sb"]; ident = env["ident"]
    l1f = env["l1ffn_done"]
    lf = env["l1ffn_bufs"]
    qtl, ktl, extf, vgwb, wvrb, vtile, qkb, prodb, protb, bd64b, onesf_row = (
        lf["qtl"], lf["ktl"], lf["extf"], lf["vgwb"], lf["wvrb"], lf["vtile"],
        lf["qkb"], lf["prodb"], lf["protb"], lf["bd64b"], lf["onesf_row"])
    pb6, pb7 = lf["pb6"], lf["pb7"]
    convsb = lf["convsb"]
    kbuf = hbuf[:, 0:4 * T]
    vbuf = hbuf[:, 4 * T:8 * T]
    mknbuf = hbuf[:, 8 * T:8 * T + 6144]
    mkf_tiles = [qtl[:, :], ktl[:, :], extf[:, 0:T],
                 env["cbB"][:, :], gtl[:, 0:T], gtl[:, T:2 * T]]
    wmvt = qkb[:, 0:1024]
    roll_slices = [prodb[:, 0:128], prodb[:, 128:256], prodb[:, 256:384],
                   prodb[:, 384:512], qkb[:, 1024:1152], qkb[:, 1152:1280]]
    if "apsb" in env:
        biasb = env["apsb"][:, 0:72]
    else:
        biasb = nc.alloc_sbuf_tensor("biasb", [128, 72], F32)
    normTb = qkb[:, 1280:1280 + 96]
    dnormb = None
    coeftb = qkb[:, 1376:1376 + 32]
    memb = vgwb
    mknrow = ftl[0:1, 0:T]
    mknrow16 = gtl[0:1, T:T + T]
    sqrow = ftl[32:33, 0:T]

    lro_toks = []
    for ri in range(6):
        lro_toks.append(P.dma("gpsimd", roll_slices[ri],
                              aps["rollc"][:, ri * 128:(ri + 1) * 128],
                              "x", waits=[l1f]))
    lbi = P.dma("sync", biasb[:, 0:4], aps["bk_in"], "hx")
    lbv = P.dma("sync", biasb[:, 4:8], aps["bv_in"], "hx")
    lbmb = P.dma("sync", biasb[:, 8:24], aps["bmb_in"], "hx")
    lbq = P.dma("sync", biasb[:, 24:40], aps["bq_in"], "hx")
    lbg = P.dma("sync", biasb[:, 40:56], aps["bg_in"], "hx")
    lnt = P.dma("gpsimd", normTb[:, :].rearrange("p (k c) -> p k c", k=6),
                aps["normT_in"].rearrange("(k p) c -> p k c", p=128), "w")
    nb = P.tsc(biasb[:, 56:72], biasb[:, 24:40], -1.0, None, ALU.mult,
               sem="dve", waits=[lbq, lbg])

    def kv_evac(name, boff):
        btok = lbi if name == "k" else lbv
        def cb(mt, half, bank, pecnt):
            dst = (kbuf if name == "k" else vbuf)
            tok = P.tsc(dst[:, mt * T + half * 512: mt * T + (half + 1) * 512],
                        bank[:, :], biasb[:, boff + mt:boff + mt + 1], None,
                        ALU.add, sem="dve", waits=[("pe", pecnt), btok,
                                                   l1f if mt + half == 0 else ("pe", 0)])
            return ("dve", tok)
        return cb
    slotuse[("wkT", 0)] = max(slotuse.get(("Bw2", 0), 0), env["l1ffn_done"][1])
    slotuse[("wkT", 1)] = slotuse[("wkT", 0)]
    slotuse[("wvT", 0)] = slotuse[("wkT", 0)]
    slotuse[("wvT", 1)] = slotuse[("wkT", 0)]
    slotuse[("knw", 0)] = slotuse[("wkT", 0)]
    slotuse[("knw", 1)] = slotuse[("wkT", 0)]
    emit_projA(P, env, w_ap=aps["wkT"], n_m=4, evac_cb=kv_evac("k", 0),
               tagkey="wkT", chunk=1, banks=(0, 1), rhs_ready=None,
               wbuf=w2b, wslot=2048)
    for s_ in (0, 1):
        slotuse[("wvT", s_)] = max(slotuse.get(("wvT", s_), 0),
                                   slotuse.get(("wkT", 0), 0),
                                   slotuse.get(("wkT", 1), 0))
    emit_projA(P, env, w_ap=aps["wvT"], n_m=4, evac_cb=kv_evac("v", 4),
               tagkey="wvT", chunk=1, banks=(0, 1), rhs_ready=None,
               wbuf=w2b, wslot=2048)

    bkB = sil[:, 0:512]
    wl = [("dve", P.cnt["dve"])]
    if 2 in bf_:
        wl.append(bf_.pop(2))
    lkr = P.dma("sync", sttl[0:1, 0:512], aps["bkrow"], "hx", waits=[l1f])
    bm = P.mm(pb[2][:, :], onesf_row[0:1, :], sttl[0:1, 0:512], start=True,
              stop=True, sem="pe", waits=wl + [lkr, lf["onesf_load"]])
    bkc = P.copy(bkB, pb[2][:, :], sem="dve", waits=[("pe", bm)])
    bf_[2] = ("dve", bkc)
    kn_tiles = [w1b[:, i * 512:(i + 1) * 512] for i in range(8)]
    for grp in range(2):
        wdc = {}
        for k in range(DK):
            slot = k % 2
            wl = []
            if ("knw", slot) in slotuse:
                wl.append(("pe", slotuse[("knw", slot)]))
            wdc[k] = P.dma("gpsimd", w3b[:, slot * 2048:slot * 2048 + 512],
                           aps["wkT"][k * 128:(k + 1) * 128, :], "w", waits=wl)
            for ti in range(4):
                tt = grp * 4 + ti
                bk_ = 2 + ti
                wl2 = [wdc[k]] if ti == 0 else []
                if k == 0 and bk_ in bf_:
                    wl2.append(bf_.pop(bk_))
                if k == 0 and ti == 0 and grp == 0:
                    wl2.append(env["xn1_ready"])
                P.mm(pb[bk_][:, :], xnb[:, k * T + tt * 128:k * T + (tt + 1) * 128],
                     w3b[:, slot * 2048:slot * 2048 + 512],
                     start=(k == 0), stop=(k == DK - 1),
                     waits=wl2,
                     sem="pe" if ti == 3 else None)
            slotuse[("knw", slot)] = P.cnt["pe"]
        pcg = P.cnt["pe"]
        for ti in range(4):
            tt = grp * 4 + ti
            tok = P.tt(kn_tiles[tt], pb[2 + ti][:, :], bkB, ALU.add, sem="dve",
                       waits=[("pe", pcg), ("dve", bkc)])
            bf_[2 + ti] = ("dve", tok)
    kn_ready = P.cnt["dve"]
    bf_[("mbw", 0)] = ("pe", P.cnt["pe"])
    bf_[("mbw", 1)] = ("pe", P.cnt["pe"])
    bf_[("memd", 0)] = env["l1ffn_done"]
    bf_[("memd", 1)] = env["l1ffn_done"]

    for hk in range(4):
        for tt in range(8):
            xcat = w2b[:, 2048:2304]
            xcat2 = w2b[:, 2304:2816]
            src = kn_tiles[tt][:, hk * 128:(hk + 1) * 128]
            awl = [("dve", kn_ready), ("dve", P.cnt["dve"]),
                   ("act", P.cnt["act"]), ("pe", P.cnt["pe"])]
            a1 = P.act(xcat[:, 0:128], src, AF.Relu, sem="act", waits=awl)
            a2 = P.act(xcat[:, 128:256], src, AF.Relu, scale=-1.0, sem="act",
                       waits=[("act", a1)])
            c1 = P.tt(xcat2[:, 0:256], xcat[:, :], xcat[:, :], ALU.max,
                      sem="dve", waits=[("act", a2)])
            c2 = P.tt(xcat2[:, 256:512], xcat[:, :], xcat[:, :], ALU.max,
                      sem="dve", waits=[("act", a2), ("dve", c1)])
            mkr = sil[:, 0:768]
            for j in (1, 2, 3):
                P.tt(mkr[:, (j - 1) * 256:j * 256], xcat[:, :],
                     xcat2[:, 256 - j:512 - j], ALU.mult, sem="dve",
                     waits=[("dve", c2), ("dve", P.cnt["dve"])])
            m1 = P.cnt["dve"]
            sqw = [("dve", m1)]
            if ("tmpd",) in bf_:
                sqw.append(bf_.pop(("tmpd",)))
            sq_ = P.tt(tmp[:, 0:768], mkr[:, :], mkr[:, :], ALU.mult, sem="dve",
                       waits=sqw)
            r1 = P.reduce(sttl[:, 0:1], tmp[:, 0:768], ALU.add, sem="dve",
                          waits=[("dve", sq_)])
            s1 = P.act(sttl[:, 1:2], sttl[:, 0:1], AF.Ln, sem="act",
                       waits=[("dve", r1)])
            s2 = P.act(sttl[:, 1:2], sttl[:, 1:2], AF.Exp, scale=0.5, sem="act",
                       waits=[("act", s1)])
            s3 = P.tsc(sttl[:, 1:2], sttl[:, 1:2], 1e-12, None, ALU.max,
                       sem="dve", waits=[("act", s2)])
            s4 = P.recip(sttl[:, 2:3], sttl[:, 1:2], sem="dve", waits=[("dve", s3)])
            mk_ = P.tsc(mknbuf[:, tt * 768:(tt + 1) * 768], mkr[:, :],
                        sttl[:, 2:3], None, ALU.mult, sem="dve",
                        waits=[("dve", s4)])
            P.copy(sttl[:, 3 + tt:4 + tt], sttl[:, 1:2], sem="dve",
                   waits=[("dve", s4), ("dve", P.cnt["dve"])])
        sq8 = P.cnt["dve"]
        for tt in range(8):
            bk_ = (tt % 2)
            wl = [("dve", sq8)]
            if bk_ in bf_:
                wl.append(bf_.pop(bk_))
            tc = P.transpose(pb[bk_][0:1, 0:128], sttl[:, 3 + tt:4 + tt],
                             ident[:, :], sem="pe", waits=wl)
            cc = P.copy(mknrow[:, tt * 128:(tt + 1) * 128], pb[bk_][0:1, 0:128],
                        sem="dve", waits=[("pe", tc)])
            bf_[bk_] = ("dve", cc)
        mkn_done = P.cnt["dve"]

        c0t = w2b[:, 0:T]; c1t = w2b[:, T:2 * T]
        ra = P.act(c0t, kbuf[:, hk * T:(hk + 1) * T], AF.Relu, sem="act",
                   waits=[("dve", P.cnt["dve"]), ("pe", P.cnt["pe"])])
        rb = P.act(c1t, kbuf[:, hk * T:(hk + 1) * T], AF.Relu, scale=-1.0,
                   sem="act", waits=[("act", ra)])
        for j in (1, 2, 3):
            for half_t in range(2):
                dst = mkf_tiles[(j - 1) * 2 + half_t]
                bk_ = 2 + (j + half_t) % 2
                src_main = c0t if half_t == 0 else c1t
                src_wrap = c1t if half_t == 0 else c0t
                wl = [("act", rb), ("dve", P.cnt["dve"])]
                if hk == 0:
                    wl.extend(lro_toks)
                if bk_ in bf_:
                    wl.append(bf_.pop(bk_))
                tok = None
                for hh in range(2):
                    wlh = wl if hh == 0 else [("dve", tok)]
                    P.mm(pb[bk_][:, 0:512], roll_slices[2 * (j - 1)],
                         src_main[:, hh * 512:(hh + 1) * 512], start=True, stop=False,
                         waits=wlh)
                    P.mm(pb[bk_][:, 0:512], roll_slices[2 * (j - 1) + 1],
                         src_wrap[:, hh * 512:(hh + 1) * 512], start=False, stop=True,
                         sem="pe")
                    pc = P.cnt["pe"]
                    x_rep = c0t if half_t == 0 else c1t
                    tok = P.tt(dst[:, hh * 512:(hh + 1) * 512],
                               x_rep[:, hh * 512:(hh + 1) * 512],
                               pb[bk_][:, 0:512], ALU.mult, sem="dve",
                               waits=[("pe", pc)])
                bf_[bk_] = ("dve", tok)
        mkf_done = ("dve", P.cnt["dve"])

        den4 = sil[32:36, 0:T]
        rec4 = sil[64:68, 0:T]
        coef4 = sil[0:4, 0:T]
        nbc4 = tmp[32:36, 0:T]
        for half in range(2):
            bk_ = 4 + half
            wl = [mkf_done, lnt]
            if bk_ in bf_:
                wl.append(bf_.pop(bk_))
            for fb in range(6):
                P.mm(pb[bk_][0:4, 0:512],
                     normTb[:, fb * 16 + 4 * hk: fb * 16 + 4 * hk + 4],
                     mkf_tiles[fb][:, half * 512:(half + 1) * 512],
                     start=(fb == 0), stop=(fb == 5),
                     waits=wl if fb == 0 else (),
                     sem="pe" if fb == 5 else None)
            dpc = P.cnt["pe"]
            bk2 = 2 + half
            wl2 = [("dve", mkn_done)]
            if bk2 in bf_:
                wl2.append(bf_.pop(bk2))
            bmn = P.mm(pb[bk2][0:4, 0:512], onesf_row[0:1, 0:4],
                       mknrow[:, half * 512:(half + 1) * 512], start=True,
                       stop=True, sem="pe", waits=wl2)
            cnb = P.copy(nbc4[:, half * 512:(half + 1) * 512], pb[bk2][0:4, :],
                         sem="dve", waits=[("pe", bmn), ("dve", P.cnt["dve"])])
            bf_[bk2] = ("dve", cnb)
            da = P.stt(den4[:, half * 512:(half + 1) * 512],
                       nbc4[:, half * 512:(half + 1) * 512], 1e-8,
                       pb[bk_][0:4, 0:512], ALU.mult, ALU.add, sem="dve",
                       waits=[("pe", dpc), ("dve", cnb)])
            bf_[bk_] = ("dve", da)
        dd = P.cnt["dve"]
        r4 = P.recip(rec4[:, :], den4[:, :], sem="dve", waits=[("dve", dd)])
        r4b = P.copy(sttl[0:4, 0:T], rec4[:, :], sem="dve", waits=[("dve", r4)])
        rec16 = mknrow16
        rec16 = vtile[0:4, 0:T]
        rc16 = P.copy(rec16, rec4[:, :], sem="dve", waits=[("dve", r4)])
        cf0 = P.recip(nbc4[:, :], nbc4[:, :], sem="dve",
                      waits=[("dve", dd), ("dve", r4b)])
        cf1 = P.tt(coef4[:, :], den4[:, :], nbc4[:, :], ALU.mult, sem="dve",
                   waits=[("dve", cf0)])
        cf2 = P.tsc(coef4[:, :], coef4[:, :], -1.0, 1.0, ALU.mult, ALU.add,
                    sem="dve", waits=[("dve", cf1)])
        cf3 = P.tsc(coef4[:, :], coef4[:, :], 0.0, 1.0, ALU.max, ALU.min,
                    sem="dve", waits=[("dve", cf2)])
        for tt in range(8):
            bk_ = tt % 2
            wl = [("dve", cf3)]
            if bk_ in bf_:
                wl.append(bf_.pop(bk_))
            tc = P.transpose(pb[bk_][0:128, 0:4], coef4[:, tt * 128:(tt + 1) * 128],
                             ident[0:4, 0:4], sem="pe", waits=wl)
            cc = P.copy(coeftb[:, tt * 4:(tt + 1) * 4], pb[bk_][0:128, 0:4],
                        sem="dve", waits=[("pe", tc)])
            bf_[bk_] = ("dve", cc)
        coeft_done = P.cnt["dve"]
        for fb in range(6):
            bk_ = 6 if fb % 2 == 0 else 7
            pbx = pb6 if fb % 2 == 0 else pb7
            wl = [("dve", coeft_done)]
            if ("d67", bk_) in bf_:
                wl.append(bf_.pop(("d67", bk_)))
            for tt in range(8):
                P.mm(pbx[:, 0:4], mknbuf[:, tt * 768 + fb * 128: tt * 768 + (fb + 1) * 128],
                     coeftb[:, tt * 4:(tt + 1) * 4], start=(tt == 0), stop=(tt == 7),
                     waits=wl if tt == 0 else (),
                     sem="pe" if tt == 7 else None)
            pcd = P.cnt["pe"]
            dnw = [("pe", pcd)]
            if ("dnst",) in bf_ and fb == 0:
                dnw.append(bf_.pop(("dnst",)))
            dn = P.copy(convsb[:, fb * 4:(fb + 1) * 4], pbx[:, 0:4], sem="dve",
                        waits=dnw)
            bf_[("d67", bk_)] = ("dve", dn)
        dnst = P.dma("sync",
                     aps["dnorm_out"][:, hk * 4:(hk + 1) * 4].rearrange(
                         "(k p) g -> p k g", p=128),
                     convsb[:, 0:24].rearrange("p (k g) -> p k g", k=6), "st",
                     waits=[("dve", P.cnt["dve"])])
        bf_[("dnst",)] = dnst

        for hq in range(4):
            h = hk * 4 + hq
            mslot = h % 2
            wl = [bf_.pop(("memd", mslot))] if ("memd", mslot) in bf_ else []
            lmem = P.dma("gpsimd", memb[:, mslot * 768:(mslot + 1) * 768]
                         .rearrange("p (k c) -> p k c", k=6),
                         aps["memC"][h].rearrange("(k p) c -> p k c", p=128),
                         "w", waits=wl)
            wl = [bf_.pop(("mbw", mslot))] if ("mbw", mslot) in bf_ else []
            lmb = P.dma("gpsimd", w3b[:, mslot * 2048: mslot * 2048 + 2048]
                        .rearrange("p (k c) -> p k c", k=16),
                        aps["wmbT"][:, h * 128:(h + 1) * 128].rearrange(
                            "(k p) c -> p k c", p=128), "w", waits=wl)
            for half in range(2):
                bk_ = 2 + half
                wl = [lmem, mkf_done]
                if bk_ in bf_:
                    wl.append(bf_.pop(bk_))
                for fb in range(6):
                    P.mm(pb[bk_][:, :], memb[:, mslot * 768 + fb * 128: mslot * 768 + (fb + 1) * 128],
                         mkf_tiles[fb][:, half * 512:(half + 1) * 512],
                         start=(fb == 0), stop=(fb == 5),
                         waits=wl if fb == 0 else ())
                bk2 = 4 + half
                wl2 = [lmb]
                if bk2 in bf_:
                    wl2.append(bf_.pop(bk2))
                for k in range(DK):
                    P.mm(pb[bk2][:, :], w3b[:, mslot * 2048 + k * 128: mslot * 2048 + (k + 1) * 128],
                         xnb[:, k * T + half * 512: k * T + (half + 1) * 512],
                         start=(k == 0), stop=(k == DK - 1),
                         waits=wl2 if k == 0 else (),
                         sem="pe" if k == DK - 1 else None)
                pcb = P.cnt["pe"]
                bk3 = 0 + half
                wl3 = [("dve", rc16)]
                if bk3 in bf_:
                    wl3.append(bf_.pop(bk3))
                sb_ = P.mm(pb[bk3][:, :], env["sel16"][0:4, hq * 128:(hq + 1) * 128],
                           rec16[:, half * 512:(half + 1) * 512], start=True,
                           stop=True, sem="pe", waits=wl3)
                rb_ = P.copy(sttl[:, half * 512:(half + 1) * 512], pb[bk3][:, :],
                             sem="dve", waits=[("pe", sb_), ("dve", P.cnt["dve"])])
                bf_[bk3] = ("dve", rb_)
                pvw = [("dve", rb_)]
                if ("tmpd",) in bf_:
                    pvw.append(bf_.pop(("tmpd",)))
                pv = P.tt(tmp[:, half * 512:(half + 1) * 512], pb[bk_][:, :],
                          sttl[:, half * 512:(half + 1) * 512], ALU.mult,
                          sem="dve", waits=pvw)
                bf_[bk_] = ("dve", pv)
                mv = P.tt(tmp[:, half * 512:(half + 1) * 512],
                          vbuf[:, hk * T + half * 512: hk * T + (half + 1) * 512],
                          tmp[:, half * 512:(half + 1) * 512], ALU.subtract,
                          sem="dve", waits=[("dve", pv)])
                am = P.act(sil[:, half * 512:(half + 1) * 512], pb[bk2][:, :],
                           AF.Sigmoid, bias=biasb[:, 8 + h:9 + h], sem="act",
                           waits=[("pe", pcb), ("dve", P.cnt["dve"]), lbmb])
                bf_[bk2] = ("act", am)
                wv_ = P.tt(ftl[:, T + half * 512:T + (half + 1) * 512],
                           tmp[:, half * 512:(half + 1) * 512],
                           sil[:, half * 512:(half + 1) * 512], ALU.mult,
                           sem="dve", waits=[("act", am), ("dve", mv)])
            wmv_done = P.cnt["dve"]
            for tt in range(8):
                bk_ = tt % 2
                wl = [("dve", wmv_done)]
                if bk_ in bf_:
                    wl.append(bf_.pop(bk_))
                tc = P.transpose(pb[bk_][:, 0:128],
                                 ftl[:, T + tt * 128:T + (tt + 1) * 128],
                                 ident[:, :], sem="pe", waits=wl)
                cc = P.copy(wmvt[:, tt * 128:(tt + 1) * 128], pb[bk_][:, 0:128],
                            sem="dve", waits=[("pe", tc)])
                bf_[bk_] = ("dve", cc)
            wmvt_done = P.cnt["dve"]
            for fb in range(6):
                bk_ = 2 + fb % 2
                wl = [("dve", wmvt_done)]
                if bk_ in bf_:
                    wl.append(bf_.pop(bk_))
                for tt in range(8):
                    P.mm(pb[bk_][:, 0:128],
                         mknbuf[:, tt * 768 + fb * 128: tt * 768 + (fb + 1) * 128],
                         wmvt[:, tt * 128:(tt + 1) * 128],
                         start=(tt == 0), stop=(tt == 7),
                         waits=wl if tt == 0 else (),
                         sem="pe" if tt == 7 else None)
                pcd = P.cnt["pe"]
                dc_ = P.copy(sil[:, fb * 128:(fb + 1) * 128] if False else
                             tmp[:, fb * 128:(fb + 1) * 128],
                             pb[bk_][:, 0:128], sem="dve", waits=[("pe", pcd)])
                bf_[bk_] = ("dve", dc_)
            dstore = P.dma("sync", aps["dmem_out"][h].rearrange("(k p) c -> p k c", p=128),
                           tmp[:, 0:768].rearrange("p (k c) -> p k c", k=6), "st",
                           waits=[("dve", P.cnt["dve"])])
            bf_[("tmpd",)] = dstore
            bf_[("memd", mslot)] = ("pe", P.cnt["pe"])
            bf_[("mbw", mslot)] = ("pe", P.cnt["pe"])


    for sft in (0, 1):
        if ("mbw", sft) in bf_:
            bf_[("mqw", sft)] = bf_.pop(("mbw", sft))
    for h in range(16):
        mslot = h % 2
        wl = [bf_.pop(("mqw", mslot))] if ("mqw", mslot) in bf_ else []
        lq = P.dma("gpsimd", w3b[:, mslot * 2048: mslot * 2048 + 2048]
                   .rearrange("p (k c) -> p k c", k=16),
                   aps["wqT2"][:, h * 128:(h + 1) * 128].rearrange(
                       "(k p) c -> p k c", p=128), "w", waits=wl)
        c0t = w2b[:, 0:T]; c1t = w2b[:, T:2 * T]
        for half in range(2):
            bk_ = 0 + half
            wl = [lq]
            if bk_ in bf_:
                wl.append(bf_.pop(bk_))
            for k in range(DK):
                P.mm(pb[bk_][:, :], w3b[:, mslot * 2048 + k * 128: mslot * 2048 + (k + 1) * 128],
                     xnb[:, k * T + half * 512: k * T + (half + 1) * 512],
                     start=(k == 0), stop=(k == DK - 1),
                     waits=wl if k == 0 else (),
                     sem="pe" if k == DK - 1 else None)
            pcq = P.cnt["pe"]
            aw = [("pe", pcq), ("dve", P.cnt["dve"]), ("act", P.cnt["act"])]
            a1 = P.act(c0t[:, half * 512:(half + 1) * 512], pb[bk_][:, :],
                       AF.Relu, bias=biasb[:, 24 + h:25 + h], sem="act", waits=aw)
            a2 = P.act(c1t[:, half * 512:(half + 1) * 512], pb[bk_][:, :],
                       AF.Relu, bias=biasb[:, 56 + h:57 + h], scale=-1.0,
                       sem="act", waits=[("act", a1)])
            bf_[bk_] = ("act", a2)
        relu_done = P.cnt["act"]
        for j in (1, 2, 3):
            for half_t in range(2):
                dst = mkf_tiles[(j - 1) * 2 + half_t]
                bk_ = 2 + (j + half_t) % 2
                src_main = c0t if half_t == 0 else c1t
                src_wrap = c1t if half_t == 0 else c0t
                wl = [("act", relu_done), ("dve", P.cnt["dve"])]
                if h == 0:
                    wl.extend(lro_toks)
                if ("mkf_war",) in bf_:
                    wl.extend(bf_.pop(("mkf_war",)))
                if bk_ in bf_:
                    wl.append(bf_.pop(bk_))
                tok = None
                for hh in range(2):
                    wlh = wl if hh == 0 else [("dve", tok)]
                    P.mm(pb[bk_][:, 0:512], roll_slices[2 * (j - 1)],
                         src_main[:, hh * 512:(hh + 1) * 512], start=True,
                         stop=False, waits=wlh)
                    P.mm(pb[bk_][:, 0:512], roll_slices[2 * (j - 1) + 1],
                         src_wrap[:, hh * 512:(hh + 1) * 512], start=False,
                         stop=True, sem="pe")
                    pc = P.cnt["pe"]
                    x_rep = c0t if half_t == 0 else c1t
                    tok = P.tt(dst[:, hh * 512:(hh + 1) * 512],
                               x_rep[:, hh * 512:(hh + 1) * 512],
                               pb[bk_][:, 0:512], ALU.mult, sem="dve",
                               waits=[("pe", pc)])
                bf_[bk_] = ("dve", tok)
        mq_done = P.cnt["dve"]
        for ti in range(6):
            sq_ = P.tt(sil[:, 0:T], mkf_tiles[ti][:, :], mkf_tiles[ti][:, :],
                       ALU.mult, sem="dve",
                       waits=[("dve", mq_done), ("dve", P.cnt["dve"]),
                              ("pe", P.cnt["pe"])])
            for half in range(2):
                bk_ = 4 + half
                wl = [("dve", sq_)]
                if ti == 0 and bk_ in bf_:
                    wl.append(bf_.pop(bk_))
                P.mm(pb[bk_][0:1, :], onesc[:, :],
                     sil[:, half * 512:(half + 1) * 512],
                     start=(ti == 0), stop=(ti == 5),
                     waits=wl, sem="pe" if half == 1 else None)
        pcs = P.cnt["pe"]
        sqw1 = [("pe", pcs)]
        if ("sqst",) in bf_:
            sqw1.append(bf_.pop(("sqst",)))
        sq1 = P.act(sqrow[:, 0:512], pb[4][0:1, :], AF.Copy, sem="act",
                    waits=sqw1)
        sq2 = P.act(sqrow[:, 512:T], pb[5][0:1, :], AF.Copy, sem="act",
                    waits=[("pe", pcs)])
        bf_[4] = ("act", sq2)
        bf_[5] = ("act", sq2)
        sst = P.dma("sync", aps["sqmq_out"][0:1, h * T:(h + 1) * T], sqrow[:, :],
                    "st", waits=[("act", sq2)])
        bf_[("sqst",)] = sst
        mqs = []
        for ti in range(6):
            mqs.append(P.dma("sync", aps["mq_out"][h, ti], mkf_tiles[ti][:, :],
                             "st", waits=[("dve", mq_done)]))
        bf_[("mqw", mslot)] = ("pe", P.cnt["pe"])
        bf_[("mkf_war",)] = mqs

    gstok = {}
    def g_evac(mt, half, bank, pecnt):
        aw = [("pe", pecnt), ("act", P.cnt["act"])]
        if half in gstok:
            aw.append(gstok.pop(half))
        ag = P.act(prodb[:, half * 256:(half + 1) * 256][:, 0:0] if False else
                   prodb[:, 0:512] if half == 0 else qkb[:, 1024:1536],
                   bank[:, :], AF.Sigmoid,
                   bias=biasb[:, 40 + mt:41 + mt], sem="act", waits=aw)
        src = prodb[:, 0:512] if half == 0 else qkb[:, 1024:1536]
        gs = P.dma("sync", aps["gate_out"][mt, :, half * 512:(half + 1) * 512],
                   src, "st", waits=[("act", ag)])
        gstok[half] = gs
        return gs
    slotuse[("wgT2", 0)] = P.cnt["pe"]
    slotuse[("wgT2", 1)] = P.cnt["pe"]
    emit_projA(P, env, w_ap=aps["wgT2"], n_m=DK, evac_cb=g_evac, tagkey="wgT2",
               chunk=1, banks=(0, 1), rhs_ready=None, wbuf=w2b, wslot=2048)

    xTb = env["acc"]
    for dk in range(DK):
        P.tt(xTb[:, dk * T:(dk + 1) * T], xTb[:, dk * T:(dk + 1) * T],
             xnb[:, dk * T:(dk + 1) * T], ALU.add, sem="dve",
             waits=[("dve", P.cnt["dve"]), ("pe", P.cnt["pe"])])
    xp = P.cnt["dve"]
    P.dma("sync", aps["xpart_out"].rearrange("k p c -> p k c"),
          xTb[:, :].rearrange("p (k c) -> p k c", k=DK), "st",
          waits=[("dve", xp)])


def emit_launchC(P, aps):
    nc = P.nc
    EPS = 1e-8
    xT = nc.alloc_sbuf_tensor("xTc", [128, DK * T], F32)
    memnew = nc.alloc_sbuf_tensor("memnew", [128, 2 * 768], BF16)
    dsum = nc.alloc_sbuf_tensor("dsum", [128, 768], F32)
    dload = nc.alloc_sbuf_tensor("dload", [128, 2 * 768], F32)
    nnorm = nc.alloc_sbuf_tensor("nnorm", [128, 96], F32)
    nnorm16 = nc.alloc_sbuf_tensor("nnorm16", [128, 96], BF16)
    mqb = nc.alloc_sbuf_tensor("mqb", [128, 2 * 6 * T], BF16)
    gtb = nc.alloc_sbuf_tensor("gtb", [128, 2 * T], BF16)
    sqr = nc.alloc_sbuf_tensor("sqr", [1, T], F32)
    drow = nc.alloc_sbuf_tensor("drow", [1, T], F32)
    rrow16 = nc.alloc_sbuf_tensor("rrow16", [1, T], BF16)
    rcpB = nc.alloc_sbuf_tensor("rcpB", [128, T], F32)
    memh = nc.alloc_sbuf_tensor("memh", [128, T], F32)
    onat = nc.alloc_sbuf_tensor("onat", [128, D], F32)
    identnp = nc.inline_tensor(np.eye(128, dtype=np.float32), name="identC")
    ident = nc.alloc_sbuf_tensor("identC_sb", [128, 128], F32)
    onesfnp = nc.inline_tensor(np.ones((1, 128), np.float32), name="onesfC")
    onesf = nc.alloc_sbuf_tensor("onesfC_sb", [1, 128], F32)
    onesbnp = nc.inline_tensor(np.ones((1, 128), np.float32).astype(ml_dtypes.bfloat16),
                               name="onesbC")
    onesb = nc.alloc_sbuf_tensor("onesbC_sb", [1, 128], BF16)
    pb = [nc.alloc_psum_tensor(f"pc{i}", [128, 512], F32) for i in range(8)]
    bf_ = {}

    lx = P.dma("sync", xT[:, :].rearrange("p (k c) -> p k c", k=DK),
               aps["xpart_in"].rearrange("k p c -> p k c"), "hx")
    lid = P.dma("sync", ident[:, :], identnp.ap(), "hx")
    lof = P.dma("sync", onesf[:, :], onesfnp.ap(), "hx")
    lob = P.dma("gpsimd", onesb[:, :], onesbnp.ap(), "x")

    t1 = None
    for c in range(8):
        sl = c % 2
        ln_ = P.dma("sync", nnorm16[:, :].rearrange("p (k g) -> p k g", k=6) if False
                    else dload[:, sl * 768:sl * 768 + 96].rearrange(
                        "p (k g) -> p k g", k=6),
                    aps["dnparts"][c].rearrange("(k p) g -> p k g", p=128), "hx",
                    waits=[bf_.pop(("dn", sl))] if ("dn", sl) in bf_ else [])
        if c == 0:
            t1 = P.copy(nnorm[:, :], dload[:, 0:96], sem="dve", waits=[ln_])
        else:
            t1 = P.tt(nnorm[:, :], nnorm[:, :], dload[:, sl * 768:sl * 768 + 96],
                      ALU.add, sem="dve", waits=[ln_, ("dve", t1)])
        bf_[("dn", sl)] = ("dve", t1)
    lnt = P.dma("gpsimd", dload[:, 768:768 + 96].rearrange("p (k g) -> p k g", k=6),
                aps["normT_in"].rearrange("(k p) g -> p k g", p=128), "w",
                waits=[bf_.pop(("dn", 1))])
    t2 = P.tt(nnorm[:, :], nnorm[:, :], dload[:, 768:768 + 96], ALU.add,
              sem="dve", waits=[lnt, ("dve", t1)])
    t3 = P.copy(nnorm16[:, :], nnorm[:, :], sem="dve", waits=[("dve", t2)])
    nn_done = ("dve", t3)
    bf_.pop(("dn", 0), None)

    sqld = P.dma("sync", drow[0:1, :], aps["sqmq_in"][0:1, 0:T], "hx") if False else None
    hstore = []
    for h in range(16):
        sl = h % 2
        tok = None
        for c in range(8):
            wl = [bf_.pop(("dl", sl, c % 2))] if ("dl", sl, c % 2) in bf_ else []
            ld = P.dma("sync", dload[:, (c % 2) * 768:(c % 2) * 768 + 768]
                       .rearrange("p (k c) -> p k c", k=6),
                       aps["dparts"][c, h].rearrange("(k p) c -> p k c", p=128),
                       "hx", waits=wl)
            if c == 0:
                tok = P.copy(dsum[:, :], dload[:, 0:768], sem="dve",
                             waits=[ld, ("dve", P.cnt["dve"])])
            else:
                tok = P.tt(dsum[:, :], dsum[:, :],
                           dload[:, (c % 2) * 768:(c % 2) * 768 + 768], ALU.add,
                           sem="dve", waits=[ld, ("dve", tok)])
            bf_[("dl", sl, c % 2)] = ("dve", tok)
        lm = P.dma("gpsimd", memnew[:, sl * 768:(sl + 1) * 768]
                   .rearrange("p (k c) -> p k c", k=6),
                   aps["memC"][h].rearrange("(k p) c -> p k c", p=128), "w",
                   waits=[bf_.pop(("mem", sl))] if ("mem", sl) in bf_ else [])
        lmf = P.dma("sync", dload[:, (0) * 768:768].rearrange("p (k c) -> p k c", k=6),
                    aps["memC"][h].rearrange("(k p) c -> p k c", p=128), "hx",
                    waits=[("dve", tok)])
        a1 = P.tt(dsum[:, :], dsum[:, :], dload[:, 0:768], ALU.add, sem="dve",
                  waits=[lmf, ("dve", tok)])
        a2 = P.copy(memnew[:, sl * 768:(sl + 1) * 768], dsum[:, :], sem="dve",
                    waits=[("dve", a1), lm])
        bf_[("mem", sl)] = ("dve", a2)
        bf_[("dl", sl, 0)] = ("dve", a1)
        lq = P.dma("sync", mqb[:, sl * 6 * T:(sl + 1) * 6 * T]
                   .rearrange("p (k c) -> p k c", k=6),
                   aps["mq_in"][h].rearrange("k p c -> p k c"), "hx",
                   waits=[bf_.pop(("mq", sl))] if ("mq", sl) in bf_ else [])
        lg = P.dma("sync", gtb[:, sl * T:(sl + 1) * T], aps["gate_in"][h], "hx",
                   waits=[bf_.pop(("gt", sl))] if ("gt", sl) in bf_ else [])
        for half in range(2):
            bk = half
            wl = [lq, ("dve", a2)]
            if bk in bf_:
                wl.append(bf_.pop(bk))
            for fb in range(6):
                P.mm(pb[bk][:, :], memnew[:, sl * 768 + fb * 128: sl * 768 + (fb + 1) * 128],
                     mqb[:, sl * 6 * T + fb * T + half * 512: sl * 6 * T + fb * T + (half + 1) * 512],
                     start=(fb == 0), stop=(fb == 5),
                     waits=wl if fb == 0 else ())
            bk2 = 2 if half == 0 else 7
            wl2 = [nn_done, lq]
            if bk2 in bf_:
                wl2.append(bf_.pop(bk2))
            for fb in range(6):
                P.mm(pb[bk2][0:1, 0:512],
                     nnorm16[:, fb * 16 + h:fb * 16 + h + 1],
                     mqb[:, sl * 6 * T + fb * T + half * 512: sl * 6 * T + fb * T + (half + 1) * 512],
                     start=(fb == 0), stop=(fb == 5),
                     waits=wl2 if fb == 0 else (),
                     sem="pe" if fb == 5 else None)
        pcd = P.cnt["pe"]
        sq1 = P.act(sqr[0:1, :], aps_sq_row(P, aps, h), AF.Ln, sem="act",
                    waits=[("pe", pcd)]) if False else None
        lsq = P.dma("sync", sqr[0:1, :], aps["sqmq_in"][0:1, h * T:(h + 1) * T],
                    "hx", waits=[bf_.pop(("sq",))] if ("sq",) in bf_ else [])
        s1 = P.act(drow[0:1, :], sqr[0:1, :], AF.Ln, sem="act",
                   waits=[lsq, ("act", P.cnt.get("act", 0)),
                          ("dve", P.cnt.get("dve", 0))])
        s2 = P.act(drow[0:1, :], drow[0:1, :], AF.Exp, scale=0.5, sem="act",
                   waits=[("act", s1)])
        bf_[("sq",)] = ("act", s2)
        d1 = P.stt(drow[0:1, :], drow[0:1, :], EPS, pb[2][0:1, 0:T], ALU.mult,
                   ALU.add, sem="dve", waits=[("act", s2), ("pe", pcd)]) if False else None
        dtoks = []
        for half in range(2):
            bk2 = 2 if half == 0 else 7
            dd = P.stt(drow[0:1, half * 512:(half + 1) * 512],
                       drow[0:1, half * 512:(half + 1) * 512], EPS,
                       pb[bk2][0:1, 0:512], ALU.mult, ALU.add,
                       sem="dve", waits=[("act", s2), ("pe", pcd),
                                         ("dve", P.cnt["dve"])])
            dtoks.append(dd)
            bf_[bk2] = ("dve", dd)
        r1 = P.recip(drow[0:1, :], drow[0:1, :], sem="dve", waits=[("dve", dtoks[-1])])
        r2 = P.copy(rrow16[0:1, :], drow[0:1, :], sem="dve", waits=[("dve", r1)])
        for half in range(2):
            bk3 = 3 + half % 2
            wl3 = [("dve", r2), lob]
            if bk3 in bf_:
                wl3.append(bf_.pop(bk3))
            bm = P.mm(pb[bk3][:, :], onesb[0:1, :],
                      rrow16[0:1, half * 512:(half + 1) * 512], start=True,
                      stop=True, sem="pe", waits=wl3)
            bc = P.copy(rcpB[:, half * 512:(half + 1) * 512], pb[bk3][:, :],
                        sem="dve", waits=[("pe", bm)])
            bf_[bk3] = ("dve", bc)
            m1 = P.tt(memh[:, half * 512:(half + 1) * 512], pb[half][:, :],
                      rcpB[:, half * 512:(half + 1) * 512], ALU.mult, sem="dve",
                      waits=[("dve", bc)])
            bf_[half] = ("dve", m1)
            m2 = P.tt(memh[:, half * 512:(half + 1) * 512],
                      memh[:, half * 512:(half + 1) * 512],
                      gtb[:, sl * T + half * 512: sl * T + (half + 1) * 512],
                      ALU.mult, sem="dve", waits=[("dve", m1), lg])
            m3 = P.tt(xT[:, h * T + half * 512: h * T + (half + 1) * 512],
                      xT[:, h * T + half * 512: h * T + (half + 1) * 512],
                      memh[:, half * 512:(half + 1) * 512], ALU.add, sem="dve",
                      waits=[("dve", m2), lx])
            hstore.append(m3)
        bf_[("mq", sl)] = ("pe", P.cnt["pe"])
        bf_[("gt", sl)] = ("dve", P.cnt["dve"])
    xdone = P.cnt["dve"]

    for tt in range(8):
        b, sr = tt // 4, (tt % 4) * 128
        for dk in range(DK):
            bk = 5 + (dk % 2)
            wl = [("dve", xdone), lid]
            if bk in bf_:
                wl.append(bf_.pop(bk))
            tc = P.transpose(pb[bk][:, 0:128], xT[:, dk * T + tt * 128: dk * T + (tt + 1) * 128],
                             ident[:, :], sem="pe", waits=wl)
            cc = P.copy(onat[:, dk * 128:(dk + 1) * 128], pb[bk][:, 0:128],
                        sem="dve", waits=[("pe", tc),
                                          bf_.pop(("on", dk % 2)) if ("on", dk % 2) in bf_ else ("dve", 0)])
            bf_[bk] = ("dve", cc)
        oc = P.cnt["dve"]
        st = P.dma("sync", aps["y_out"][b, sr:sr + 128, :], onat[:, :], "st",
                   waits=[("dve", oc)])
        bf_[("on", 0)] = st
        bf_[("on", 1)] = st
    P.wait_stream("sync", "st")
    P.wait_stream("gpsimd", "w")


from concourse.bass_utils import run_bass_kernel_spmd

NC = 8
SC = 512


def _consts():
    c = {}
    inv = 1.0 / 10000.0 ** (np.arange(0, 64, 2, np.float32) / 64)
    cs, sn = np.cos(inv), np.sin(inv)
    coscol = np.zeros(128, np.float32)
    sgnsin = np.zeros(128, np.float32)
    for p in range(128):
        i = p % 64
        j = i if i < 32 else i - 32
        coscol[p] = cs[j]
        sgnsin[p] = -sn[j] if i < 32 else sn[j]
    c["rotcs"] = np.ascontiguousarray(np.stack([coscol, sgnsin], 1))
    prot = np.zeros((128, 128), np.float32)
    for i in range(128):
        ii = i % 64
        prot[i, (i + 32) if ii < 32 else (i - 32)] = 1.0
    c["protc"] = prot.astype(ml_dtypes.bfloat16)
    bd = np.zeros((128, 8 * 16), np.float32)
    for tt in range(8):
        bd[0:64, tt * 16 + 2 * tt] = 1.0
        bd[64:128, tt * 16 + 2 * tt + 1] = 1.0
    c["bd64c"] = bd.astype(ml_dtypes.bfloat16)
    rollc = np.zeros((128, 6 * 128), np.float32)
    for j in (1, 2, 3):
        S = np.zeros((128, 128), np.float32)
        Tm = np.zeros((128, 128), np.float32)
        for i in range(128):
            if i >= j:
                S[i, i - j] = 1.0
            else:
                Tm[i, 128 - j + i] = 1.0
        rollc[:, (2 * (j - 1)) * 128:(2 * (j - 1) + 1) * 128] = S.T
        rollc[:, (2 * (j - 1) + 1) * 128:(2 * (j - 1) + 2) * 128] = Tm.T
    c["rollc"] = rollc.astype(ml_dtypes.bfloat16)
    return c


def _build_A():
    P = Prog()
    nc = P.nc
    aps = {}
    def di(name, shape, dt=F32):
        aps[name] = nc.dram_tensor(name, shape, dt, kind="ExternalInput").ap()
    def do(name, shape, dt=F32):
        aps[name] = nc.dram_tensor(name, shape, dt, kind="ExternalOutput").ap()
    di("xs", [2, SC, D]); di("wf", [D, D]); di("wi", [D, D]); di("wv", [D, D])
    di("wvt", [D, D]); di("rwA_in", [128, 32]); di("waux", [1, D])
    do("xt_out", [DK, 128, T]); do("xn_out", [DK, 128, T], BF16)
    do("f_out", [DK, 128, T]); do("g_out", [DK, 128, T], BF16)
    do("ap_out", [128, 64]); do("wvr_out", [128, 1024], BF16)
    emit_launchA(P, xs_ap=aps["xs"], wf_ap=aps["wf"], wi_ap=aps["wi"],
                 wv_ap=aps["wv"], xt_out=aps["xt_out"], xn_out=aps["xn_out"],
                 f_out=aps["f_out"], g_out=aps["g_out"], ap_out=aps["ap_out"],
                 wvt_ap=aps["wvt"], rw_ap=aps["rwA_in"], waux_ap=aps["waux"],
                 wvr_out=aps["wvr_out"])
    return P.finalize()


def _build_B():
    P = Prog()
    nc = P.nc
    aps = {}
    def di(name, shape, dt=F32):
        aps[name] = nc.dram_tensor(name, shape, dt, kind="ExternalInput").ap()
    def do(name, shape, dt=F32):
        aps[name] = nc.dram_tensor(name, shape, dt, kind="ExternalOutput").ap()
    di("xt_in", [DK, 128, T]); di("xn_in", [DK, 128, T], BF16)
    di("f_in", [DK, 128, T]); di("g_in", [DK, 128, T], BF16)
    di("apall", [128, 8, 64]); di("initsb_in", [128, 16]); di("selvec", [1, 8])
    for w in ("wq", "wro", "wup", "wgate", "wdown"):
        di(w, [D, D])
    di("wgmoe", [D, 16])
    di("w1t", [16, D, 1408]); di("w3t", [16, D, 1408]); di("w2t", [16, 1408, D])
    di("ws1t", [D, 2816]); di("ws3t", [D, 2816]); di("ws2t", [2816, D])
    di("convw", [128, 32]); di("rotcs", [128, 2])
    di("wqf", [D, 1024]); di("wkf", [D, 1024]); di("wvf", [D, D]); di("wgf", [D, D])
    di("wvr_in", [128, 1024], BF16); di("woutf", [D, D])
    di("protc", [128, 128], BF16); di("bd64c", [128, 8 * 16], BF16)
    di("wkT", [D, 512]); di("wvT", [D, 512]); di("wmbT", [D, D])
    di("wqT2", [D, D]); di("wgT2", [D, D])
    di("bk_in", [128, 4]); di("bv_in", [128, 4]); di("bmb_in", [128, 16])
    di("bq_in", [128, 16]); di("bg_in", [128, 16]); di("bkrow", [1, 512])
    di("memC", [16, 768, 128]); di("normT_in", [768, 16])
    di("rollc", [128, 6 * 128], BF16)
    do("dmem_out", [16, 768, 128]); do("dnorm_out", [768, 16])
    do("mq_out", [16, 6, 128, T], BF16); do("sqmq_out", [1, 16 * T])
    do("gate_out", [DK, 128, T], BF16); do("xpart_out", [DK, 128, T])

    env = emit_launchB0(P, aps)
    env["x1done"] = ("dve", P.cnt["dve"])
    env["h_guard2"] = ("pe", P.cnt["pe"])
    emit_LN1(P, env)
    emit_l1ffn(P, env, aps)
    emit_l1attn(P, env, aps)
    P.wait("sync", "dve", P.cnt["dve"])
    P.wait("sync", "pe", P.cnt["pe"])
    P.wait("sync", "act", P.cnt["act"])
    P.wait_stream("sync", "st")
    P.wait_stream("sync", "w")
    P.wait_stream("sync", "x")
    P.wait_stream("sync", "hx")
    return P.finalize()


def _build_C():
    P = Prog()
    nc = P.nc
    aps = {}
    def di(name, shape, dt=F32):
        aps[name] = nc.dram_tensor(name, shape, dt, kind="ExternalInput").ap()
    di("xpart_in", [DK, 128, T]); di("mq_in", [16, 6, 128, T], BF16)
    di("sqmq_in", [1, 16 * T]); di("gate_in", [DK, 128, T], BF16)
    di("dparts", [8, 16, 768, 128]); di("dnparts", [8, 768, 16])
    di("memC", [16, 768, 128]); di("normT_in", [768, 16])
    aps["y_out"] = nc.dram_tensor("y_out", [2, SC, D], F32,
                                  kind="ExternalOutput").ap()
    emit_launchC(P, aps)
    return P.finalize()


def kernel(**inp):
    co = _consts()
    ac = np.ascontiguousarray
    x = inp["L0_attn_x"]

    ncA = _build_A()
    wvt_h = ac(inp["L1_ffn_Wv"].T)
    rw_h = ac(inp["L1_ffn_router_w"].transpose(1, 0, 2).reshape(128, 32))
    waux_h = ac(inp["L0_attn_w_aux"][:1])
    mapsA = []
    for c in range(NC):
        mapsA.append({
            "xs": ac(x[:, SC * c:SC * (c + 1), :]),
            "wf": inp["L0_attn_w_f"], "wi": inp["L0_attn_w_i"],
            "wv": inp["L0_attn_w_v"],
            "wvt": wvt_h, "rwA_in": rw_h, "waux": waux_h,
        })
    rA = run_bass_kernel_spmd(ncA, mapsA, core_ids=list(range(NC)))
    oA = rA.results

    ncB = _build_B()
    apall = ac(np.stack([oA[c]["ap_out"] for c in range(NC)], axis=1))
    initsb = ac(inp["L0_attn_init_state"].reshape(16, 128).T)
    shared = {
        "apall": apall, "initsb_in": initsb,
        "wq": inp["L0_attn_w_q"], "wro": inp["L0_attn_w_ro"],
        "wup": inp["L0_attn_w_up"], "wgate": inp["L0_attn_w_gate"],
        "wdown": inp["L0_attn_w_down"],
        "wgmoe": ac(inp["L0_ffn_Wg"].T),
        "w1t": ac(inp["L0_ffn_W1"].transpose(0, 2, 1)),
        "w3t": ac(inp["L0_ffn_W3"].transpose(0, 2, 1)),
        "w2t": ac(inp["L0_ffn_W2"].transpose(0, 2, 1)),
        "ws1t": ac(inp["L0_ffn_Ws1"].T), "ws3t": ac(inp["L0_ffn_Ws3"].T),
        "ws2t": ac(inp["L0_ffn_Ws2"].T),
        "convw": ac(inp["L1_ffn_conv_w"].reshape(DK, 128, 2)
                    .transpose(1, 0, 2).reshape(128, 32)),
        "rotcs": co["rotcs"], "protc": co["protc"], "bd64c": co["bd64c"],
        "wqf": inp["L1_ffn_Wq"], "wkf": inp["L1_ffn_Wk"],
        "wvf": inp["L1_ffn_Wv"], "wgf": inp["L1_ffn_Wg"],
        "woutf": inp["L1_ffn_Wout"],
        "wkT": ac(inp["L1_attn_Wk"].T), "wvT": ac(inp["L1_attn_Wv"].T),
        "wmbT": ac(inp["L1_attn_Wmb"].T), "wqT2": ac(inp["L1_attn_Wq"].T),
        "wgT2": ac(inp["L1_attn_Wg"].T),
        "bk_in": ac(inp["L1_attn_bk"].reshape(4, 128).T),
        "bv_in": ac(inp["L1_attn_bv"].reshape(4, 128).T),
        "bmb_in": ac(inp["L1_attn_bmb"].reshape(16, 128).T),
        "bq_in": ac(inp["L1_attn_bq"].reshape(16, 128).T),
        "bg_in": ac(inp["L1_attn_bg"].reshape(16, 128).T),
        "bkrow": ac(inp["L1_attn_bk"][None, :]),
        "memC": inp["L1_attn_memory"],
        "normT_in": ac(inp["L1_attn_norm"].T),
        "rollc": co["rollc"],
    }
    mapsB = []
    for c in range(NC):
        sv = np.zeros((1, 8), np.float32)
        sv[0, c] = 1.0
        m = dict(shared)
        m.update({
            "xt_in": oA[c]["xt_out"], "xn_in": oA[c]["xn_out"],
            "f_in": oA[c]["f_out"], "g_in": oA[c]["g_out"],
            "wvr_in": oA[c]["wvr_out"], "selvec": sv,
        })
        mapsB.append(m)
    rB = run_bass_kernel_spmd(ncB, mapsB, core_ids=list(range(NC)))
    oB = rB.results

    ncC = _build_C()
    dparts = ac(np.stack([oB[c]["dmem_out"] for c in range(NC)], axis=0))
    dnparts = ac(np.stack([oB[c]["dnorm_out"] for c in range(NC)], axis=0))
    sharedC = {"dparts": dparts, "dnparts": dnparts,
               "memC": inp["L1_attn_memory"],
               "normT_in": ac(inp["L1_attn_norm"].T)}
    mapsC = []
    for c in range(NC):
        m = dict(sharedC)
        m.update({"xpart_in": oB[c]["xpart_out"], "mq_in": oB[c]["mq_out"],
                  "sqmq_in": oB[c]["sqmq_out"], "gate_in": oB[c]["gate_out"]})
        mapsC.append(m)
    rC = run_bass_kernel_spmd(ncC, mapsC, core_ids=list(range(NC)))
    oC = rC.results

    y = np.empty((2, 4096, D), np.float32)
    for c in range(NC):
        y[:, SC * c:SC * (c + 1), :] = oC[c]["y_out"]
    return y


# version 03
# speedup vs baseline: 1.0523x; 1.0523x over previous
import numpy as np
import ml_dtypes
import concourse.bass as bass
import concourse.mybir as mybir

F32 = mybir.dt.float32
BF16 = mybir.dt.bfloat16
I32 = mybir.dt.int32
AF = mybir.ActivationFunctionType
ALU = mybir.AluOpType
AXL = mybir.AxisListType

ENGINES = ("sync", "scalar", "vector", "tensor", "gpsimd")


class Prog:
    def __init__(self):
        self.nc = bass.Bass("TRN2", target_bir_lowering=False)
        self.ops = {e: [] for e in ENGINES}
        self.sems = {}
        self.cnt = {}
        self._const_cols = {}
        self._finalized = False

    def sem(self, name):
        if name not in self.sems:
            self.sems[name] = self.nc.alloc_semaphore(name)
            self.cnt[name] = 0
        return self.sems[name]

    def bump(self, name, k):
        self.sem(name)
        self.cnt[name] += k
        return self.cnt[name]

    def _inc_of(self, sem, k):
        if sem is None:
            return None, None
        tgt = self.bump(sem, k)
        return self.sems[sem], tgt

    def raw(self, eng, fn):
        self.ops[eng].append(fn)

    def wait(self, eng, sem, val):
        if val is None or val <= 0:
            return
        s = self.sem(eng) if False else self.sem(sem)
        self.ops[eng].append(lambda q, _s=s, _v=val: q.wait_ge(_s, _v))

    def waits(self, eng, wl):
        for (sem, val) in wl:
            self.wait(eng, sem, val)

    NSLOT = 6

    def dma(self, eng, out, in_, stream, waits=()):
        key = ("dmaslot", stream)
        idx = self._dma_idx = getattr(self, "_dma_idx", {})
        i = idx.get(key, 0)
        idx[key] = i + 1
        sname = f"D{stream}{i % self.NSLOT}"
        self.sem(sname)
        prev = self.cnt[sname]
        wl = list(waits)
        if prev:
            wl.append((sname, prev))
        sh, tgt = self._inc_of(sname, 16)
        self.waits(eng, wl)
        self.ops[eng].append(
            lambda q, _o=out, _i=in_, _s=sh: q.dma_start(out=_o, in_=_i).then_inc(_s, 16))
        return (sname, tgt)

    def stream_tokens(self, stream):
        toks = []
        for s in range(self.NSLOT):
            n = f"D{stream}{s}"
            if self.cnt.get(n, 0):
                toks.append((n, self.cnt[n]))
        return toks

    def wait_stream(self, eng, stream):
        self.waits(eng, self.stream_tokens(stream))

    def mm(self, psum, lhsT, rhs, start, stop, sem=None, waits=()):
        self.waits("tensor", waits)
        sh, tgt = self._inc_of(sem, 1)
        if sh is None:
            self.ops["tensor"].append(
                lambda q, _p=psum, _l=lhsT, _r=rhs, _a=start, _b=stop:
                q.matmul(_p, _l, _r, start=_a, stop=_b))
        else:
            self.ops["tensor"].append(
                lambda q, _p=psum, _l=lhsT, _r=rhs, _a=start, _b=stop, _s=sh:
                q.matmul(_p, _l, _r, start=_a, stop=_b).then_inc(_s, 1))
        return tgt

    def transpose(self, psum, in_, ident, sem=None, waits=()):
        self.waits("tensor", waits)
        sh, tgt = self._inc_of(sem, 1)
        if sh is None:
            self.ops["tensor"].append(
                lambda q, _p=psum, _i=in_, _d=ident: q.transpose(_p, _i, _d))
        else:
            self.ops["tensor"].append(
                lambda q, _p=psum, _i=in_, _d=ident, _s=sh:
                q.transpose(_p, _i, _d).then_inc(_s, 1))
        return tgt

    def _eop(self, eng, build, sem=None, waits=(), inc=1):
        self.waits(eng, waits)
        sh, tgt = self._inc_of(sem, inc)
        if sh is None:
            self.ops[eng].append(build)
        else:
            self.ops[eng].append(lambda q, _b=build, _s=sh, _i=inc: _b(q).then_inc(_s, _i))
        return tgt

    def tt(self, out, a, b, alu, sem=None, waits=(), eng="vector"):
        return self._eop(eng, lambda q, _o=out, _a=a, _b=b, _u=alu:
                         q.tensor_tensor(_o, _a, _b, _u), sem, waits)

    def tsc(self, out, a, s1, s2, op0, op1=None, sem=None, waits=(), eng="vector"):
        if op1 is None:
            return self._eop(eng, lambda q, _o=out, _a=a, _s=s1, _u=op0:
                             q.tensor_scalar(_o, _a, _s, None, _u), sem, waits)
        return self._eop(eng, lambda q, _o=out, _a=a, _s=s1, _t=s2, _u=op0, _v=op1:
                         q.tensor_scalar(_o, _a, _s, _t, _u, _v), sem, waits)

    def stt(self, out, in0, scalar, in1, op0, op1, sem=None, waits=(), eng="vector"):
        return self._eop(eng, lambda q, _o=out, _a=in0, _s=scalar, _b=in1, _u=op0, _v=op1:
                         q.scalar_tensor_tensor(_o, _a, _s, _b, _u, _v), sem, waits)

    def copy(self, out, a, sem=None, waits=(), eng="vector"):
        return self._eop(eng, lambda q, _o=out, _a=a: q.tensor_copy(_o, _a), sem, waits)

    def memset(self, out, v, sem=None, waits=(), eng="vector"):
        return self._eop(eng, lambda q, _o=out, _v=v: q.memset(_o, _v), sem, waits)

    def reduce(self, out, a, alu, negate=False, sem=None, waits=(), eng="vector"):
        return self._eop(eng, lambda q, _o=out, _a=a, _u=alu, _n=negate:
                         q.tensor_reduce(_o, _a, AXL.X, _u, negate=_n), sem, waits)

    def recip(self, out, a, sem=None, waits=()):
        return self._eop("vector", lambda q, _o=out, _a=a: q.reciprocal(_o, _a), sem, waits)

    def select(self, out, mask, t_, f_, sem=None, waits=()):
        return self._eop("vector", lambda q, _o=out, _m=mask, _t=t_, _f=f_:
                         q.select(_o, _m, _t, _f), sem, waits)

    def scan(self, out, d0, d1, init, op0, op1, sem=None, waits=()):
        return self._eop("vector", lambda q, _o=out, _a=d0, _b=d1, _i=init, _u=op0, _v=op1:
                         q.tensor_tensor_scan(_o, _a, _b, _i, _u, _v), sem, waits)

    def act(self, out, a, fn, bias=0.0, scale=1.0, sem=None, waits=(), accum_out=None):
        if isinstance(bias, float):
            self.fconst(bias)
        if isinstance(scale, float):
            self.fconst(scale)
        if accum_out is None:
            return self._eop("scalar", lambda q, _o=out, _a=a, _f=fn, _b=bias, _s=scale:
                             q.activation(_o, _a, _f, bias=_b, scale=_s), sem, waits)
        return self._eop("scalar", lambda q, _o=out, _a=a, _f=fn, _b=bias, _s=scale, _x=accum_out:
                         q.activation(_o, _a, _f, bias=_b, scale=_s, accum_out=_x), sem, waits)

    def fconst(self, value):
        if value not in self._const_cols:
            assert not self._finalized
            self._const_cols[value] = len(self._const_cols)
        return value

    def init_consts(self, *values):
        for v in values:
            self.fconst(v)

    def finalize(self):
        assert not self._finalized
        self._finalized = True
        nc = self.nc
        pre = {e: [] for e in ENGINES}
        if self._const_cols:
            vals = sorted(self._const_cols, key=self._const_cols.get)
            arr = np.tile(np.array([vals], dtype=np.float32), (128, 1))
            th = nc.inline_tensor(arr.astype(np.float32), name="constvals")
            tile = nc.alloc_sbuf_tensor("constsb", [128, len(vals)], F32)
            for v in vals:
                c = self._const_cols[v]
                nc.const_aps.register(F32, float(v), tile[:, c:c + 1])
            cs = self.sem("cinit")
            pre["sync"].append(lambda q, _t=tile, _a=th.ap(), _s=cs:
                               q.dma_start(out=_t[:, :], in_=_a).then_inc(_s, 16))
            for e in ("scalar",):
                pre[e].append(lambda q, _s=cs: q.wait_ge(_s, 16))
        with nc.Block() as block:
            for eng in ENGINES:
                lst = pre[eng] + self.ops[eng]
                if lst:
                    getattr(block, eng)(lambda q, _l=tuple(lst): [f(q) for f in _l])
        return nc


def tsl(i, n=128):
    return slice(i * n, (i + 1) * n)


"""Section emitters for the megakernel (dev copies; final versions get pasted
into kernel.py)."""

T = 1024
D = 2048
DK = D // 128


def glu_env(P, *, icmax=2, w2slot=6144, hmax=16, acc=None, xnb=None):
    nc = P.nc
    env = {}
    env["xnb"] = xnb if xnb is not None else nc.alloc_sbuf_tensor("xnb", [128, DK * T], BF16)
    env["ICMAX"] = icmax
    env["W1SLOT"] = 16 * icmax * 128
    env["W2SLOT"] = w2slot
    env["w1b"] = nc.alloc_sbuf_tensor("w1b", [128, 2 * env["W1SLOT"]], BF16)
    env["w3b"] = nc.alloc_sbuf_tensor("w3b", [128, 2 * env["W1SLOT"]], BF16)
    env["w2b"] = nc.alloc_sbuf_tensor("w2b", [128, 2 * env["W2SLOT"]], BF16)
    env["h"] = nc.alloc_sbuf_tensor("hbuf", [128, hmax * T], BF16)
    env["sil"] = nc.alloc_sbuf_tensor("sil", [128, 2 * 512], F32)
    env["tmp"] = nc.alloc_sbuf_tensor("glutmp", [128, 2 * 512], F32)
    env["acc"] = acc if acc is not None else nc.alloc_sbuf_tensor("accb", [128, DK * T], F32)
    env["psum"] = [nc.alloc_psum_tensor(f"pb{i}", [128, 512], F32) for i in range(6)]
    env["slotuse"] = {}
    env["bankfree"] = {}
    env["silfree"] = {}
    return env


def emit_glu_ffn(P, env, *, w1_ap, w3_ap, w2_ap, n_i, c_col=None, tag,
                 i_chunk=2, d_chunk=4, first_into_acc=False):
    xnb = env["xnb"]; w1b = env["w1b"]; w3b = env["w3b"]; w2b = env["w2b"]
    hbuf = env["h"]; sil = env["sil"]; tmp = env["tmp"]; acc = env["acc"]
    pb = env["psum"]; slotuse = env["slotuse"]; bankfree = env["bankfree"]
    silfree = env["silfree"]
    W1SLOT = env["W1SLOT"]; W2SLOT = env["W2SLOT"]
    assert i_chunk <= env["ICMAX"] and n_i * T <= hbuf.shape[1]

    chunks = [list(range(c0, min(n_i, c0 + i_chunk))) for c0 in range(0, n_i, i_chunk)]
    dchunks = [list(range(c0, min(DK, c0 + d_chunk))) for c0 in range(0, DK, d_chunk)]
    assert n_i * d_chunk * 128 <= W2SLOT

    gidx = 0
    for ci, chunk in enumerate(chunks):
        cs = len(chunk)
        slot = ci % 2
        dcnt = {}
        for name, wb, src in (("w1", w1b, w1_ap), ("w3", w3b, w3_ap)):
            key = ("A" + name, slot)
            wl = []
            if key in slotuse:
                wl.append(("pe", slotuse[key]))
            dcnt[name] = P.dma(
                "gpsimd",
                wb[:, slot * W1SLOT: slot * W1SLOT + 16 * cs * 128].rearrange(
                    "p (k c) -> p k c", k=16),
                src[:, chunk[0] * 128:(chunk[0] + cs) * 128].rearrange(
                    "(k p) c -> p k c", p=128),
                "w", waits=wl)
        for li, gi in enumerate(chunk):
            for half in range(2):
                par = gidx % 2
                bA, bB = pb[2 * par], pb[2 * par + 1]
                iA, iB = 2 * par, 2 * par + 1
                gidx += 1
                wl = [dcnt["w1"], dcnt["w3"]]
                for ib in (iA, iB):
                    if ib in bankfree:
                        wl.append(bankfree.pop(ib))
                first = True
                for k in range(DK):
                    off = slot * W1SLOT + k * cs * 128 + li * 128
                    rhs = xnb[:, k * T + half * 512: k * T + (half + 1) * 512]
                    P.mm(bA[:, :], w1b[:, off:off + 128], rhs,
                         start=(k == 0), stop=(k == DK - 1),
                         waits=wl if first else ())
                    first = False
                    P.mm(bB[:, :], w3b[:, off:off + 128], rhs,
                         start=(k == 0), stop=(k == DK - 1),
                         sem="pe" if k == DK - 1 else None)
                pecnt = P.cnt["pe"]
                slotuse[("Aw1", slot)] = pecnt
                slotuse[("Aw3", slot)] = pecnt
                ssl = gidx % 2
                swl = [("pe", pecnt)]
                if ssl in silfree:
                    swl.append(silfree.pop(ssl))
                acnt = P.act(sil[:, ssl * 512:(ssl + 1) * 512], bA[:, :], AF.Silu,
                             sem="act", waits=swl)
                hdst = hbuf[:, gi * T + half * 512: gi * T + (half + 1) * 512]
                hg = [env["h_guard"]] if "h_guard" in env else []
                if c_col is None:
                    dcnt2 = P.tt(hdst, sil[:, ssl * 512:(ssl + 1) * 512], bB[:, :],
                                 ALU.mult, sem="dve", waits=[("act", acnt)] + hg)
                else:
                    t1 = P.tt(tmp[:, ssl * 512:(ssl + 1) * 512], bB[:, :],
                              c_col[:, half * 512:(half + 1) * 512], ALU.mult,
                              sem="dve", waits=[("pe", pecnt)])
                    dcnt2 = P.tt(hdst, sil[:, ssl * 512:(ssl + 1) * 512],
                                 tmp[:, ssl * 512:(ssl + 1) * 512],
                                 ALU.mult, sem="dve",
                                 waits=[("act", acnt), ("dve", t1)] + hg)
                bankfree[iA] = ("act", acnt)
                bankfree[iB] = ("dve", dcnt2)
                silfree[ssl] = ("dve", dcnt2)
    env.pop("h_guard", None)
    h_ready = P.cnt["dve"]

    emit_phaseB(P, env, w2_ap=w2_ap, n_i=n_i, rhs_buf=hbuf,
                d_chunk=d_chunk, first_into_acc=first_into_acc,
                hready=("dve", h_ready))


def emit_moe_router(P, env, *, wg_ap, call_tile, ident_sb, tag="rt"):
    nc = P.nc
    xnb = env["xnb"]
    pb = env["psum"]
    wgt = nc.alloc_sbuf_tensor(f"wgt{tag}", [128, DK * 16], BF16)
    sc = nc.alloc_sbuf_tensor(f"sc{tag}", [128, 16], F32)
    msk = nc.alloc_sbuf_tensor(f"msk{tag}", [128, 16], F32)
    comb = nc.alloc_sbuf_tensor(f"comb{tag}", [128, 16], F32)
    eqt = nc.alloc_sbuf_tensor(f"eq{tag}", [128, 16], F32)
    gs = nc.alloc_sbuf_tensor(f"gs{tag}", [128, 4], F32)
    gmb = nc.alloc_sbuf_tensor(f"gmb{tag}", [128, 4], F32)
    red = nc.alloc_sbuf_tensor(f"red{tag}", [128, 6], F32)

    wcnt = P.dma("gpsimd", wgt[:, :].rearrange("p (k c) -> p k c", k=DK),
                 wg_ap.rearrange("(k p) c -> p k c", p=128), "w")
    for tt in range(8):
        wl = [wcnt]
        if "rtbank0" in env:
            wl.append(env.pop("rtbank0"))
        for k in range(DK):
            P.mm(pb[0][:, 0:16], xnb[:, k * T + tt * 128: k * T + (tt + 1) * 128],
                 wgt[:, k * 16:(k + 1) * 16],
                 start=(k == 0), stop=(k == DK - 1),
                 waits=wl if k == 0 else (), sem="pe" if k == DK - 1 else None)
        pc = P.cnt["pe"]
        a1 = P.act(sc[:, :], pb[0][:, 0:16], AF.Exp, sem="act", waits=[("pe", pc)])
        env["rtbank0"] = ("act", a1)
        d1 = P.reduce(red[:, 0:1], sc[:, :], ALU.add, sem="dve", waits=[("act", a1)])
        d2 = P.recip(red[:, 1:2], red[:, 0:1], sem="dve", waits=[("dve", d1)])
        for g in range(4):
            P.reduce(gs[:, g:g + 1], sc[:, 4 * g:4 * (g + 1)], ALU.max,
                     sem="dve", waits=[("act", a1)])
        d3 = P.cnt["dve"]
        d4 = P.reduce(red[:, 2:3], gs[:, :], ALU.max, sem="dve", waits=[("dve", d3)])
        e1 = P.tsc(gmb[:, :], gs[:, :], red[:, 2:3], None, ALU.is_equal,
                   sem="dve", waits=[("dve", d4)])
        f1 = P.stt(gs[:, :], gmb[:, :], -1e30, gs[:, :], ALU.mult, ALU.add,
                   sem="dve", waits=[("dve", e1)])
        f2 = P.reduce(red[:, 3:4], gs[:, :], ALU.max, sem="dve", waits=[("dve", f1)])
        f3 = P.tsc(gs[:, :], gs[:, :], red[:, 3:4], None, ALU.is_equal,
                   sem="dve", waits=[("dve", f2)])
        f4 = P.tt(gmb[:, :], gmb[:, :], gs[:, :], ALU.add, sem="dve", waits=[("dve", f3)])
        f5 = P.tsc(gmb[:, :], gmb[:, :], 1e30, -1e30, ALU.mult, ALU.add,
                   sem="dve", waits=[("dve", f4)])
        for g in range(4):
            P.tsc(msk[:, 4 * g:4 * (g + 1)], sc[:, 4 * g:4 * (g + 1)],
                  gmb[:, g:g + 1], None, ALU.add, sem="dve", waits=[("dve", f5)])
        for r in range(4):
            r1 = P.reduce(red[:, 4:5], msk[:, :], ALU.max, sem="dve",
                          waits=[("dve", P.cnt["dve"])])
            r3 = P.tsc(eqt[:, :], msk[:, :], red[:, 4:5], None, ALU.is_equal,
                       sem="dve", waits=[("dve", r1)])
            if r == 0:
                r4 = P.tsc(comb[:, :], eqt[:, :], red[:, 4:5], None, ALU.mult,
                           sem="dve", waits=[("dve", r3)])
            else:
                r4 = P.stt(comb[:, :], eqt[:, :], red[:, 4:5], comb[:, :],
                           ALU.mult, ALU.add, sem="dve", waits=[("dve", r3)])
            P.stt(msk[:, :], eqt[:, :], -1e30, msk[:, :],
                  ALU.mult, ALU.add, sem="dve", waits=[("dve", r4)])
        s1 = P.tsc(comb[:, :], comb[:, :], red[:, 1:2], None, ALU.mult,
                   sem="dve", waits=[("dve", P.cnt["dve"])])
        wl = [("dve", s1)]
        if "rtbank1" in env:
            wl.append(env.pop("rtbank1"))
        tcnt = P.transpose(pb[1][0:16, 0:128], comb[:, 0:16], ident_sb[:, :],
                           sem="pe", waits=wl)
        c1 = P.copy(call_tile[0:16, tt * 128:(tt + 1) * 128], pb[1][0:16, 0:128],
                    sem="dve", waits=[("pe", tcnt)])
        env["rtbank1"] = ("dve", c1)
    return P.cnt["dve"]


def emit_launchA(P, *, xs_ap, wf_ap, wi_ap, wv_ap, xt_out, xn_out, f_out, g_out,
                 ap_out, wvt_ap=None, rw_ap=None, waux_ap=None, wvr_out=None):
    nc = P.nc
    xnat = nc.alloc_sbuf_tensor("xnat", [128, 4 * D], F32)
    xntok = nc.alloc_sbuf_tensor("xntok", [128, 4 * D], BF16)
    sqt = nc.alloc_sbuf_tensor("sqt", [128, D], F32)
    stat = nc.alloc_sbuf_tensor("stat", [128, 4], F32)
    xTa = nc.alloc_sbuf_tensor("xTa", [128, DK * 512], F32)
    xnb = nc.alloc_sbuf_tensor("xnbA", [128, DK * T], BF16)
    wfb = nc.alloc_sbuf_tensor("wfb", [128, 2 * 16 * 2 * 128], BF16)
    wib = nc.alloc_sbuf_tensor("wib", [128, 2 * 16 * 2 * 128], BF16)
    wvb = nc.alloc_sbuf_tensor("wvb", [128, 2 * 16 * 2 * 128], BF16)
    ftile = nc.alloc_sbuf_tensor("ftile", [128, 2 * T], F32)
    gtile = nc.alloc_sbuf_tensor("gtile", [128, 2 * T], BF16)
    sg1 = nc.alloc_sbuf_tensor("sg1", [128, 2 * 512], F32)
    sg2 = nc.alloc_sbuf_tensor("sg2", [128, 2 * 512], F32)
    t1b = nc.alloc_sbuf_tensor("t1b", [128, 2 * 512], F32)
    scr = nc.alloc_sbuf_tensor("scr", [128, 512], F32)
    zt = nc.alloc_sbuf_tensor("zt", [128, 512], F32)
    apb = nc.alloc_sbuf_tensor("apb", [128, 64], F32)
    identnp = nc.inline_tensor(np.eye(128, dtype=np.float32), name="ident128")
    ident = nc.alloc_sbuf_tensor("ident", [128, 128], F32)
    identbnp = nc.inline_tensor(np.eye(128, dtype=np.float32).astype(ml_dtypes.bfloat16), name="identb128")
    identb = nc.alloc_sbuf_tensor("identb", [128, 128], BF16)
    pb = [nc.alloc_psum_tensor(f"pa{i}", [128, 512], F32) for i in range(6)]
    pbb = [nc.alloc_psum_tensor(f"pab{i}", [128, 128], BF16) for i in range(2)]

    hcnt = P.dma("sync", ident[:, :], identnp.ap(), "hx")
    hcnt2 = P.dma("sync", identb[:, :], identbnp.ap(), "hx")
    P.memset(zt[:, :], 0.0, sem="dve")
    ztc = P.cnt["dve"]

    bankuse = {}
    xtstore = []
    for rnd in range(2):
        lcnt = []
        for j in range(4):
            tt = rnd * 4 + j
            b, sr = tt // 4, (tt % 4) * 128
            lw = []
            if rnd == 1:
                lw = xtstore
            lcnt.append(P.dma("sync", xnat[:, j * D:(j + 1) * D],
                              xs_ap[b, sr:sr + 128, :], "hx", waits=lw))
        if rnd == 1:
            xtstore = []
        for j in range(4):
            tt = rnd * 4 + j
            wl = [lcnt[j]]
            s0 = P.tt(sqt[:, :], xnat[:, j * D:(j + 1) * D],
                      xnat[:, j * D:(j + 1) * D], ALU.mult, sem="dve", waits=wl)
            s1 = P.reduce(stat[:, 0:1], xnat[:, j * D:(j + 1) * D], ALU.add,
                          sem="dve", waits=[("dve", s0)])
            s2 = P.reduce(stat[:, 2:3], sqt[:, :], ALU.add, sem="dve",
                          waits=[("dve", s0)])
            m1 = P.tsc(stat[:, 0:1], stat[:, 0:1], 1.0 / D, None, ALU.mult,
                       sem="dve", waits=[("dve", s1)])
            m2 = P.tsc(stat[:, 2:3], stat[:, 2:3], 1.0 / D, None, ALU.mult,
                       sem="dve", waits=[("dve", s2)])
            m3 = P.stt(stat[:, 3:4], stat[:, 0:1], stat[:, 0:1], stat[:, 2:3],
                       ALU.mult, ALU.subtract, sem="dve", waits=[("dve", m2)])
            a1 = P.act(stat[:, 1:2], stat[:, 3:4], AF.Ln, bias=1e-5, scale=-1.0,
                       sem="act", waits=[("dve", m3)])
            a2 = P.act(stat[:, 1:2], stat[:, 1:2], AF.Exp, scale=-0.5,
                       sem="act", waits=[("act", a1)])
            x1 = P.tsc(xntok[:, j * D:(j + 1) * D], xnat[:, j * D:(j + 1) * D],
                       stat[:, 0:1], stat[:, 1:2], ALU.subtract, ALU.mult,
                       sem="dve", waits=[("act", a2)])
            P.wait("tensor", "dve", x1)
            for dk in range(DK):
                bk = dk % 2
                wl = [hcnt]
                if bk in bankuse:
                    wl.append(bankuse.pop(bk))
                tc = P.transpose(pb[bk][:, 0:128],
                                 xnat[:, j * D + dk * 128: j * D + (dk + 1) * 128],
                                 ident[:, :], sem="pe", waits=wl)
                cc = P.copy(xTa[:, dk * 512 + j * 128: dk * 512 + (j + 1) * 128],
                            pb[bk][:, 0:128], sem="dve", waits=[("pe", tc)])
                bankuse[bk] = ("dve", cc)
                bk2 = 2 + dk % 2
                wl = [hcnt2]
                if bk2 in bankuse:
                    wl.append(bankuse.pop(bk2))
                tc2 = P.transpose(pbb[dk % 2][:, 0:128],
                                  xntok[:, j * D + dk * 128: j * D + (dk + 1) * 128],
                                  identb[:, :], sem="pe", waits=wl)
                cc2 = P.copy(xnb[:, dk * T + rnd * 512 + j * 128: dk * T + rnd * 512 + (j + 1) * 128],
                             pbb[dk % 2][:, 0:128], sem="dve", waits=[("pe", tc2)])
                bankuse[bk2] = ("dve", cc2)
        xc = P.cnt["dve"]
        xtstore.append(P.dma("sync",
                             xt_out[:, :, rnd * 512:(rnd + 1) * 512].rearrange(
                                 "k p c -> p k c"),
                             xTa[:, :].rearrange("p (k c) -> p k c", k=DK), "st",
                             waits=[("dve", xc)]))
    xnc = P.cnt["dve"]
    P.dma("sync", xn_out.rearrange("k p c -> p k c"),
          xnb[:, :].rearrange("p (k c) -> p k c", k=DK), "st", waits=[("dve", xnc)])

    WSLOT = 16 * 2 * 128
    slotuse = {}
    bankuse2 = {}
    sttok = {}
    for ci in range(8):
        chunk = [2 * ci, 2 * ci + 1]
        slot = ci % 2
        dc = {}
        for nm, wb, src in (("wf", wfb, wf_ap), ("wi", wib, wi_ap), ("wv", wvb, wv_ap)):
            wl = [("pe", slotuse[(nm, slot)])] if (nm, slot) in slotuse else []
            dc[nm] = P.dma("gpsimd",
                           wb[:, slot * WSLOT: slot * WSLOT + 16 * 2 * 128].rearrange(
                               "p (k c) -> p k c", k=16),
                           src[:, chunk[0] * 128:(chunk[0] + 2) * 128].rearrange(
                               "(k p) c -> p k c", p=128),
                           "w", waits=wl)
        for li, rt in enumerate(chunk):
            par = rt % 2
            bF, bI, bV = pb[3 * par], pb[3 * par + 1], pb[3 * par + 2]
            for half in range(2):
                wl = [dc["wf"], dc["wi"], dc["wv"], ("dve", xnc)]
                for ib in (3 * par, 3 * par + 1, 3 * par + 2):
                    if ib in bankuse2:
                        wl.append(bankuse2.pop(ib))
                first = True
                for k in range(DK):
                    off = slot * WSLOT + k * 2 * 128 + li * 128
                    rhs = xnb[:, k * T + half * 512: k * T + (half + 1) * 512]
                    P.mm(bF[:, :], wfb[:, off:off + 128], rhs, start=(k == 0),
                         stop=(k == DK - 1), waits=wl if first else ())
                    first = False
                    P.mm(bI[:, :], wib[:, off:off + 128], rhs, start=(k == 0),
                         stop=(k == DK - 1))
                    P.mm(bV[:, :], wvb[:, off:off + 128], rhs, start=(k == 0),
                         stop=(k == DK - 1), sem="pe" if k == DK - 1 else None)
                pc = P.cnt["pe"]
                for nm in ("wf", "wi", "wv"):
                    slotuse[(nm, slot)] = pc
                ssl = half
                fwl = [("pe", pc)]
                if ("ft", par, half) in sttok:
                    fwl.extend(sttok.pop(("ft", par, half)))
                af = P.act(ftile[:, par * T + half * 512: par * T + (half + 1) * 512],
                           bF[:, :], AF.Tanh, sem="act", waits=fwl)
                ai = P.act(sg1[:, ssl * 512:(ssl + 1) * 512], bI[:, :], AF.Sigmoid,
                           sem="act", waits=[("pe", pc)])
                av = P.act(sg2[:, ssl * 512:(ssl + 1) * 512], bV[:, :], AF.Sigmoid,
                           sem="act", waits=[("pe", pc)])
                v1 = P.tt(t1b[:, ssl * 512:(ssl + 1) * 512], bV[:, :],
                          sg2[:, ssl * 512:(ssl + 1) * 512], ALU.mult,
                          sem="dve", waits=[("act", av)])
                gwl = [("act", ai), ("dve", v1)]
                if ("gt", par, half) in sttok:
                    gwl.extend(sttok.pop(("gt", par, half)))
                v2 = P.tt(gtile[:, par * T + half * 512: par * T + (half + 1) * 512],
                          sg1[:, ssl * 512:(ssl + 1) * 512],
                          t1b[:, ssl * 512:(ssl + 1) * 512], ALU.mult,
                          sem="dve", waits=gwl)
                bankuse2[3 * par] = ("act", af)
                bankuse2[3 * par + 1] = ("act", ai)
                bankuse2[3 * par + 2] = ("dve", v2)
            fv = P.cnt["act"]
            gv = P.cnt["dve"]
            for b in range(2):
                s1 = P.scan(scr[:, :], ftile[:, par * T + b * 512: par * T + (b + 1) * 512],
                            gtile[:, par * T + b * 512: par * T + (b + 1) * 512],
                            0.0, ALU.mult, ALU.add, sem="dve",
                            waits=[("act", fv), ("dve", P.cnt["dve"])])
                P.copy(apb[:, rt * 2 + b: rt * 2 + b + 1], scr[:, 511:512],
                       sem="dve", waits=[("dve", s1)])
                s2 = P.scan(scr[:, :], ftile[:, par * T + b * 512: par * T + (b + 1) * 512],
                            zt[:, :], 1.0, ALU.mult, ALU.add, sem="dve",
                            waits=[("dve", P.cnt["dve"]), ("dve", ztc)])
                P.copy(apb[:, 32 + rt * 2 + b: 32 + rt * 2 + b + 1], scr[:, 511:512],
                       sem="dve", waits=[("dve", s2)])
            sc = P.cnt["dve"]
            stf = P.dma("sync", f_out[rt], ftile[:, par * T:(par + 1) * T], "st",
                        waits=[("act", fv)])
            stg = P.dma("sync", g_out[rt], gtile[:, par * T:(par + 1) * T], "st",
                        waits=[("dve", sc)])
            sttok[("ft", par, 0)] = [stf, ("dve", sc)]
            sttok[("ft", par, 1)] = [stf, ("dve", sc)]
            sttok[("gt", par, 0)] = [stg, ("dve", sc)]
            sttok[("gt", par, 1)] = [stg, ("dve", sc)]
    apc = P.cnt["dve"]
    P.dma("sync", ap_out, apb[:, :], "st", waits=[("dve", apc)])

    if wvt_ap is not None:
        slabw = nc.alloc_sbuf_tensor("slabw", [128, 2 * 1024], F32)
        rwA = nc.alloc_sbuf_tensor("rwA", [128, 32], F32)
        wvrA = nc.alloc_sbuf_tensor("wvrA", [128, 1024], BF16)
        lrwA = P.dma("sync", rwA[:, :], rw_ap, "hx")
        mzA = P.memset(wvrA[:, :], 0.0, sem="dve")
        slab_free = {}
        wvb_free = {}
        for h in range(16):
            for sl in range(2):
                b = (h * 2 + sl) % 2
                wl = []
                if b in slab_free:
                    wl.append(slab_free.pop(b))
                ls = P.dma("scalar", slabw[:, b * 1024:(b + 1) * 1024],
                           wvt_ap[128 * h:128 * (h + 1),
                                  sl * 1024:(sl + 1) * 1024],
                           "wvt", waits=wl)
                for j in range(8):
                    dcx = sl * 8 + j
                    pbk = 4 + (h * DK + dcx) % 2
                    mlw = [ls, lrwA]
                    if pbk in bankuse2:
                        mlw.append(bankuse2.pop(pbk))
                    if ("wv", pbk) in wvb_free:
                        mlw.append(wvb_free.pop(("wv", pbk)))
                    mc = P.mm(pb[pbk][:, 0:2],
                              slabw[:, b * 1024 + j * 128: b * 1024 + (j + 1) * 128],
                              rwA[:, 2 * h:2 * h + 2], start=True, stop=True,
                              sem="pe", waits=mlw)
                    cc = P.copy(wvrA[:, dcx * 64 + h: dcx * 64 + h + 33: 32],
                                pb[pbk][:, 0:2], sem="dve",
                                waits=[("pe", mc), ("dve", mzA)])
                    wvb_free[("wv", pbk)] = ("dve", cc)
                slab_free[b] = ("pe", P.cnt["pe"])
        P.dma("sync", wvr_out, wvrA[:, :], "st", waits=[("dve", P.cnt["dve"])])
    if waux_ap is not None:
        P.dma("sync", xnat[0:1, 0:D], waux_ap, "hx", waits=[("dve", apc)])
    P.wait_stream("sync", "st")
    P.wait_stream("sync", "hx")
    P.wait_stream("scalar", "wvt")


def emit_phaseB(P, env, *, w2_ap, n_i, rhs_buf, d_chunk, first_into_acc,
                hready):
    w2b = env["w2b"]; acc = env["acc"]; pb = env["psum"]
    slotuse = env["slotuse"]; bankfree = env["bankfree"]
    W2SLOT = env["W2SLOT"]
    dchunks = [list(range(c0, min(DK, c0 + d_chunk))) for c0 in range(0, DK, d_chunk)]
    assert n_i * d_chunk * 128 <= W2SLOT
    bidx = env.get("bidx", 0)
    for dci, dchunk in enumerate(dchunks):
        dcs = len(dchunk)
        slot = dci % 2
        key = ("Bw2", slot)
        wl = []
        if key in slotuse:
            wl.append(("pe", slotuse[key]))
        w2cnt = P.dma(
            "gpsimd",
            w2b[:, slot * W2SLOT: slot * W2SLOT + n_i * dcs * 128].rearrange(
                "p (k c) -> p k c", k=n_i),
            w2_ap[:, dchunk[0] * 128:(dchunk[0] + dcs) * 128].rearrange(
                "(k p) c -> p k c", p=128),
            "w", waits=wl)
        for mi, m2 in enumerate(dchunk):
            for half in range(2):
                par = bidx % 2
                bY = pb[4 + par]; iY = 4 + par
                bidx += 1
                wl = [w2cnt, hready]
                if iY in bankfree:
                    wl.append(bankfree.pop(iY))
                for k2 in range(n_i):
                    off = slot * W2SLOT + k2 * dcs * 128 + mi * 128
                    if isinstance(rhs_buf, list):
                        ent = rhs_buf[k2]
                        if isinstance(ent, list):
                            rhs = ent[half]
                        else:
                            rhs = ent[:, half * 512:(half + 1) * 512]
                    else:
                        rhs = rhs_buf[:, k2 * T + half * 512: k2 * T + (half + 1) * 512]
                    P.mm(bY[:, :], w2b[:, off:off + 128], rhs,
                         start=(k2 == 0), stop=(k2 == n_i - 1),
                         waits=wl if k2 == 0 else (),
                         sem="pe" if k2 == n_i - 1 else None)
                pecnt = P.cnt["pe"]
                slotuse[("Bw2", slot)] = pecnt
                adst = acc[:, m2 * T + half * 512: m2 * T + (half + 1) * 512]
                if first_into_acc:
                    dc = P.copy(adst, bY[:, :], sem="dve", waits=[("pe", pecnt)])
                else:
                    dc = P.tt(adst, adst, bY[:, :], ALU.add, sem="dve",
                              waits=[("pe", pecnt)])
                bankfree[iY] = ("dve", dc)
    env["bidx"] = bidx


def emit_projA(P, env, *, w_ap, n_m, evac_cb, tagkey, chunk=2, banks=(0, 1),
               rhs_ready=None, wbuf=None, wslot=None):
    xnb = env["xnb"]; pb = env["psum"]
    slotuse = env["slotuse"]; bankfree = env["bankfree"]
    wbuf = wbuf if wbuf is not None else env["w1b"]
    WSLOT = wslot if wslot is not None else env["W1SLOT"]
    chunks = [list(range(c0, min(n_m, c0 + chunk))) for c0 in range(0, n_m, chunk)]
    gidx = env.get(("gidx", tagkey), 0)
    for ci, ch in enumerate(chunks):
        cs = len(ch)
        slot = ci % 2
        key = (tagkey, slot)
        wl = []
        if key in slotuse:
            wl.append(("pe", slotuse[key]))
        dcnt = P.dma("gpsimd",
                     wbuf[:, slot * WSLOT: slot * WSLOT + 16 * cs * 128].rearrange(
                         "p (k c) -> p k c", k=16),
                     w_ap[:, ch[0] * 128:(ch[0] + cs) * 128].rearrange(
                         "(k p) c -> p k c", p=128),
                     "w", waits=wl)
        for li, mt in enumerate(ch):
            for half in range(2):
                par = gidx % 2
                bk = banks[par]
                gidx += 1
                wl = [dcnt]
                if rhs_ready is not None:
                    wl.append(rhs_ready)
                if bk in bankfree:
                    wl.append(bankfree.pop(bk))
                for k in range(DK):
                    off = slot * WSLOT + k * cs * 128 + li * 128
                    rhs = xnb[:, k * T + half * 512: k * T + (half + 1) * 512]
                    P.mm(pb[bk][:, :], wbuf[:, off:off + 128], rhs,
                         start=(k == 0), stop=(k == DK - 1),
                         waits=wl if k == 0 else (),
                         sem="pe" if k == DK - 1 else None)
                pecnt = P.cnt["pe"]
                slotuse[(tagkey, slot)] = pecnt
                tok = evac_cb(mt, half, pb[bk], pecnt)
                bankfree[bk] = tok


def emit_launchB0(P, aps):
    nc = P.nc
    env = glu_env(P, icmax=1, w2slot=2048, hmax=16)
    xTb = env["acc"]
    xnb = env["xnb"]
    hbuf = env["h"]

    carry = nc.alloc_sbuf_tensor("carry", [128, 32], F32)
    apsb = nc.alloc_sbuf_tensor("apsb", [128, 8 * 64], F32)
    initsb = nc.alloc_sbuf_tensor("initsb", [128, 16], F32)
    selb = nc.alloc_sbuf_tensor("selb", [128, 8], F32)
    selrow = nc.alloc_sbuf_tensor("selrow", [1, 8], BF16)
    scr8 = nc.alloc_sbuf_tensor("scr8", [128, 8], F32)
    cat8 = nc.alloc_sbuf_tensor("cat8", [128, 8], F32)
    ftl = nc.alloc_sbuf_tensor("ftl", [128, 2 * T], F32)
    gtl = nc.alloc_sbuf_tensor("gtl", [128, 2 * T], BF16)
    stt_ = nc.alloc_sbuf_tensor("sttl", [128, 2 * 512], F32)
    tq = env["sil"]
    sgq = env["tmp"]
    callb = nc.alloc_sbuf_tensor("callb", [16, T], BF16)
    selnp = np.zeros((16, 16 * 128), np.float32)
    for hh in range(16):
        selnp[hh, 128 * hh:128 * (hh + 1)] = 1.0
    sel16c = nc.inline_tensor(selnp.astype(ml_dtypes.bfloat16), name="sel16c")
    sel16 = nc.alloc_sbuf_tensor("sel16", [16, 16 * 128], BF16)
    identnp = nc.inline_tensor(np.eye(128, dtype=np.float32), name="identB128")
    ident = nc.alloc_sbuf_tensor("identB", [128, 128], F32)
    ones1 = nc.inline_tensor(np.ones((1, 128), np.float32).astype(ml_dtypes.bfloat16),
                             name="ones1B")
    ones_sb = nc.alloc_sbuf_tensor("ones_sbB", [1, 128], BF16)
    env["ident"] = ident
    env["ones_sb"] = ones_sb
    env["sel16"] = sel16
    env["callb"] = callb

    lxt = P.dma("sync", xTb[:, :].rearrange("p (k c) -> p k c", k=DK),
                aps["xt_in"].rearrange("k p c -> p k c"), "hx")
    lxn = P.dma("sync", xnb[:, :].rearrange("p (k c) -> p k c", k=DK),
                aps["xn_in"].rearrange("k p c -> p k c"), "hx")
    lap = P.dma("sync", apsb[:, :].rearrange("p (c k) -> p c k", c=8),
                aps["apall"], "hx")
    lin = P.dma("sync", initsb[:, :], aps["initsb_in"], "hx")
    lid = P.dma("sync", ident[:, :], identnp.ap(), "hx")
    lsel = P.dma("gpsimd", sel16[:, :], sel16c.ap(), "x")
    lone = P.dma("gpsimd", ones_sb[:, :], ones1.ap(), "x")
    lsv = P.dma("gpsimd", selrow[0:1, 0:8], aps["selvec"], "x")
    bc = P.mm(env["psum"][0][:, 0:8], ones_sb[0:1, :], selrow[0:1, 0:8],
              start=True, stop=True, sem="pe", waits=[lone, lsv])
    c0 = P.copy(selb[:, :], env["psum"][0][:, 0:8], sem="dve", waits=[("pe", bc)])
    env["bankfree"][0] = ("dve", c0)

    for rt in range(DK):
        for b in range(2):
            col = rt * 2 + b
            s1 = P.scan(scr8[:, :], apsb[:, :].rearrange("p (c k) -> p c k", c=8)[:, :, 32 + col],
                        apsb[:, :].rearrange("p (c k) -> p c k", c=8)[:, :, col],
                        initsb[:, rt:rt + 1], ALU.mult, ALU.add, sem="dve",
                        waits=[lap, lin, ("dve", P.cnt["dve"])])
            c1 = P.copy(cat8[:, 0:1], initsb[:, rt:rt + 1], sem="dve",
                        waits=[("dve", s1)])
            c2 = P.copy(cat8[:, 1:8], scr8[:, 0:7], sem="dve", waits=[("dve", c1)])
            c3 = P.tt(cat8[:, :], cat8[:, :], selb[:, :], ALU.mult, sem="dve",
                      waits=[("dve", c2)])
            P.reduce(carry[:, col:col + 1], cat8[:, :], ALU.add, sem="dve",
                     waits=[("dve", c3)])
    carry_done = P.cnt["dve"]

    fgtok = {}
    def q_evac(rt, b, bank, pecnt):
        swl = [("pe", pecnt), fgtok[("f", rt % 2)], fgtok[("g", rt % 2)],
               ("dve", carry_done), ("dve", P.cnt["dve"])]
        ssl = b
        s1 = P.scan(stt_[:, ssl * 512:(ssl + 1) * 512],
                    ftl[:, (rt % 2) * T + b * 512: (rt % 2) * T + (b + 1) * 512],
                    gtl[:, (rt % 2) * T + b * 512: (rt % 2) * T + (b + 1) * 512],
                    carry[:, rt * 2 + b: rt * 2 + b + 1], ALU.mult, ALU.add,
                    sem="dve", waits=swl)
        t1 = P.tt(tq[:, ssl * 512:(ssl + 1) * 512], bank[:, :],
                  stt_[:, ssl * 512:(ssl + 1) * 512], ALU.mult, sem="dve",
                  waits=[("dve", s1)])
        a1 = P.act(sgq[:, ssl * 512:(ssl + 1) * 512],
                   tq[:, ssl * 512:(ssl + 1) * 512], AF.Sigmoid, sem="act",
                   waits=[("dve", t1)])
        r1 = P.tt(hbuf[:, rt * T + b * 512: rt * T + (b + 1) * 512],
                  tq[:, ssl * 512:(ssl + 1) * 512],
                  sgq[:, ssl * 512:(ssl + 1) * 512], ALU.mult, sem="dve",
                  waits=[("act", a1)])
        return ("dve", r1)

    def fg_load(rt):
        wl = []
        if ("fw", rt % 2) in fgtok:
            wl = fgtok.pop(("fw", rt % 2))
        f1 = P.dma("sync", ftl[:, (rt % 2) * T:(rt % 2 + 1) * T], aps["f_in"][rt],
                   "hx", waits=wl)
        g1 = P.dma("sync", gtl[:, (rt % 2) * T:(rt % 2 + 1) * T], aps["g_in"][rt],
                   "hx", waits=wl)
        fgtok[("f", rt % 2)] = f1
        fgtok[("g", rt % 2)] = g1

    fg_load(0)
    fg_load(1)
    def q_evac_outer(mt, half, bank, pecnt):
        tok = q_evac(mt, half, bank, pecnt)
        if half == 1:
            if mt + 2 < DK:
                fgtok[("fw", mt % 2)] = [tok]
                fg_load(mt + 2)
        return tok

    emit_projA(P, env, w_ap=aps["wq"], n_m=DK, evac_cb=q_evac_outer, tagkey="wqs",
               rhs_ready=lxn, banks=(0, 1), chunk=1)
    ro_ready = P.cnt["dve"]

    emit_phaseB(P, env, w2_ap=aps["wro"], n_i=DK, rhs_buf=hbuf, d_chunk=1,
                first_into_acc=False, hready=("dve", ro_ready))
    ro_used = P.cnt["pe"]

    env["h_guard"] = ("pe", ro_used)
    emit_glu_ffn(P, env, w1_ap=aps["wgate"], w3_ap=aps["wup"], w2_ap=aps["wdown"],
                 n_i=16, c_col=None, tag="lg", i_chunk=1, d_chunk=1)

    emit_moe_router(P, env, wg_ap=aps["wgmoe"], call_tile=callb, ident_sb=ident)
    env["bankfree"][0] = env.pop("rtbank0")
    env["bankfree"][1] = env.pop("rtbank1")
    cb = nc.alloc_sbuf_tensor("cbB", [128, T], BF16)
    pbc = env["psum"][2]
    for e in range(16):
        for half in range(2):
            wl = [("dve", P.cnt["dve"])]
            if e == 0 and half == 0:
                wl.append(lsel)
            if 2 in env["bankfree"]:
                wl.append(env["bankfree"].pop(2))
            pe1 = P.mm(pbc[:, 0:512], sel16[:, e * 128:(e + 1) * 128],
                       callb[:, half * 512:(half + 1) * 512],
                       start=True, stop=True, sem="pe", waits=wl)
            cc = P.copy(cb[:, half * 512:(half + 1) * 512], pbc[:, 0:512],
                        sem="dve", waits=[("pe", pe1)])
            env["bankfree"][2] = ("dve", cc)
        emit_glu_ffn(P, env, w1_ap=aps["w1t"][e], w3_ap=aps["w3t"][e],
                     w2_ap=aps["w2t"][e], n_i=11, c_col=cb[:, :], tag=f"e{e}",
                     i_chunk=1, d_chunk=1)
    for sh in range(2):
        emit_glu_ffn(P, env, w1_ap=aps["ws1t"][:, sh * 1408:(sh + 1) * 1408],
                     w3_ap=aps["ws3t"][:, sh * 1408:(sh + 1) * 1408],
                     w2_ap=aps["ws2t"][sh * 1408:(sh + 1) * 1408, :],
                     n_i=11, c_col=None, tag=f"sh{sh}", i_chunk=1, d_chunk=1)
    env["xn0_lastuse"] = ("pe", P.cnt["pe"])
    env["apsb"] = apsb
    env["ftl"] = ftl
    env["gtl"] = gtl
    env["sttl"] = stt_
    env["cbB"] = cb
    env["sel16_load"] = lsel
    env["ftl_free"] = ("dve", ro_ready)
    env["gtl_free"] = ("dve", ro_ready)
    return env


def emit_LN1(P, env):
    nc = P.nc
    xTb = env["acc"]; xnb = env["xnb"]; pb = env["psum"]
    sq = env["sil"]
    ftl = env["ftl"]; gtl = env["gtl"]
    meanB = ftl[:, 0:T]; rstdB = ftl[:, T:2 * T]
    m16 = gtl[0:1, 0:T]; r16 = gtl[0:1, T:2 * T]
    onescnp = nc.inline_tensor(np.ones((128, 1), np.float32), name="onescol")
    onesc = nc.alloc_sbuf_tensor("onescol_sb", [128, 1], F32)
    env["onescol_sb"] = onesc
    oc = P.dma("sync", onesc[:, :], onescnp.ap(), "hx")
    x1done = env["x1done"]
    mrow = env["sil"][32:33, 0:T]
    vrow = env["sil"][64:65, 0:T]
    bf_ = env["bankfree"]

    wl0 = [oc, x1done]
    for bkk in range(4):
        if bkk in bf_:
            wl0.append(bf_.pop(bkk))
    sqfree = {}
    for dk in range(DK):
        for half in range(2):
            sq_sl = sq[:, half * 512: half * 512 + 512]
            swl = [x1done]
            if half in sqfree:
                swl.append(sqfree.pop(half))
            s_ = P.tt(sq_sl, xTb[:, dk * T + half * 512: dk * T + (half + 1) * 512],
                      xTb[:, dk * T + half * 512: dk * T + (half + 1) * 512],
                      ALU.mult, sem="dve", waits=swl)
            P.mm(pb[half][0:1, :], onesc[:, :],
                 xTb[:, dk * T + half * 512: dk * T + (half + 1) * 512],
                 start=(dk == 0), stop=(dk == DK - 1),
                 waits=(wl0 if dk == 0 and half == 0 else ()))
            mq_ = P.mm(pb[2 + half][0:1, :], onesc[:, :], sq_sl,
                       start=(dk == 0), stop=(dk == DK - 1),
                       waits=[("dve", s_)], sem="pe")
            sqfree[half] = ("pe", mq_)
    pc = P.cnt["pe"]
    c1 = P.act(mrow[:, 0:512], pb[0][0:1, :], AF.Copy, scale=1.0 / D, sem="act",
               waits=[("pe", pc)])
    c2 = P.act(mrow[:, 512:T], pb[1][0:1, :], AF.Copy, scale=1.0 / D, sem="act",
               waits=[("pe", pc)])
    c3 = P.act(vrow[:, 0:512], pb[2][0:1, :], AF.Copy, scale=1.0 / D, sem="act",
               waits=[("pe", pc)])
    c4 = P.act(vrow[:, 512:T], pb[3][0:1, :], AF.Copy, scale=1.0 / D, sem="act",
               waits=[("pe", pc)])
    for bkk in range(4):
        bf_[bkk] = ("act", [c1, c2, c3, c4][bkk])
    scr_row = env["tmp"][64:65, 0:T]
    d0 = P.tt(scr_row, mrow[:, :], mrow[:, :], ALU.mult, sem="dve",
              waits=[("act", c4)])
    d1 = P.tt(vrow[:, :], scr_row, vrow[:, :], ALU.subtract, sem="dve",
              waits=[("dve", d0)])
    a1 = P.act(vrow[:, :], vrow[:, :], AF.Ln, bias=1e-5, scale=-1.0, sem="act",
               waits=[("dve", d1)])
    a2 = P.act(vrow[:, :], vrow[:, :], AF.Exp, scale=-0.5, sem="act",
               waits=[("act", a1)])
    c5 = P.copy(m16[:, :], mrow[:, :], sem="dve",
                waits=[("act", a2), env["gtl_free"]])
    c6 = P.copy(r16[:, :], vrow[:, :], sem="dve", waits=[("act", a2), ("dve", c5)])
    for half in range(2):
        for j, (dst, srow) in enumerate(((meanB, m16), (rstdB, r16))):
            bk = half * 2 + j
            wl = [("dve", c6)]
            if half == 0 and j == 0:
                wl.append(env["ftl_free"])
            if bk in bf_:
                wl.append(bf_.pop(bk))
            mc = P.mm(pb[bk][:, :], env["ones_sb"][0:1, :],
                      srow[:, half * 512:(half + 1) * 512],
                      start=True, stop=True, sem="pe", waits=wl)
            cc = P.copy(dst[:, half * 512:(half + 1) * 512], pb[bk][:, :],
                        sem="dve", waits=[("pe", mc)])
            bf_[bk] = ("dve", cc)
    bc = P.cnt["dve"]
    for dk in range(DK):
        u1 = P.tt(env["tmp"][:, :], xTb[:, dk * T:(dk + 1) * T], meanB[:, :],
                  ALU.subtract, sem="dve",
                  waits=[("dve", bc), env["xn0_lastuse"], ("dve", P.cnt["dve"])])
        P.tt(xnb[:, dk * T:(dk + 1) * T], env["tmp"][:, :], rstdB[:, :], ALU.mult,
             sem="dve", waits=[("dve", u1)])
    env["xn1_ready"] = ("dve", P.cnt["dve"])


def emit_l1ffn(P, env, aps):
    nc = P.nc
    xnb = env["xnb"]; pb = env["psum"]; bf_ = env["bankfree"]
    slotuse = env["slotuse"]
    xc = env["h"]
    sil = env["sil"]; tmp = env["tmp"]; sttl = env["sttl"]
    ftl = env["ftl"]; gtl = env["gtl"]
    sel16 = env["sel16"]; onesb = env["ones_sb"]
    onesc = env["onescol_sb"]
    W1SLOT = env["W1SLOT"]
    convsb = nc.alloc_sbuf_tensor("convsb", [128, 32], F32)
    rotsb = nc.alloc_sbuf_tensor("rotsb", [128, 2], F32)
    qtl = nc.alloc_sbuf_tensor("qtl", [128, T], BF16)
    ktl = nc.alloc_sbuf_tensor("ktl", [128, T], BF16)
    prodb = nc.alloc_sbuf_tensor("prodb", [128, 512], BF16)
    vgwb = nc.alloc_sbuf_tensor("vgwb", [128, 16 * 128], BF16)
    vtile = nc.alloc_sbuf_tensor("vtile", [128, T], BF16)
    qkrow = nc.alloc_sbuf_tensor("qkrow", [16, 3 * 512], BF16)
    qkb = nc.alloc_sbuf_tensor("qkb", [128, 3 * 512], BF16)
    extf = nc.alloc_sbuf_tensor("extf", [128, T], BF16)
    protb = vgwb[:, 0:128]
    bd64b = vgwb[:, 128:256]
    wvrb = qtl
    rtrD = ftl[0:16, T:2 * T]
    pb6 = nc.alloc_psum_tensor("pb6", [128, 512], F32)
    pb7 = nc.alloc_psum_tensor("pb7", [128, 512], F32)
    onesfnp = nc.inline_tensor(np.ones((1, 128), np.float32), name="onesfrow")
    onesf_row = nc.alloc_sbuf_tensor("onesf_row", [1, 128], F32)
    onesc = env["onescol_sb"]
    outf_tiles = ([env["w1b"][:, i * T:(i + 1) * T] for i in range(4)] +
                  [env["w3b"][:, i * T:(i + 1) * T] for i in range(4)] +
                  [gtl[:, 0:T], gtl[:, T:2 * T], env["cbB"][:, :]] +
                  [qtl[:, :], ktl[:, :]] +
                  [extf[:, 0:T], qkb[:, 0:T],
                   [prodb[:, 0:512], qkb[:, 1024:1536]]])
    rtrA = sil[0:16, :]; rtrB = tmp[0:16, :]; rtrC = sttl[0:16, :]
    mrow = sttl[0:1, 0:T]; vrow = tmp[0:1, 0:T]
    o_t = ftl[:, 0:T]

    lcv = P.dma("sync", convsb[:, :], aps["convw"], "hx")
    lof = P.dma("sync", onesf_row[:, :], onesfnp.ap(), "hx")
    lrt = P.dma("sync", rotsb[:, :], aps["rotcs"], "hx")
    lpr = P.dma("gpsimd", protb[:, :], aps["protc"], "x")
    lbd = P.dma("gpsimd", bd64b[:, :], aps["bd64c"], "x")

    hguard = env["h_guard2"]
    xnr = env["xn1_ready"]
    xc_tok = []
    cscr_war = {}
    for dk in range(DK):
        cw0 = convsb[:, 2 * dk:2 * dk + 1]
        cw1 = convsb[:, 2 * dk + 1:2 * dk + 2]
        S = sttl if dk % 2 == 0 else tmp
        w0 = [xnr, lcv] if dk == 0 else []
        if (dk % 2, 0) in cscr_war:
            w0.append(cscr_war.pop((dk % 2, 0)))
        t0 = P.tsc(S[:, 0:512], xnb[:, dk * T:dk * T + 512], cw1, None,
                   ALU.mult, sem="dve", waits=w0)
        w1 = []
        if (dk % 2, 1) in cscr_war:
            w1.append(cscr_war.pop((dk % 2, 1)))
        t1 = P.tsc(S[:, 512:T], xnb[:, dk * T + 512:(dk + 1) * T], cw1, None,
                   ALU.mult, sem="dve", waits=w1)
        t2 = P.stt(S[:, 512:T], xnb[:, dk * T:dk * T + 512], cw0,
                   S[:, 512:T], ALU.mult, ALU.add, sem="dve",
                   waits=[("dve", t1)])
        aw = [hguard] if dk < 2 else []
        a0 = P.act(xc[:, dk * T:dk * T + 512], S[:, 0:512], AF.Silu, sem="act",
                   waits=aw + [("dve", t0)])
        a1 = P.act(xc[:, dk * T + 512:(dk + 1) * T], S[:, 512:T], AF.Silu,
                   sem="act", waits=[("dve", t2)])
        cscr_war[(dk % 2, 0)] = ("act", a0)
        cscr_war[(dk % 2, 1)] = ("act", a1)
        xc_tok.append(("act", a1))
    xc_ready = xc_tok[-1]

    dpsum = [pb[3][0:16, :], pb[4][0:16, :], pb[5][0:16, :]]
    for bb in (2, 3, 4, 5):
        if bb in bf_:
            pass
    qk_free = {}
    for tt in range(8):
        dc = {}
        for nm, wsrc in (("q", aps["wqf"]), ("k", aps["wkf"])):
            slot = tt % 2
            key = ("l1" + nm, slot)
            wl = [("pe", slotuse[key])] if key in slotuse else []
            wb = env["w1b"] if nm == "q" else env["w3b"]
            dc[nm] = P.dma("gpsimd",
                           wb[:, slot * W1SLOT: slot * W1SLOT + 16 * 128].rearrange(
                               "p (k c) -> p k c", k=16),
                           wsrc[:, tt * 128:(tt + 1) * 128].rearrange(
                               "(k p) c -> p k c", p=128),
                           "w", waits=wl)
        for nm, bk in (("q", 0), ("k", 1)):
            wb = env["w1b"] if nm == "q" else env["w3b"]
            slot = tt % 2
            for half in range(2):
                wl = [dc[nm], xc_ready]
                if (nm, "bank") in qk_free:
                    wl.append(qk_free.pop((nm, "bank")))
                if tt == 0 and half == 0 and bk in bf_:
                    wl.append(bf_.pop(bk))
                for k in range(DK):
                    off = slot * W1SLOT + k * 128
                    P.mm(pb[bk][:, :], wb[:, off:off + 128],
                         xc[:, k * T + half * 512: k * T + (half + 1) * 512],
                         start=(k == 0), stop=(k == DK - 1),
                         waits=wl if k == 0 else (),
                         sem="pe" if k == DK - 1 else None)
                pc = P.cnt["pe"]
                slotuse[("l1" + nm, slot)] = pc
                dst = qtl if nm == "q" else ktl
                dwl = [("pe", pc)]
                if (nm, "tile", half) in qk_free:
                    dwl.append(qk_free.pop((nm, "tile", half)))
                if nm == "q":
                    aq = P.act(dst[:, half * 512:(half + 1) * 512], pb[bk][:, :],
                               AF.Silu, sem="act", waits=dwl)
                    qk_free[(nm, "bank")] = ("act", aq)
                else:
                    dq = P.copy(dst[:, half * 512:(half + 1) * 512], pb[bk][:, :],
                                sem="dve", waits=dwl)
                    qk_free[(nm, "bank")] = ("dve", dq)
            evac_tok = qk_free[(nm, "bank")]
            if nm == "q":
                q_act = evac_tok
            wl2 = [("dve", P.cnt["dve"]), lpr, lrt, evac_tok]
            if ("rot", "bank") in qk_free:
                wl2.append(qk_free.pop(("rot", "bank")))
            rm = P.mm(pb[2][:, :], protb[:, :], dst[:, 512:T], start=True,
                      stop=True, sem="pe", waits=wl2)
            rt1 = P.tsc(sttl[:, 0:512], dst[:, 512:T], rotsb[:, 0:1], None,
                        ALU.mult, sem="dve",
                        waits=[("dve", P.cnt["dve"]), lrt, evac_tok] +
                              ([xc_ready] if tt == 0 else []))
            rt2 = P.stt(dst[:, 512:T], pb[2][:, :], rotsb[:, 1:2],
                        sttl[:, 0:512], ALU.mult, ALU.add, sem="dve",
                        waits=[("pe", rm), ("dve", rt1)])
            qk_free[("rot", "bank")] = ("dve", rt2)
            qk_free[(nm, "tile", 0)] = ("dve", rt2)
            qk_free[(nm, "tile", 1)] = ("dve", rt2)
        qr = ("dve", P.cnt["dve"])
        for pi, (lh, mh) in enumerate(((0, 0), (1, 0), (1, 1))):
            pw = [qr, ("dve", P.cnt["dve"]), lbd, q_act]
            if ("prod",) in qk_free:
                pw.append(qk_free.pop(("prod",)))
            pd = P.tt(prodb[:, :], qtl[:, lh * 512:(lh + 1) * 512],
                      ktl[:, mh * 512:(mh + 1) * 512], ALU.mult, sem="dve",
                      waits=pw)
            dmw = [("dve", pd)]
            if tt == 0 and 3 + pi in bf_:
                dmw.append(bf_.pop(3 + pi))
            dm = P.mm(dpsum[pi], bd64b[:, tt * 16:(tt + 1) * 16], prodb[:, :],
                      start=(tt == 0), stop=(tt == 7), sem="pe", waits=dmw)
            qk_free[("prod",)] = ("pe", dm)
        qk_free[("q", "tile", 0)] = ("pe", P.cnt["pe"])
        qk_free[("q", "tile", 1)] = qk_free[("q", "tile", 0)]
        qk_free[("k", "tile", 0)] = qk_free[("q", "tile", 0)]
        qk_free[("k", "tile", 1)] = qk_free[("q", "tile", 0)]
    dots_pe = P.cnt["pe"]
    if ("rot", "bank") in qk_free:
        bf_[2] = qk_free.pop(("rot", "bank"))

    wvr_ready = P.dma("sync", wvrb[:, :], aps["wvr_in"], "hx",
                      waits=[("pe", dots_pe), ("dve", P.cnt["dve"])])

    lg_pe = []
    for half in range(2):
        bk = half
        wl = [wvr_ready, xc_ready]
        if bk in bf_:
            wl.append(bf_.pop(bk))
        if (("q", "bank") if bk == 0 else ("k", "bank")) in qk_free:
            wl.append(qk_free.pop(("q", "bank") if bk == 0 else ("k", "bank")))
        for dk in range(DK):
            P.mm(pb[bk][0:64, :], wvrb[:, dk * 64:(dk + 1) * 64],
                 xc[:, dk * T + half * 512: dk * T + (half + 1) * 512],
                 start=(dk == 0), stop=(dk == DK - 1),
                 waits=wl if dk == 0 else (),
                 sem="pe" if dk == DK - 1 else None)
        lg_pe.append(P.cnt["pe"])

    e0a = P.act(rtrA[:, 0:512], pb[0][0:16, :], AF.Exp, sem="act",
                waits=[("pe", lg_pe[0])])
    e0b = P.act(rtrA[:, 512:T], pb[1][0:16, :], AF.Exp, sem="act",
                waits=[("pe", lg_pe[1])])
    e1a = P.act(rtrB[:, 0:512], pb[0][32:48, :], AF.Exp, sem="act",
                waits=[("pe", lg_pe[0])])
    e1b = P.act(rtrB[:, 512:T], pb[1][32:48, :], AF.Exp, sem="act",
                waits=[("pe", lg_pe[1])])
    bf_[0] = ("act", e1a)
    bf_[1] = ("act", e1b)
    r1 = P.tt(rtrC[:, :], rtrA[:, :], rtrB[:, :], ALU.add, sem="dve",
              waits=[("act", e1b)])
    r2 = P.recip(rtrC[:, :], rtrC[:, :], sem="dve", waits=[("dve", r1)])
    r3 = P.tt(rtrD[:, :], rtrA[:, :], rtrB[:, :], ALU.is_ge, sem="dve",
              waits=[("dve", r2)])
    r4 = P.tt(rtrA[:, :], rtrA[:, :], rtrB[:, :], ALU.max, sem="dve",
              waits=[("dve", r3)])
    r5 = P.tt(rtrA[:, :], rtrA[:, :], rtrC[:, :], ALU.mult, sem="dve",
              waits=[("dve", r4)])
    r6 = P.tt(rtrB[:, 0:512], rtrD[:, 0:512], rtrD[:, 512:T], ALU.is_equal,
              sem="dve", waits=[("dve", r5)])
    r7 = P.tt(rtrC[:, 0:512], rtrB[:, 0:512], rtrA[:, 512:T], ALU.mult,
              sem="dve", waits=[("dve", r6)])
    r8 = P.tsc(rtrB[:, 512:T], rtrB[:, 0:512], 1.0, None, ALU.add, sem="dve",
               waits=[("dve", r7)])
    r8r = P.recip(rtrB[:, 512:T], rtrB[:, 512:T], sem="dve", waits=[("dve", r8)])
    r9 = P.tt(rtrC[:, 512:T], rtrA[:, 512:T], rtrB[:, 512:T], ALU.mult,
              sem="dve", waits=[("dve", r8r)])
    q1 = P.tt(rtrD[:, 0:512], rtrA[:, 0:512], dpsum[0], ALU.mult, sem="dve",
              waits=[("dve", r9), ("pe", dots_pe)])
    q2 = P.tsc(qkrow[:, 0:512], rtrD[:, 0:512], 0.125, None, ALU.mult,
               sem="dve", waits=[("dve", q1)])
    q3 = P.tt(rtrD[:, 512:T], rtrC[:, 0:512], dpsum[1], ALU.mult, sem="dve",
              waits=[("dve", q2)])
    q4 = P.tsc(qkrow[:, 512:1024], rtrD[:, 512:T], 0.0625, None, ALU.mult,
               sem="dve", waits=[("dve", q3)])
    q5 = P.tt(rtrD[:, 0:512], rtrC[:, 512:T], dpsum[2], ALU.mult, sem="dve",
              waits=[("dve", q4)])
    q6 = P.tsc(qkrow[:, 1024:1536], rtrD[:, 0:512], 0.125, None, ALU.mult,
               sem="dve", waits=[("dve", q5)])
    for bb in (3, 4, 5):
        bf_[bb] = ("dve", q6)
    qk_ready = ("dve", q6)

    vg_free = {}
    outf_done = []
    ldv_tok = [None] * 16
    ldv_tok[0] = P.dma("gpsimd", vgwb[:, :].rearrange("p (k c) -> p k c", k=16),
                       aps["wvf"][:, 0:128].rearrange("(k p) c -> p k c", p=128),
                       "w", waits=[("pe", P.cnt["pe"])])
    W2SLOT = env["W2SLOT"]
    w2b = env["w2b"]
    for h in range(16):
        ldv = ldv_tok[h]
        for half in range(2):
            bk = half
            wl = [ldv, xc_ready]
            if bk in bf_:
                wl.append(bf_.pop(bk))
            for k in range(DK):
                P.mm(pb[bk][:, :], vgwb[:, k * 128:(k + 1) * 128],
                     xc[:, k * T + half * 512: k * T + (half + 1) * 512],
                     start=(k == 0), stop=(k == DK - 1),
                     waits=wl if k == 0 else (),
                     sem="pe" if k == DK - 1 else None)
            pc = P.cnt["pe"]
            vw = [("pe", pc)]
            if ("vtile", half) in vg_free:
                vw.append(vg_free.pop(("vtile", half)))
            cv = P.copy(vtile[:, half * 512:(half + 1) * 512], pb[bk][:, :],
                        sem="dve", waits=vw)
            bf_[bk] = ("dve", cv)
        v_ready = P.cnt["dve"]
        vmm_done = ("pe", P.cnt["pe"])
        gslot = h % 2
        gwl = [("pe", slotuse[("Bw2", gslot)])] if ("Bw2", gslot) in slotuse else []
        ldg = P.dma("gpsimd",
                    w2b[:, gslot * W2SLOT: gslot * W2SLOT + 16 * 128].rearrange(
                        "p (k c) -> p k c", k=16),
                    aps["wgf"][:, 128 * h:(h + 1) * 128].rearrange(
                        "(k p) c -> p k c", p=128), "w", waits=gwl)
        if h < 15:
            ldv_tok[h + 1] = P.dma(
                "gpsimd", vgwb[:, :].rearrange("p (k c) -> p k c", k=16),
                aps["wvf"][:, 128 * (h + 1):128 * (h + 2)].rearrange(
                    "(k p) c -> p k c", p=128), "w", waits=[vmm_done])
        odst = [(o_t[:, 0:512], vtile[:, 0:512]),
                (sttl[:, 0:512], vtile[:, 0:512]),
                (tmp[:, 512:T], vtile[:, 512:T])]
        otoks = []
        for j in range(3):
            bk = 2 + j % 2
            wl = [qk_ready, ("dve", v_ready)]
            if h == 0 and j == 0 and "sel16_load" in env:
                wl.append(env["sel16_load"])
            if bk in bf_:
                wl.append(bf_.pop(bk))
            bm = P.mm(pb[bk][:, :], sel16[:, h * 128:(h + 1) * 128],
                      qkrow[:, j * 512:(j + 1) * 512], start=True, stop=True,
                      sem="pe", waits=wl)
            oj = P.tt(odst[j][0], pb[bk][:, :], odst[j][1], ALU.mult,
                      sem="dve", waits=[("pe", bm), ("dve", P.cnt["dve"])])
            otoks.append(oj)
            bf_[bk] = ("dve", oj)
        o4 = P.tt(o_t[:, 512:T], sttl[:, 0:512], tmp[:, 512:T], ALU.add,
                  sem="dve", waits=[("dve", otoks[2])])
        scr_row = ftl[0:1, T:2 * T]
        for half in range(2):
            base = 0 if half == 0 else 32
            w5 = [("dve", o4)]
            if half == 0:
                for bb in (6, 7):
                    if bb in bf_:
                        w5.append(bf_.pop(bb))
            sq1 = P.tt(sttl[:, half * 512:(half + 1) * 512],
                       o_t[:, half * 512:(half + 1) * 512],
                       o_t[:, half * 512:(half + 1) * 512], ALU.mult,
                       sem="dve", waits=[("dve", o4), ("dve", P.cnt["dve"])])
            P.mm(pb6[base:base + 1, :], onesc[:, :],
                 o_t[:, half * 512:(half + 1) * 512], start=True, stop=True,
                 waits=w5)
            P.mm(pb7[base:base + 1, :], onesc[:, :],
                 sttl[:, half * 512:(half + 1) * 512], start=True, stop=True,
                 waits=[("dve", sq1)], sem="pe")
        lnpc = P.cnt["pe"]
        rw0 = [("pe", lnpc), ("dve", P.cnt["dve"])]
        x1_ = P.act(mrow[:, 0:512], pb6[0:1, :], AF.Copy, scale=1.0 / 128,
                    sem="act", waits=rw0)
        x2_ = P.act(mrow[:, 512:T], pb6[32:33, :], AF.Copy, scale=1.0 / 128,
                    sem="act")
        x3_ = P.act(vrow[:, 0:512], pb7[0:1, :], AF.Copy, scale=1.0 / 128,
                    sem="act")
        x4_ = P.act(vrow[:, 512:T], pb7[32:33, :], AF.Copy, scale=1.0 / 128,
                    sem="act")
        bf_[6] = ("act", x4_)
        bf_[7] = ("act", x4_)
        v1_ = P.tt(scr_row, mrow[:, :], mrow[:, :], ALU.mult, sem="dve",
                   waits=[("act", x4_)])
        v2_ = P.tt(vrow[:, :], scr_row, vrow[:, :], ALU.subtract, sem="dve",
                   waits=[("dve", v1_)])
        a1_ = P.act(vrow[:, :], vrow[:, :], AF.Ln, bias=1e-5, scale=-1.0,
                    sem="act", waits=[("dve", v2_)])
        a2_ = P.act(vrow[:, :], vrow[:, :], AF.Exp, scale=-0.5, sem="act",
                    waits=[("act", a1_)])
        bdone = []
        for j, (srow, dstb) in enumerate(((mrow, sttl), (vrow, tmp))):
            for half in range(2):
                bk = 2 + (2 * j + half) % 2
                wl = [("act", a2_), ("dve", P.cnt["dve"])]
                if bk in bf_:
                    wl.append(bf_.pop(bk))
                if h == 0:
                    wl.append(lof)
                bm = P.mm(pb[bk][:, :], onesf_row[0:1, :],
                          srow[:, half * 512:(half + 1) * 512], start=True,
                          stop=True, sem="pe", waits=wl)
                bc2 = P.copy(dstb[:, half * 512:(half + 1) * 512], pb[bk][:, :],
                             sem="dve", waits=[("pe", bm)])
                bf_[bk] = ("dve", bc2)
                bdone.append(bc2)
        l1_ = P.tt(o_t[:, 0:512], o_t[:, 0:512], sttl[:, 0:512], ALU.subtract,
                   sem="dve", waits=[("dve", bdone[-1])])
        l2_ = P.tt(o_t[:, 512:T], o_t[:, 512:T], sttl[:, 512:T], ALU.subtract,
                   sem="dve", waits=[("dve", l1_)])
        l3_ = P.tt(o_t[:, 0:512], o_t[:, 0:512], tmp[:, 0:512], ALU.mult,
                   sem="dve", waits=[("dve", l2_)])
        l4_ = P.tt(o_t[:, 512:T], o_t[:, 512:T], tmp[:, 512:T], ALU.mult,
                   sem="dve", waits=[("dve", l3_)])
        for half in range(2):
            bk = half
            wl = [ldg]
            if bk in bf_:
                wl.append(bf_.pop(bk))
            for k in range(DK):
                off = gslot * W2SLOT + k * 128
                P.mm(pb[bk][:, :], w2b[:, off:off + 128],
                     xc[:, k * T + half * 512: k * T + (half + 1) * 512],
                     start=(k == 0), stop=(k == DK - 1),
                     waits=wl if k == 0 else (),
                     sem="pe" if k == DK - 1 else None)
            gpc = P.cnt["pe"]
            slotuse[("Bw2", gslot)] = gpc
            ag = P.act(sttl[:, half * 512:(half + 1) * 512], pb[bk][:, :],
                       AF.Silu, sem="act", waits=[("pe", gpc),
                                                  ("dve", l4_ if half == 0 else P.cnt["dve"])])
            tile = outf_tiles[h]
            if isinstance(tile, list):
                tdst = tile[half]
            else:
                tdst = tile[:, half * 512:(half + 1) * 512]
            ow = [("act", ag), ("dve", l4_)]
            if h < 8:
                g_ = max(slotuse.get(("l1q", 0), 0), slotuse.get(("l1q", 1), 0),
                         slotuse.get(("l1k", 0), 0), slotuse.get(("l1k", 1), 0))
                ow.append(("pe", g_))
            elif h in (11, 12, 13, 14, 15):
                ow.append(("pe", dots_pe))
            ot = P.tt(tdst,
                      sttl[:, half * 512:(half + 1) * 512],
                      o_t[:, half * 512:(half + 1) * 512], ALU.mult,
                      sem="dve", waits=ow)
            bf_[bk] = ("act", ag)
        vg_free["wdma"] = ("pe", P.cnt["pe"])
        vg_free[("vtile", 0)] = ("dve", P.cnt["dve"])
        vg_free[("vtile", 1)] = vg_free[("vtile", 0)]
        outf_done.append(P.cnt["dve"])
    outf_ready = ("dve", P.cnt["dve"])

    emit_phaseB(P, env, w2_ap=aps["woutf"], n_i=DK, rhs_buf=outf_tiles,
                d_chunk=1, first_into_acc=False, hready=outf_ready)
    env["l1ffn_done"] = ("pe", P.cnt["pe"])
    env["l1ffn_bufs"] = {"qtl": qtl, "ktl": ktl, "extf": extf, "vgwb": vgwb,
                         "convsb": convsb,
                         "wvrb": wvrb, "vtile": vtile, "qkb": qkb,
                         "prodb": prodb, "protb": protb, "bd64b": bd64b,
                         "onesf_row": onesf_row, "onesf_load": lof,
                         "pb6": pb6, "pb7": pb7}


def emit_l1attn(P, env, aps):
    nc = P.nc
    EPS = 1e-8
    xnb = env["xnb"]; pb = env["psum"]; bf_ = env["bankfree"]
    slotuse = env["slotuse"]
    hbuf = env["h"]; sil = env["sil"]; tmp = env["tmp"]; sttl = env["sttl"]
    ftl = env["ftl"]; gtl = env["gtl"]
    w1b = env["w1b"]; w3b = env["w3b"]; w2b = env["w2b"]
    onesb = env["ones_sb"]; onesc = env["onescol_sb"]; ident = env["ident"]
    l1f = env["l1ffn_done"]
    lf = env["l1ffn_bufs"]
    qtl, ktl, extf, vgwb, wvrb, vtile, qkb, prodb, protb, bd64b, onesf_row = (
        lf["qtl"], lf["ktl"], lf["extf"], lf["vgwb"], lf["wvrb"], lf["vtile"],
        lf["qkb"], lf["prodb"], lf["protb"], lf["bd64b"], lf["onesf_row"])
    pb6, pb7 = lf["pb6"], lf["pb7"]
    convsb = lf["convsb"]
    kbuf = hbuf[:, 0:4 * T]
    vbuf = hbuf[:, 4 * T:8 * T]
    mknbuf = hbuf[:, 8 * T:8 * T + 6144]
    mkf_tiles = [qtl[:, :], ktl[:, :], extf[:, 0:T],
                 env["cbB"][:, :], gtl[:, 0:T], gtl[:, T:2 * T]]
    wmvt = qkb[:, 0:1024]
    roll_slices = [prodb[:, 0:128], prodb[:, 128:256], prodb[:, 256:384],
                   prodb[:, 384:512], qkb[:, 1024:1152], qkb[:, 1152:1280]]
    if "apsb" in env:
        biasb = env["apsb"][:, 0:72]
    else:
        biasb = nc.alloc_sbuf_tensor("biasb", [128, 72], F32)
    normTb = qkb[:, 1280:1280 + 96]
    dnormb = None
    coeftb = qkb[:, 1376:1376 + 32]
    memb = vgwb
    mknrow = ftl[0:1, 0:T]
    mknrow16 = gtl[0:1, T:T + T]
    sqrow = ftl[32:33, 0:T]

    lro_toks = []
    for ri in range(6):
        lro_toks.append(P.dma("gpsimd", roll_slices[ri],
                              aps["rollc"][:, ri * 128:(ri + 1) * 128],
                              "x", waits=[l1f]))
    lbi = P.dma("sync", biasb[:, 0:4], aps["bk_in"], "hx")
    lbv = P.dma("sync", biasb[:, 4:8], aps["bv_in"], "hx")
    lbmb = P.dma("sync", biasb[:, 8:24], aps["bmb_in"], "hx")
    lbq = P.dma("sync", biasb[:, 24:40], aps["bq_in"], "hx")
    lbg = P.dma("sync", biasb[:, 40:56], aps["bg_in"], "hx")
    lnt = P.dma("gpsimd", normTb[:, :].rearrange("p (k c) -> p k c", k=6),
                aps["normT_in"].rearrange("(k p) c -> p k c", p=128), "w")
    nb = P.tsc(biasb[:, 56:72], biasb[:, 24:40], -1.0, None, ALU.mult,
               sem="dve", waits=[lbq, lbg])

    def kv_evac(name, boff):
        btok = lbi if name == "k" else lbv
        def cb(mt, half, bank, pecnt):
            dst = (kbuf if name == "k" else vbuf)
            tok = P.tsc(dst[:, mt * T + half * 512: mt * T + (half + 1) * 512],
                        bank[:, :], biasb[:, boff + mt:boff + mt + 1], None,
                        ALU.add, sem="dve", waits=[("pe", pecnt), btok,
                                                   l1f if mt + half == 0 else ("pe", 0)])
            return ("dve", tok)
        return cb
    slotuse[("wkT", 0)] = max(slotuse.get(("Bw2", 0), 0), env["l1ffn_done"][1])
    slotuse[("wkT", 1)] = slotuse[("wkT", 0)]
    slotuse[("wvT", 0)] = slotuse[("wkT", 0)]
    slotuse[("wvT", 1)] = slotuse[("wkT", 0)]
    slotuse[("knw", 0)] = slotuse[("wkT", 0)]
    slotuse[("knw", 1)] = slotuse[("wkT", 0)]
    emit_projA(P, env, w_ap=aps["wkT"], n_m=4, evac_cb=kv_evac("k", 0),
               tagkey="wkT", chunk=1, banks=(0, 1), rhs_ready=None,
               wbuf=w2b, wslot=2048)
    for s_ in (0, 1):
        slotuse[("wvT", s_)] = max(slotuse.get(("wvT", s_), 0),
                                   slotuse.get(("wkT", 0), 0),
                                   slotuse.get(("wkT", 1), 0))
    emit_projA(P, env, w_ap=aps["wvT"], n_m=4, evac_cb=kv_evac("v", 4),
               tagkey="wvT", chunk=1, banks=(0, 1), rhs_ready=None,
               wbuf=w2b, wslot=2048)

    bkB = sil[:, 0:512]
    wl = [("dve", P.cnt["dve"])]
    if 2 in bf_:
        wl.append(bf_.pop(2))
    lkr = P.dma("sync", sttl[0:1, 0:512], aps["bkrow"], "hx", waits=[l1f])
    bm = P.mm(pb[2][:, :], onesf_row[0:1, :], sttl[0:1, 0:512], start=True,
              stop=True, sem="pe", waits=wl + [lkr, lf["onesf_load"]])
    bkc = P.copy(bkB, pb[2][:, :], sem="dve", waits=[("pe", bm)])
    bf_[2] = ("dve", bkc)
    kn_tiles = [w1b[:, i * 512:(i + 1) * 512] for i in range(8)]
    for grp in range(2):
        wdc = {}
        for k in range(DK):
            slot = k % 2
            wl = []
            if ("knw", slot) in slotuse:
                wl.append(("pe", slotuse[("knw", slot)]))
            wdc[k] = P.dma("gpsimd", w3b[:, slot * 2048:slot * 2048 + 512],
                           aps["wkT"][k * 128:(k + 1) * 128, :], "w", waits=wl)
            for ti in range(4):
                tt = grp * 4 + ti
                bk_ = 2 + ti
                wl2 = [wdc[k]] if ti == 0 else []
                if k == 0 and bk_ in bf_:
                    wl2.append(bf_.pop(bk_))
                if k == 0 and ti == 0 and grp == 0:
                    wl2.append(env["xn1_ready"])
                P.mm(pb[bk_][:, :], xnb[:, k * T + tt * 128:k * T + (tt + 1) * 128],
                     w3b[:, slot * 2048:slot * 2048 + 512],
                     start=(k == 0), stop=(k == DK - 1),
                     waits=wl2,
                     sem="pe" if ti == 3 else None)
            slotuse[("knw", slot)] = P.cnt["pe"]
        pcg = P.cnt["pe"]
        for ti in range(4):
            tt = grp * 4 + ti
            tok = P.tt(kn_tiles[tt], pb[2 + ti][:, :], bkB, ALU.add, sem="dve",
                       waits=[("pe", pcg), ("dve", bkc)])
            bf_[2 + ti] = ("dve", tok)
    kn_ready = P.cnt["dve"]
    bf_[("mbw", 0)] = ("pe", P.cnt["pe"])
    bf_[("mbw", 1)] = ("pe", P.cnt["pe"])
    bf_[("memd", 0)] = env["l1ffn_done"]
    bf_[("memd", 1)] = env["l1ffn_done"]

    for hk in range(4):
        for tt in range(8):
            xcat = w2b[:, 2048:2304]
            xcat2 = w2b[:, 2304:2816]
            src = kn_tiles[tt][:, hk * 128:(hk + 1) * 128]
            awl = [("dve", kn_ready), ("dve", P.cnt["dve"]),
                   ("act", P.cnt["act"]), ("pe", P.cnt["pe"])]
            a1 = P.act(xcat[:, 0:128], src, AF.Relu, sem="act", waits=awl)
            a2 = P.act(xcat[:, 128:256], src, AF.Relu, scale=-1.0, sem="act",
                       waits=[("act", a1)])
            c1 = P.tt(xcat2[:, 0:256], xcat[:, :], xcat[:, :], ALU.max,
                      sem="dve", waits=[("act", a2)])
            c2 = P.tt(xcat2[:, 256:512], xcat[:, :], xcat[:, :], ALU.max,
                      sem="dve", waits=[("act", a2), ("dve", c1)])
            mkr = sil[:, 0:768]
            for j in (1, 2, 3):
                P.tt(mkr[:, (j - 1) * 256:j * 256], xcat[:, :],
                     xcat2[:, 256 - j:512 - j], ALU.mult, sem="dve",
                     waits=[("dve", c2), ("dve", P.cnt["dve"])])
            m1 = P.cnt["dve"]
            sqw = [("dve", m1)]
            if ("tmpd",) in bf_:
                sqw.append(bf_.pop(("tmpd",)))
            sq_ = P.tt(tmp[:, 0:768], mkr[:, :], mkr[:, :], ALU.mult, sem="dve",
                       waits=sqw)
            r1 = P.reduce(sttl[:, 0:1], tmp[:, 0:768], ALU.add, sem="dve",
                          waits=[("dve", sq_)])
            s1 = P.act(sttl[:, 1:2], sttl[:, 0:1], AF.Ln, sem="act",
                       waits=[("dve", r1)])
            s2 = P.act(sttl[:, 1:2], sttl[:, 1:2], AF.Exp, scale=0.5, sem="act",
                       waits=[("act", s1)])
            s3 = P.tsc(sttl[:, 1:2], sttl[:, 1:2], 1e-12, None, ALU.max,
                       sem="dve", waits=[("act", s2)])
            s4 = P.recip(sttl[:, 2:3], sttl[:, 1:2], sem="dve", waits=[("dve", s3)])
            mk_ = P.tsc(mknbuf[:, tt * 768:(tt + 1) * 768], mkr[:, :],
                        sttl[:, 2:3], None, ALU.mult, sem="dve",
                        waits=[("dve", s4)])
            P.copy(sttl[:, 3 + tt:4 + tt], sttl[:, 1:2], sem="dve",
                   waits=[("dve", s4), ("dve", P.cnt["dve"])])
        sq8 = P.cnt["dve"]
        for tt in range(8):
            bk_ = (tt % 2)
            wl = [("dve", sq8)]
            if bk_ in bf_:
                wl.append(bf_.pop(bk_))
            tc = P.transpose(pb[bk_][0:1, 0:128], sttl[:, 3 + tt:4 + tt],
                             ident[:, :], sem="pe", waits=wl)
            cc = P.copy(mknrow[:, tt * 128:(tt + 1) * 128], pb[bk_][0:1, 0:128],
                        sem="dve", waits=[("pe", tc)])
            bf_[bk_] = ("dve", cc)
        mkn_done = P.cnt["dve"]

        c0t = w2b[:, 0:T]; c1t = w2b[:, T:2 * T]
        ra = P.act(c0t, kbuf[:, hk * T:(hk + 1) * T], AF.Relu, sem="act",
                   waits=[("dve", P.cnt["dve"]), ("pe", P.cnt["pe"])])
        rb = P.act(c1t, kbuf[:, hk * T:(hk + 1) * T], AF.Relu, scale=-1.0,
                   sem="act", waits=[("act", ra)])
        for j in (1, 2, 3):
            for half_t in range(2):
                dst = mkf_tiles[(j - 1) * 2 + half_t]
                bk_ = 2 + (j + half_t) % 2
                src_main = c0t if half_t == 0 else c1t
                src_wrap = c1t if half_t == 0 else c0t
                wl = [("act", rb), ("dve", P.cnt["dve"])]
                if hk == 0:
                    wl.extend(lro_toks)
                if bk_ in bf_:
                    wl.append(bf_.pop(bk_))
                tok = None
                for hh in range(2):
                    wlh = wl if hh == 0 else [("dve", tok)]
                    P.mm(pb[bk_][:, 0:512], roll_slices[2 * (j - 1)],
                         src_main[:, hh * 512:(hh + 1) * 512], start=True, stop=False,
                         waits=wlh)
                    P.mm(pb[bk_][:, 0:512], roll_slices[2 * (j - 1) + 1],
                         src_wrap[:, hh * 512:(hh + 1) * 512], start=False, stop=True,
                         sem="pe")
                    pc = P.cnt["pe"]
                    x_rep = c0t if half_t == 0 else c1t
                    tok = P.tt(dst[:, hh * 512:(hh + 1) * 512],
                               x_rep[:, hh * 512:(hh + 1) * 512],
                               pb[bk_][:, 0:512], ALU.mult, sem="dve",
                               waits=[("pe", pc)])
                bf_[bk_] = ("dve", tok)
        mkf_done = ("dve", P.cnt["dve"])

        den4 = sil[32:36, 0:T]
        rec4 = sil[64:68, 0:T]
        coef4 = sil[0:4, 0:T]
        nbc4 = tmp[32:36, 0:T]
        for half in range(2):
            bk_ = 4 + half
            wl = [mkf_done, lnt]
            if bk_ in bf_:
                wl.append(bf_.pop(bk_))
            for fb in range(6):
                P.mm(pb[bk_][0:4, 0:512],
                     normTb[:, fb * 16 + 4 * hk: fb * 16 + 4 * hk + 4],
                     mkf_tiles[fb][:, half * 512:(half + 1) * 512],
                     start=(fb == 0), stop=(fb == 5),
                     waits=wl if fb == 0 else (),
                     sem="pe" if fb == 5 else None)
            dpc = P.cnt["pe"]
            bk2 = 2 + half
            wl2 = [("dve", mkn_done)]
            if bk2 in bf_:
                wl2.append(bf_.pop(bk2))
            bmn = P.mm(pb[bk2][0:4, 0:512], onesf_row[0:1, 0:4],
                       mknrow[:, half * 512:(half + 1) * 512], start=True,
                       stop=True, sem="pe", waits=wl2)
            cnb = P.copy(nbc4[:, half * 512:(half + 1) * 512], pb[bk2][0:4, :],
                         sem="dve", waits=[("pe", bmn), ("dve", P.cnt["dve"])])
            bf_[bk2] = ("dve", cnb)
            da = P.stt(den4[:, half * 512:(half + 1) * 512],
                       nbc4[:, half * 512:(half + 1) * 512], 1e-8,
                       pb[bk_][0:4, 0:512], ALU.mult, ALU.add, sem="dve",
                       waits=[("pe", dpc), ("dve", cnb)])
            bf_[bk_] = ("dve", da)
        dd = P.cnt["dve"]
        r4 = P.recip(rec4[:, :], den4[:, :], sem="dve", waits=[("dve", dd)])
        r4b = P.copy(sttl[0:4, 0:T], rec4[:, :], sem="dve", waits=[("dve", r4)])
        rec16 = mknrow16
        rec16 = vtile[0:4, 0:T]
        rc16 = P.copy(rec16, rec4[:, :], sem="dve", waits=[("dve", r4)])
        cf0 = P.recip(nbc4[:, :], nbc4[:, :], sem="dve",
                      waits=[("dve", dd), ("dve", r4b)])
        cf1 = P.tt(coef4[:, :], den4[:, :], nbc4[:, :], ALU.mult, sem="dve",
                   waits=[("dve", cf0)])
        cf2 = P.tsc(coef4[:, :], coef4[:, :], -1.0, 1.0, ALU.mult, ALU.add,
                    sem="dve", waits=[("dve", cf1)])
        cf3 = P.tsc(coef4[:, :], coef4[:, :], 0.0, 1.0, ALU.max, ALU.min,
                    sem="dve", waits=[("dve", cf2)])
        for tt in range(8):
            bk_ = tt % 2
            wl = [("dve", cf3)]
            if bk_ in bf_:
                wl.append(bf_.pop(bk_))
            tc = P.transpose(pb[bk_][0:128, 0:4], coef4[:, tt * 128:(tt + 1) * 128],
                             ident[0:4, 0:4], sem="pe", waits=wl)
            cc = P.copy(coeftb[:, tt * 4:(tt + 1) * 4], pb[bk_][0:128, 0:4],
                        sem="dve", waits=[("pe", tc)])
            bf_[bk_] = ("dve", cc)
        coeft_done = P.cnt["dve"]
        for fb in range(6):
            bk_ = 6 if fb % 2 == 0 else 7
            pbx = pb6 if fb % 2 == 0 else pb7
            wl = [("dve", coeft_done)]
            if ("d67", bk_) in bf_:
                wl.append(bf_.pop(("d67", bk_)))
            for tt in range(8):
                P.mm(pbx[:, 0:4], mknbuf[:, tt * 768 + fb * 128: tt * 768 + (fb + 1) * 128],
                     coeftb[:, tt * 4:(tt + 1) * 4], start=(tt == 0), stop=(tt == 7),
                     waits=wl if tt == 0 else (),
                     sem="pe" if tt == 7 else None)
            pcd = P.cnt["pe"]
            dnw = [("pe", pcd)]
            if ("dnst",) in bf_ and fb == 0:
                dnw.append(bf_.pop(("dnst",)))
            dn = P.copy(convsb[:, fb * 4:(fb + 1) * 4], pbx[:, 0:4], sem="dve",
                        waits=dnw)
            bf_[("d67", bk_)] = ("dve", dn)
        dnst = P.dma("sync",
                     aps["dnorm_out"][:, hk * 4:(hk + 1) * 4].rearrange(
                         "(k p) g -> p k g", p=128),
                     convsb[:, 0:24].rearrange("p (k g) -> p k g", k=6), "st",
                     waits=[("dve", P.cnt["dve"])])
        bf_[("dnst",)] = dnst

        for hq in range(4):
            h = hk * 4 + hq
            mslot = h % 2
            wl = [bf_.pop(("memd", mslot))] if ("memd", mslot) in bf_ else []
            lmem = P.dma("gpsimd", memb[:, mslot * 768:(mslot + 1) * 768]
                         .rearrange("p (k c) -> p k c", k=6),
                         aps["memC"][h].rearrange("(k p) c -> p k c", p=128),
                         "w", waits=wl)
            wl = [bf_.pop(("mbw", mslot))] if ("mbw", mslot) in bf_ else []
            lmb = P.dma("gpsimd", w3b[:, mslot * 2048: mslot * 2048 + 2048]
                        .rearrange("p (k c) -> p k c", k=16),
                        aps["wmbT"][:, h * 128:(h + 1) * 128].rearrange(
                            "(k p) c -> p k c", p=128), "w", waits=wl)
            for half in range(2):
                bk_ = 2 + half
                wl = [lmem, mkf_done]
                if bk_ in bf_:
                    wl.append(bf_.pop(bk_))
                for fb in range(6):
                    P.mm(pb[bk_][:, :], memb[:, mslot * 768 + fb * 128: mslot * 768 + (fb + 1) * 128],
                         mkf_tiles[fb][:, half * 512:(half + 1) * 512],
                         start=(fb == 0), stop=(fb == 5),
                         waits=wl if fb == 0 else ())
                bk2 = 4 + half
                wl2 = [lmb]
                if bk2 in bf_:
                    wl2.append(bf_.pop(bk2))
                for k in range(DK):
                    P.mm(pb[bk2][:, :], w3b[:, mslot * 2048 + k * 128: mslot * 2048 + (k + 1) * 128],
                         xnb[:, k * T + half * 512: k * T + (half + 1) * 512],
                         start=(k == 0), stop=(k == DK - 1),
                         waits=wl2 if k == 0 else (),
                         sem="pe" if k == DK - 1 else None)
                pcb = P.cnt["pe"]
                bk3 = 0 + half
                wl3 = [("dve", rc16)]
                if bk3 in bf_:
                    wl3.append(bf_.pop(bk3))
                sb_ = P.mm(pb[bk3][:, :], env["sel16"][0:4, hq * 128:(hq + 1) * 128],
                           rec16[:, half * 512:(half + 1) * 512], start=True,
                           stop=True, sem="pe", waits=wl3)
                rb_ = P.copy(sttl[:, half * 512:(half + 1) * 512], pb[bk3][:, :],
                             sem="dve", waits=[("pe", sb_), ("dve", P.cnt["dve"])])
                bf_[bk3] = ("dve", rb_)
                pvw = [("dve", rb_)]
                if ("tmpd",) in bf_:
                    pvw.append(bf_.pop(("tmpd",)))
                pv = P.tt(tmp[:, half * 512:(half + 1) * 512], pb[bk_][:, :],
                          sttl[:, half * 512:(half + 1) * 512], ALU.mult,
                          sem="dve", waits=pvw)
                bf_[bk_] = ("dve", pv)
                mv = P.tt(tmp[:, half * 512:(half + 1) * 512],
                          vbuf[:, hk * T + half * 512: hk * T + (half + 1) * 512],
                          tmp[:, half * 512:(half + 1) * 512], ALU.subtract,
                          sem="dve", waits=[("dve", pv)])
                am = P.act(sil[:, half * 512:(half + 1) * 512], pb[bk2][:, :],
                           AF.Sigmoid, bias=biasb[:, 8 + h:9 + h], sem="act",
                           waits=[("pe", pcb), ("dve", P.cnt["dve"]), lbmb])
                bf_[bk2] = ("act", am)
                wv_ = P.tt(ftl[:, T + half * 512:T + (half + 1) * 512],
                           tmp[:, half * 512:(half + 1) * 512],
                           sil[:, half * 512:(half + 1) * 512], ALU.mult,
                           sem="dve", waits=[("act", am), ("dve", mv)])
            wmv_done = P.cnt["dve"]
            for tt in range(8):
                bk_ = tt % 2
                wl = [("dve", wmv_done)]
                if bk_ in bf_:
                    wl.append(bf_.pop(bk_))
                tc = P.transpose(pb[bk_][:, 0:128],
                                 ftl[:, T + tt * 128:T + (tt + 1) * 128],
                                 ident[:, :], sem="pe", waits=wl)
                cc = P.copy(wmvt[:, tt * 128:(tt + 1) * 128], pb[bk_][:, 0:128],
                            sem="dve", waits=[("pe", tc)])
                bf_[bk_] = ("dve", cc)
            wmvt_done = P.cnt["dve"]
            for fb in range(6):
                bk_ = 2 + fb % 2
                wl = [("dve", wmvt_done)]
                if bk_ in bf_:
                    wl.append(bf_.pop(bk_))
                for tt in range(8):
                    P.mm(pb[bk_][:, 0:128],
                         mknbuf[:, tt * 768 + fb * 128: tt * 768 + (fb + 1) * 128],
                         wmvt[:, tt * 128:(tt + 1) * 128],
                         start=(tt == 0), stop=(tt == 7),
                         waits=wl if tt == 0 else (),
                         sem="pe" if tt == 7 else None)
                pcd = P.cnt["pe"]
                dc_ = P.copy(sil[:, fb * 128:(fb + 1) * 128] if False else
                             tmp[:, fb * 128:(fb + 1) * 128],
                             pb[bk_][:, 0:128], sem="dve", waits=[("pe", pcd)])
                bf_[bk_] = ("dve", dc_)
            dstore = P.dma("sync", aps["dmem_out"][h].rearrange("(k p) c -> p k c", p=128),
                           tmp[:, 0:768].rearrange("p (k c) -> p k c", k=6), "st",
                           waits=[("dve", P.cnt["dve"])])
            bf_[("tmpd",)] = dstore
            bf_[("memd", mslot)] = ("pe", P.cnt["pe"])
            bf_[("mbw", mslot)] = ("pe", P.cnt["pe"])


    for sft in (0, 1):
        if ("mbw", sft) in bf_:
            bf_[("mqw", sft)] = bf_.pop(("mbw", sft))
    for h in range(16):
        mslot = h % 2
        wl = [bf_.pop(("mqw", mslot))] if ("mqw", mslot) in bf_ else []
        lq = P.dma("gpsimd", w3b[:, mslot * 2048: mslot * 2048 + 2048]
                   .rearrange("p (k c) -> p k c", k=16),
                   aps["wqT2"][:, h * 128:(h + 1) * 128].rearrange(
                       "(k p) c -> p k c", p=128), "w", waits=wl)
        c0t = w2b[:, 0:T]; c1t = w2b[:, T:2 * T]
        for half in range(2):
            bk_ = 0 + half
            wl = [lq]
            if bk_ in bf_:
                wl.append(bf_.pop(bk_))
            for k in range(DK):
                P.mm(pb[bk_][:, :], w3b[:, mslot * 2048 + k * 128: mslot * 2048 + (k + 1) * 128],
                     xnb[:, k * T + half * 512: k * T + (half + 1) * 512],
                     start=(k == 0), stop=(k == DK - 1),
                     waits=wl if k == 0 else (),
                     sem="pe" if k == DK - 1 else None)
            pcq = P.cnt["pe"]
            aw = [("pe", pcq), ("dve", P.cnt["dve"]), ("act", P.cnt["act"])]
            a1 = P.act(c0t[:, half * 512:(half + 1) * 512], pb[bk_][:, :],
                       AF.Relu, bias=biasb[:, 24 + h:25 + h], sem="act", waits=aw)
            a2 = P.act(c1t[:, half * 512:(half + 1) * 512], pb[bk_][:, :],
                       AF.Relu, bias=biasb[:, 56 + h:57 + h], scale=-1.0,
                       sem="act", waits=[("act", a1)])
            bf_[bk_] = ("act", a2)
        relu_done = P.cnt["act"]
        for j in (1, 2, 3):
            for half_t in range(2):
                dst = mkf_tiles[(j - 1) * 2 + half_t]
                bk_ = 2 + (j + half_t) % 2
                src_main = c0t if half_t == 0 else c1t
                src_wrap = c1t if half_t == 0 else c0t
                wl = [("act", relu_done), ("dve", P.cnt["dve"])]
                if h == 0:
                    wl.extend(lro_toks)
                if ("mkf_war",) in bf_:
                    wl.extend(bf_.pop(("mkf_war",)))
                if bk_ in bf_:
                    wl.append(bf_.pop(bk_))
                tok = None
                for hh in range(2):
                    wlh = wl if hh == 0 else [("dve", tok)]
                    P.mm(pb[bk_][:, 0:512], roll_slices[2 * (j - 1)],
                         src_main[:, hh * 512:(hh + 1) * 512], start=True,
                         stop=False, waits=wlh)
                    P.mm(pb[bk_][:, 0:512], roll_slices[2 * (j - 1) + 1],
                         src_wrap[:, hh * 512:(hh + 1) * 512], start=False,
                         stop=True, sem="pe")
                    pc = P.cnt["pe"]
                    x_rep = c0t if half_t == 0 else c1t
                    tok = P.tt(dst[:, hh * 512:(hh + 1) * 512],
                               x_rep[:, hh * 512:(hh + 1) * 512],
                               pb[bk_][:, 0:512], ALU.mult, sem="dve",
                               waits=[("pe", pc)])
                bf_[bk_] = ("dve", tok)
        mq_done = P.cnt["dve"]
        for ti in range(6):
            sq_ = P.tt(sil[:, 0:T], mkf_tiles[ti][:, :], mkf_tiles[ti][:, :],
                       ALU.mult, sem="dve",
                       waits=[("dve", mq_done), ("dve", P.cnt["dve"]),
                              ("pe", P.cnt["pe"])])
            for half in range(2):
                bk_ = 4 + half
                wl = [("dve", sq_)]
                if ti == 0 and bk_ in bf_:
                    wl.append(bf_.pop(bk_))
                P.mm(pb[bk_][0:1, :], onesc[:, :],
                     sil[:, half * 512:(half + 1) * 512],
                     start=(ti == 0), stop=(ti == 5),
                     waits=wl, sem="pe" if half == 1 else None)
        pcs = P.cnt["pe"]
        sqw1 = [("pe", pcs)]
        if ("sqst",) in bf_:
            sqw1.append(bf_.pop(("sqst",)))
        sq1 = P.act(sqrow[:, 0:512], pb[4][0:1, :], AF.Copy, sem="act",
                    waits=sqw1)
        sq2 = P.act(sqrow[:, 512:T], pb[5][0:1, :], AF.Copy, sem="act",
                    waits=[("pe", pcs)])
        bf_[4] = ("act", sq2)
        bf_[5] = ("act", sq2)
        sst = P.dma("sync", aps["sqmq_out"][0:1, h * T:(h + 1) * T], sqrow[:, :],
                    "st", waits=[("act", sq2)])
        bf_[("sqst",)] = sst
        mqs = []
        for ti in range(6):
            mqs.append(P.dma("sync", aps["mq_out"][h, ti], mkf_tiles[ti][:, :],
                             "st", waits=[("dve", mq_done)]))
        bf_[("mqw", mslot)] = ("pe", P.cnt["pe"])
        bf_[("mkf_war",)] = mqs

    gstok = {}
    def g_evac(mt, half, bank, pecnt):
        aw = [("pe", pecnt), ("act", P.cnt["act"])]
        if half in gstok:
            aw.append(gstok.pop(half))
        ag = P.act(prodb[:, half * 256:(half + 1) * 256][:, 0:0] if False else
                   prodb[:, 0:512] if half == 0 else qkb[:, 1024:1536],
                   bank[:, :], AF.Sigmoid,
                   bias=biasb[:, 40 + mt:41 + mt], sem="act", waits=aw)
        src = prodb[:, 0:512] if half == 0 else qkb[:, 1024:1536]
        gs = P.dma("sync", aps["gate_out"][mt, :, half * 512:(half + 1) * 512],
                   src, "st", waits=[("act", ag)])
        gstok[half] = gs
        return gs
    slotuse[("wgT2", 0)] = P.cnt["pe"]
    slotuse[("wgT2", 1)] = P.cnt["pe"]
    emit_projA(P, env, w_ap=aps["wgT2"], n_m=DK, evac_cb=g_evac, tagkey="wgT2",
               chunk=1, banks=(0, 1), rhs_ready=None, wbuf=w2b, wslot=2048)

    xTb = env["acc"]
    for dk in range(DK):
        P.tt(xTb[:, dk * T:(dk + 1) * T], xTb[:, dk * T:(dk + 1) * T],
             xnb[:, dk * T:(dk + 1) * T], ALU.add, sem="dve",
             waits=[("dve", P.cnt["dve"]), ("pe", P.cnt["pe"])])
    xp = P.cnt["dve"]
    P.dma("sync", aps["xpart_out"].rearrange("k p c -> p k c"),
          xTb[:, :].rearrange("p (k c) -> p k c", k=DK), "st",
          waits=[("dve", xp)])


def emit_launchC(P, aps):
    nc = P.nc
    EPS = 1e-8
    xT = nc.alloc_sbuf_tensor("xTc", [128, DK * T], F32)
    memnew = nc.alloc_sbuf_tensor("memnew", [128, 2 * 768], BF16)
    dsum = nc.alloc_sbuf_tensor("dsum", [128, 768], F32)
    dload = nc.alloc_sbuf_tensor("dload", [128, 2 * 768], F32)
    nnorm = nc.alloc_sbuf_tensor("nnorm", [128, 96], F32)
    nnorm16 = nc.alloc_sbuf_tensor("nnorm16", [128, 96], BF16)
    mqb = nc.alloc_sbuf_tensor("mqb", [128, 2 * 6 * T], BF16)
    gtb = nc.alloc_sbuf_tensor("gtb", [128, 2 * T], BF16)
    sqr = nc.alloc_sbuf_tensor("sqr", [1, T], F32)
    drow = nc.alloc_sbuf_tensor("drow", [1, T], F32)
    rrow16 = nc.alloc_sbuf_tensor("rrow16", [1, T], BF16)
    rcpB = nc.alloc_sbuf_tensor("rcpB", [128, T], F32)
    memh = nc.alloc_sbuf_tensor("memh", [128, T], F32)
    onat = nc.alloc_sbuf_tensor("onat", [128, D], F32)
    identnp = nc.inline_tensor(np.eye(128, dtype=np.float32), name="identC")
    ident = nc.alloc_sbuf_tensor("identC_sb", [128, 128], F32)
    onesfnp = nc.inline_tensor(np.ones((1, 128), np.float32), name="onesfC")
    onesf = nc.alloc_sbuf_tensor("onesfC_sb", [1, 128], F32)
    onesbnp = nc.inline_tensor(np.ones((1, 128), np.float32).astype(ml_dtypes.bfloat16),
                               name="onesbC")
    onesb = nc.alloc_sbuf_tensor("onesbC_sb", [1, 128], BF16)
    pb = [nc.alloc_psum_tensor(f"pc{i}", [128, 512], F32) for i in range(8)]
    bf_ = {}

    lx = P.dma("sync", xT[:, :].rearrange("p (k c) -> p k c", k=DK),
               aps["xpart_in"].rearrange("k p c -> p k c"), "hx")
    lid = P.dma("sync", ident[:, :], identnp.ap(), "hx")
    lof = P.dma("sync", onesf[:, :], onesfnp.ap(), "hx")
    lob = P.dma("gpsimd", onesb[:, :], onesbnp.ap(), "x")

    t1 = None
    for c in range(8):
        sl = c % 2
        ln_ = P.dma("sync", nnorm16[:, :].rearrange("p (k g) -> p k g", k=6) if False
                    else dload[:, sl * 768:sl * 768 + 96].rearrange(
                        "p (k g) -> p k g", k=6),
                    aps["dnparts"][c].rearrange("(k p) g -> p k g", p=128), "hx",
                    waits=[bf_.pop(("dn", sl))] if ("dn", sl) in bf_ else [])
        if c == 0:
            t1 = P.copy(nnorm[:, :], dload[:, 0:96], sem="dve", waits=[ln_])
        else:
            t1 = P.tt(nnorm[:, :], nnorm[:, :], dload[:, sl * 768:sl * 768 + 96],
                      ALU.add, sem="dve", waits=[ln_, ("dve", t1)])
        bf_[("dn", sl)] = ("dve", t1)
    lnt = P.dma("gpsimd", dload[:, 768:768 + 96].rearrange("p (k g) -> p k g", k=6),
                aps["normT_in"].rearrange("(k p) g -> p k g", p=128), "w",
                waits=[bf_.pop(("dn", 1))])
    t2 = P.tt(nnorm[:, :], nnorm[:, :], dload[:, 768:768 + 96], ALU.add,
              sem="dve", waits=[lnt, ("dve", t1)])
    t3 = P.copy(nnorm16[:, :], nnorm[:, :], sem="dve", waits=[("dve", t2)])
    nn_done = ("dve", t3)
    bf_.pop(("dn", 0), None)

    sqld = P.dma("sync", drow[0:1, :], aps["sqmq_in"][0:1, 0:T], "hx") if False else None
    hstore = []
    for h in range(16):
        sl = h % 2
        tok = None
        for c in range(8):
            wl = [bf_.pop(("dl", sl, c % 2))] if ("dl", sl, c % 2) in bf_ else []
            ld = P.dma("sync", dload[:, (c % 2) * 768:(c % 2) * 768 + 768]
                       .rearrange("p (k c) -> p k c", k=6),
                       aps["dparts"][c, h].rearrange("(k p) c -> p k c", p=128),
                       "hx", waits=wl)
            if c == 0:
                tok = P.copy(dsum[:, :], dload[:, 0:768], sem="dve",
                             waits=[ld, ("dve", P.cnt["dve"])])
            else:
                tok = P.tt(dsum[:, :], dsum[:, :],
                           dload[:, (c % 2) * 768:(c % 2) * 768 + 768], ALU.add,
                           sem="dve", waits=[ld, ("dve", tok)])
            bf_[("dl", sl, c % 2)] = ("dve", tok)
        lm = P.dma("gpsimd", memnew[:, sl * 768:(sl + 1) * 768]
                   .rearrange("p (k c) -> p k c", k=6),
                   aps["memC"][h].rearrange("(k p) c -> p k c", p=128), "w",
                   waits=[bf_.pop(("mem", sl))] if ("mem", sl) in bf_ else [])
        lmf = P.dma("sync", dload[:, (0) * 768:768].rearrange("p (k c) -> p k c", k=6),
                    aps["memC"][h].rearrange("(k p) c -> p k c", p=128), "hx",
                    waits=[("dve", tok)])
        a1 = P.tt(dsum[:, :], dsum[:, :], dload[:, 0:768], ALU.add, sem="dve",
                  waits=[lmf, ("dve", tok)])
        a2 = P.copy(memnew[:, sl * 768:(sl + 1) * 768], dsum[:, :], sem="dve",
                    waits=[("dve", a1), lm])
        bf_[("mem", sl)] = ("dve", a2)
        bf_[("dl", sl, 0)] = ("dve", a1)
        lq = P.dma("sync", mqb[:, sl * 6 * T:(sl + 1) * 6 * T]
                   .rearrange("p (k c) -> p k c", k=6),
                   aps["mq_in"][h].rearrange("k p c -> p k c"), "hx",
                   waits=[bf_.pop(("mq", sl))] if ("mq", sl) in bf_ else [])
        lg = P.dma("sync", gtb[:, sl * T:(sl + 1) * T], aps["gate_in"][h], "hx",
                   waits=[bf_.pop(("gt", sl))] if ("gt", sl) in bf_ else [])
        for half in range(2):
            bk = half
            wl = [lq, ("dve", a2)]
            if bk in bf_:
                wl.append(bf_.pop(bk))
            for fb in range(6):
                P.mm(pb[bk][:, :], memnew[:, sl * 768 + fb * 128: sl * 768 + (fb + 1) * 128],
                     mqb[:, sl * 6 * T + fb * T + half * 512: sl * 6 * T + fb * T + (half + 1) * 512],
                     start=(fb == 0), stop=(fb == 5),
                     waits=wl if fb == 0 else ())
            bk2 = 2 if half == 0 else 7
            wl2 = [nn_done, lq]
            if bk2 in bf_:
                wl2.append(bf_.pop(bk2))
            for fb in range(6):
                P.mm(pb[bk2][0:1, 0:512],
                     nnorm16[:, fb * 16 + h:fb * 16 + h + 1],
                     mqb[:, sl * 6 * T + fb * T + half * 512: sl * 6 * T + fb * T + (half + 1) * 512],
                     start=(fb == 0), stop=(fb == 5),
                     waits=wl2 if fb == 0 else (),
                     sem="pe" if fb == 5 else None)
        pcd = P.cnt["pe"]
        sq1 = P.act(sqr[0:1, :], aps_sq_row(P, aps, h), AF.Ln, sem="act",
                    waits=[("pe", pcd)]) if False else None
        lsq = P.dma("sync", sqr[0:1, :], aps["sqmq_in"][0:1, h * T:(h + 1) * T],
                    "hx", waits=[bf_.pop(("sq",))] if ("sq",) in bf_ else [])
        s1 = P.act(drow[0:1, :], sqr[0:1, :], AF.Ln, sem="act",
                   waits=[lsq, ("act", P.cnt.get("act", 0)),
                          ("dve", P.cnt.get("dve", 0))])
        s2 = P.act(drow[0:1, :], drow[0:1, :], AF.Exp, scale=0.5, sem="act",
                   waits=[("act", s1)])
        bf_[("sq",)] = ("act", s2)
        d1 = P.stt(drow[0:1, :], drow[0:1, :], EPS, pb[2][0:1, 0:T], ALU.mult,
                   ALU.add, sem="dve", waits=[("act", s2), ("pe", pcd)]) if False else None
        dtoks = []
        for half in range(2):
            bk2 = 2 if half == 0 else 7
            dd = P.stt(drow[0:1, half * 512:(half + 1) * 512],
                       drow[0:1, half * 512:(half + 1) * 512], EPS,
                       pb[bk2][0:1, 0:512], ALU.mult, ALU.add,
                       sem="dve", waits=[("act", s2), ("pe", pcd),
                                         ("dve", P.cnt["dve"])])
            dtoks.append(dd)
            bf_[bk2] = ("dve", dd)
        r1 = P.recip(drow[0:1, :], drow[0:1, :], sem="dve", waits=[("dve", dtoks[-1])])
        r2 = P.copy(rrow16[0:1, :], drow[0:1, :], sem="dve", waits=[("dve", r1)])
        for half in range(2):
            bk3 = 3 + half % 2
            wl3 = [("dve", r2), lob]
            if bk3 in bf_:
                wl3.append(bf_.pop(bk3))
            bm = P.mm(pb[bk3][:, :], onesb[0:1, :],
                      rrow16[0:1, half * 512:(half + 1) * 512], start=True,
                      stop=True, sem="pe", waits=wl3)
            bc = P.copy(rcpB[:, half * 512:(half + 1) * 512], pb[bk3][:, :],
                        sem="dve", waits=[("pe", bm)])
            bf_[bk3] = ("dve", bc)
            m1 = P.tt(memh[:, half * 512:(half + 1) * 512], pb[half][:, :],
                      rcpB[:, half * 512:(half + 1) * 512], ALU.mult, sem="dve",
                      waits=[("dve", bc)])
            bf_[half] = ("dve", m1)
            m2 = P.tt(memh[:, half * 512:(half + 1) * 512],
                      memh[:, half * 512:(half + 1) * 512],
                      gtb[:, sl * T + half * 512: sl * T + (half + 1) * 512],
                      ALU.mult, sem="dve", waits=[("dve", m1), lg])
            m3 = P.tt(xT[:, h * T + half * 512: h * T + (half + 1) * 512],
                      xT[:, h * T + half * 512: h * T + (half + 1) * 512],
                      memh[:, half * 512:(half + 1) * 512], ALU.add, sem="dve",
                      waits=[("dve", m2), lx])
            hstore.append(m3)
        bf_[("mq", sl)] = ("pe", P.cnt["pe"])
        bf_[("gt", sl)] = ("dve", P.cnt["dve"])
    xdone = P.cnt["dve"]

    for tt in range(8):
        b, sr = tt // 4, (tt % 4) * 128
        for dk in range(DK):
            bk = 5 + (dk % 2)
            wl = [("dve", xdone), lid]
            if bk in bf_:
                wl.append(bf_.pop(bk))
            tc = P.transpose(pb[bk][:, 0:128], xT[:, dk * T + tt * 128: dk * T + (tt + 1) * 128],
                             ident[:, :], sem="pe", waits=wl)
            cc = P.copy(onat[:, dk * 128:(dk + 1) * 128], pb[bk][:, 0:128],
                        sem="dve", waits=[("pe", tc),
                                          bf_.pop(("on", dk % 2)) if ("on", dk % 2) in bf_ else ("dve", 0)])
            bf_[bk] = ("dve", cc)
        oc = P.cnt["dve"]
        st = P.dma("sync", aps["y_out"][b, sr:sr + 128, :], onat[:, :], "st",
                   waits=[("dve", oc)])
        bf_[("on", 0)] = st
        bf_[("on", 1)] = st
    P.wait_stream("sync", "st")
    P.wait_stream("gpsimd", "w")


from concourse.bass_utils import run_bass_kernel_spmd

NC = 8
SC = 512


def _consts():
    c = {}
    inv = 1.0 / 10000.0 ** (np.arange(0, 64, 2, np.float32) / 64)
    cs, sn = np.cos(inv), np.sin(inv)
    coscol = np.zeros(128, np.float32)
    sgnsin = np.zeros(128, np.float32)
    for p in range(128):
        i = p % 64
        j = i if i < 32 else i - 32
        coscol[p] = cs[j]
        sgnsin[p] = -sn[j] if i < 32 else sn[j]
    c["rotcs"] = np.ascontiguousarray(np.stack([coscol, sgnsin], 1))
    prot = np.zeros((128, 128), np.float32)
    for i in range(128):
        ii = i % 64
        prot[i, (i + 32) if ii < 32 else (i - 32)] = 1.0
    c["protc"] = prot.astype(ml_dtypes.bfloat16)
    bd = np.zeros((128, 8 * 16), np.float32)
    for tt in range(8):
        bd[0:64, tt * 16 + 2 * tt] = 1.0
        bd[64:128, tt * 16 + 2 * tt + 1] = 1.0
    c["bd64c"] = bd.astype(ml_dtypes.bfloat16)
    rollc = np.zeros((128, 6 * 128), np.float32)
    for j in (1, 2, 3):
        S = np.zeros((128, 128), np.float32)
        Tm = np.zeros((128, 128), np.float32)
        for i in range(128):
            if i >= j:
                S[i, i - j] = 1.0
            else:
                Tm[i, 128 - j + i] = 1.0
        rollc[:, (2 * (j - 1)) * 128:(2 * (j - 1) + 1) * 128] = S.T
        rollc[:, (2 * (j - 1) + 1) * 128:(2 * (j - 1) + 2) * 128] = Tm.T
    c["rollc"] = rollc.astype(ml_dtypes.bfloat16)
    return c


def _build_A():
    P = Prog()
    nc = P.nc
    aps = {}
    def di(name, shape, dt=F32):
        aps[name] = nc.dram_tensor(name, shape, dt, kind="ExternalInput").ap()
    def do(name, shape, dt=F32):
        aps[name] = nc.dram_tensor(name, shape, dt, kind="ExternalOutput").ap()
    di("xs", [2, SC, D]); di("wf", [D, D]); di("wi", [D, D]); di("wv", [D, D])
    di("wvt", [D, D]); di("rwA_in", [128, 32]); di("waux", [1, D])
    do("xt_out", [DK, 128, T]); do("xn_out", [DK, 128, T], BF16)
    do("f_out", [DK, 128, T]); do("g_out", [DK, 128, T], BF16)
    do("ap_out", [128, 64]); do("wvr_out", [128, 1024], BF16)
    emit_launchA(P, xs_ap=aps["xs"], wf_ap=aps["wf"], wi_ap=aps["wi"],
                 wv_ap=aps["wv"], xt_out=aps["xt_out"], xn_out=aps["xn_out"],
                 f_out=aps["f_out"], g_out=aps["g_out"], ap_out=aps["ap_out"],
                 wvt_ap=aps["wvt"], rw_ap=aps["rwA_in"], waux_ap=aps["waux"],
                 wvr_out=aps["wvr_out"])
    return P.finalize()


def _build_B():
    P = Prog()
    nc = P.nc
    aps = {}
    def di(name, shape, dt=F32):
        aps[name] = nc.dram_tensor(name, shape, dt, kind="ExternalInput").ap()
    def do(name, shape, dt=F32):
        aps[name] = nc.dram_tensor(name, shape, dt, kind="ExternalOutput").ap()
    di("xt_in", [DK, 128, T]); di("xn_in", [DK, 128, T], BF16)
    di("f_in", [DK, 128, T]); di("g_in", [DK, 128, T], BF16)
    di("apall", [128, 8, 64]); di("initsb_in", [128, 16]); di("selvec", [1, 8])
    for w in ("wq", "wro", "wup", "wgate", "wdown"):
        di(w, [D, D])
    di("wgmoe", [D, 16])
    di("w1t", [16, D, 1408]); di("w3t", [16, D, 1408]); di("w2t", [16, 1408, D])
    di("ws1t", [D, 2816]); di("ws3t", [D, 2816]); di("ws2t", [2816, D])
    di("convw", [128, 32]); di("rotcs", [128, 2])
    di("wqf", [D, 1024]); di("wkf", [D, 1024]); di("wvf", [D, D]); di("wgf", [D, D])
    di("wvr_in", [128, 1024], BF16); di("woutf", [D, D])
    di("protc", [128, 128], BF16); di("bd64c", [128, 8 * 16], BF16)
    di("wkT", [D, 512]); di("wvT", [D, 512]); di("wmbT", [D, D])
    di("wqT2", [D, D]); di("wgT2", [D, D])
    di("bk_in", [128, 4]); di("bv_in", [128, 4]); di("bmb_in", [128, 16])
    di("bq_in", [128, 16]); di("bg_in", [128, 16]); di("bkrow", [1, 512])
    di("memC", [16, 768, 128]); di("normT_in", [768, 16])
    di("rollc", [128, 6 * 128], BF16)
    do("dmem_out", [16, 768, 128]); do("dnorm_out", [768, 16])
    do("mq_out", [16, 6, 128, T], BF16); do("sqmq_out", [1, 16 * T])
    do("gate_out", [DK, 128, T], BF16); do("xpart_out", [DK, 128, T])

    env = emit_launchB0(P, aps)
    env["x1done"] = ("dve", P.cnt["dve"])
    env["h_guard2"] = ("pe", P.cnt["pe"])
    emit_LN1(P, env)
    emit_l1ffn(P, env, aps)
    emit_l1attn(P, env, aps)
    P.wait("sync", "dve", P.cnt["dve"])
    P.wait("sync", "pe", P.cnt["pe"])
    P.wait("sync", "act", P.cnt["act"])
    P.wait_stream("sync", "st")
    P.wait_stream("sync", "w")
    P.wait_stream("sync", "x")
    P.wait_stream("sync", "hx")
    return P.finalize()


def _build_C():
    P = Prog()
    nc = P.nc
    aps = {}
    def di(name, shape, dt=F32):
        aps[name] = nc.dram_tensor(name, shape, dt, kind="ExternalInput").ap()
    di("xpart_in", [DK, 128, T]); di("mq_in", [16, 6, 128, T], BF16)
    di("sqmq_in", [1, 16 * T]); di("gate_in", [DK, 128, T], BF16)
    di("dparts", [8, 16, 768, 128]); di("dnparts", [8, 768, 16])
    di("memC", [16, 768, 128]); di("normT_in", [768, 16])
    aps["y_out"] = nc.dram_tensor("y_out", [2, SC, D], F32,
                                  kind="ExternalOutput").ap()
    emit_launchC(P, aps)
    return P.finalize()


def kernel(**inp):
    co = _consts()
    ac = np.ascontiguousarray
    x = inp["L0_attn_x"]

    ncA = _build_A()
    wvt_h = ac(inp["L1_ffn_Wv"].T)
    rw_h = ac(inp["L1_ffn_router_w"].transpose(1, 0, 2).reshape(128, 32))
    waux_h = ac(inp["L0_attn_w_aux"][:1])
    mapsA = []
    for c in range(NC):
        mapsA.append({
            "xs": ac(x[:, SC * c:SC * (c + 1), :]),
            "wf": inp["L0_attn_w_f"], "wi": inp["L0_attn_w_i"],
            "wv": inp["L0_attn_w_v"],
            "wvt": wvt_h, "rwA_in": rw_h, "waux": waux_h,
        })
    rA = run_bass_kernel_spmd(ncA, mapsA, core_ids=list(range(NC)))
    oA = rA.results

    ncB = _build_B()
    apall = ac(np.stack([oA[c]["ap_out"] for c in range(NC)], axis=1))
    initsb = ac(inp["L0_attn_init_state"].reshape(16, 128).T)
    shared = {
        "apall": apall, "initsb_in": initsb,
        "wq": inp["L0_attn_w_q"], "wro": inp["L0_attn_w_ro"],
        "wup": inp["L0_attn_w_up"], "wgate": inp["L0_attn_w_gate"],
        "wdown": inp["L0_attn_w_down"],
        "wgmoe": ac(inp["L0_ffn_Wg"].T),
        "w1t": ac(inp["L0_ffn_W1"].transpose(0, 2, 1)),
        "w3t": ac(inp["L0_ffn_W3"].transpose(0, 2, 1)),
        "w2t": ac(inp["L0_ffn_W2"].transpose(0, 2, 1)),
        "ws1t": ac(inp["L0_ffn_Ws1"].T), "ws3t": ac(inp["L0_ffn_Ws3"].T),
        "ws2t": ac(inp["L0_ffn_Ws2"].T),
        "convw": ac(inp["L1_ffn_conv_w"].reshape(DK, 128, 2)
                    .transpose(1, 0, 2).reshape(128, 32)),
        "rotcs": co["rotcs"], "protc": co["protc"], "bd64c": co["bd64c"],
        "wqf": inp["L1_ffn_Wq"], "wkf": inp["L1_ffn_Wk"],
        "wvf": inp["L1_ffn_Wv"], "wgf": inp["L1_ffn_Wg"],
        "woutf": inp["L1_ffn_Wout"],
        "wkT": ac(inp["L1_attn_Wk"].T), "wvT": ac(inp["L1_attn_Wv"].T),
        "wmbT": ac(inp["L1_attn_Wmb"].T), "wqT2": ac(inp["L1_attn_Wq"].T),
        "wgT2": ac(inp["L1_attn_Wg"].T),
        "bk_in": ac(inp["L1_attn_bk"].reshape(4, 128).T),
        "bv_in": ac(inp["L1_attn_bv"].reshape(4, 128).T),
        "bmb_in": ac(inp["L1_attn_bmb"].reshape(16, 128).T),
        "bq_in": ac(inp["L1_attn_bq"].reshape(16, 128).T),
        "bg_in": ac(inp["L1_attn_bg"].reshape(16, 128).T),
        "bkrow": ac(inp["L1_attn_bk"][None, :]),
        "memC": inp["L1_attn_memory"],
        "normT_in": ac(inp["L1_attn_norm"].T),
        "rollc": co["rollc"],
    }
    mapsB = []
    for c in range(NC):
        sv = np.zeros((1, 8), np.float32)
        sv[0, c] = 1.0
        m = dict(shared)
        m.update({
            "xt_in": oA[c]["xt_out"], "xn_in": oA[c]["xn_out"],
            "f_in": oA[c]["f_out"], "g_in": oA[c]["g_out"],
            "wvr_in": oA[c]["wvr_out"], "selvec": sv,
        })
        mapsB.append(m)
    rB = run_bass_kernel_spmd(ncB, mapsB, core_ids=list(range(NC)))
    oB = rB.results

    ncC = _build_C()
    dparts = ac(np.stack([oB[c]["dmem_out"] for c in range(NC)], axis=0))
    dnparts = ac(np.stack([oB[c]["dnorm_out"] for c in range(NC)], axis=0))
    sharedC = {"dparts": dparts, "dnparts": dnparts,
               "memC": inp["L1_attn_memory"],
               "normT_in": ac(inp["L1_attn_norm"].T)}
    mapsC = []
    for c in range(NC):
        m = dict(sharedC)
        m.update({"xpart_in": oB[c]["xpart_out"], "mq_in": oB[c]["mq_out"],
                  "sqmq_in": oB[c]["sqmq_out"], "gate_in": oB[c]["gate_out"]})
        mapsC.append(m)
    rC = run_bass_kernel_spmd(ncC, mapsC, core_ids=list(range(NC)))
    oC = rC.results

    y = np.empty((2, 4096, D), np.float32)
    for c in range(NC):
        y[:, SC * c:SC * (c + 1), :] = oC[c]["y_out"]
    return y


# version 04 (final)
# speedup vs baseline: 1.1035x; 1.0486x over previous
import numpy as np
import ml_dtypes
import concourse.bass as bass
import concourse.mybir as mybir

F32 = mybir.dt.float32
BF16 = mybir.dt.bfloat16
I32 = mybir.dt.int32
AF = mybir.ActivationFunctionType
ALU = mybir.AluOpType
AXL = mybir.AxisListType

ENGINES = ("sync", "scalar", "vector", "tensor", "gpsimd")


class Prog:
    def __init__(self):
        self.nc = bass.Bass("TRN2", target_bir_lowering=False)
        self.ops = {e: [] for e in ENGINES}
        self.sems = {}
        self.cnt = {}
        self._const_cols = {}
        self._finalized = False

    def sem(self, name):
        if name not in self.sems:
            self.sems[name] = self.nc.alloc_semaphore(name)
            self.cnt[name] = 0
        return self.sems[name]

    def bump(self, name, k):
        self.sem(name)
        self.cnt[name] += k
        return self.cnt[name]

    def _inc_of(self, sem, k):
        if sem is None:
            return None, None
        tgt = self.bump(sem, k)
        return self.sems[sem], tgt

    def raw(self, eng, fn):
        self.ops[eng].append(fn)

    def wait(self, eng, sem, val):
        if val is None or val <= 0:
            return
        s = self.sem(eng) if False else self.sem(sem)
        self.ops[eng].append(lambda q, _s=s, _v=val: q.wait_ge(_s, _v))

    def waits(self, eng, wl):
        for (sem, val) in wl:
            self.wait(eng, sem, val)

    NSLOT = 6

    def dma(self, eng, out, in_, stream, waits=()):
        key = ("dmaslot", stream)
        idx = self._dma_idx = getattr(self, "_dma_idx", {})
        i = idx.get(key, 0)
        idx[key] = i + 1
        sname = f"D{stream}{i % self.NSLOT}"
        self.sem(sname)
        prev = self.cnt[sname]
        wl = list(waits)
        if prev:
            wl.append((sname, prev))
        sh, tgt = self._inc_of(sname, 16)
        self.waits(eng, wl)
        self.ops[eng].append(
            lambda q, _o=out, _i=in_, _s=sh: q.dma_start(out=_o, in_=_i).then_inc(_s, 16))
        return (sname, tgt)

    def stream_tokens(self, stream):
        toks = []
        for s in range(self.NSLOT):
            n = f"D{stream}{s}"
            if self.cnt.get(n, 0):
                toks.append((n, self.cnt[n]))
        return toks

    def wait_stream(self, eng, stream):
        self.waits(eng, self.stream_tokens(stream))

    def mm(self, psum, lhsT, rhs, start, stop, sem=None, waits=()):
        self.waits("tensor", waits)
        sh, tgt = self._inc_of(sem, 1)
        if sh is None:
            self.ops["tensor"].append(
                lambda q, _p=psum, _l=lhsT, _r=rhs, _a=start, _b=stop:
                q.matmul(_p, _l, _r, start=_a, stop=_b))
        else:
            self.ops["tensor"].append(
                lambda q, _p=psum, _l=lhsT, _r=rhs, _a=start, _b=stop, _s=sh:
                q.matmul(_p, _l, _r, start=_a, stop=_b).then_inc(_s, 1))
        return tgt

    def transpose(self, psum, in_, ident, sem=None, waits=()):
        self.waits("tensor", waits)
        sh, tgt = self._inc_of(sem, 1)
        if sh is None:
            self.ops["tensor"].append(
                lambda q, _p=psum, _i=in_, _d=ident: q.transpose(_p, _i, _d))
        else:
            self.ops["tensor"].append(
                lambda q, _p=psum, _i=in_, _d=ident, _s=sh:
                q.transpose(_p, _i, _d).then_inc(_s, 1))
        return tgt

    def _eop(self, eng, build, sem=None, waits=(), inc=1):
        self.waits(eng, waits)
        sh, tgt = self._inc_of(sem, inc)
        if sh is None:
            self.ops[eng].append(build)
        else:
            self.ops[eng].append(lambda q, _b=build, _s=sh, _i=inc: _b(q).then_inc(_s, _i))
        return tgt

    def tt(self, out, a, b, alu, sem=None, waits=(), eng="vector"):
        return self._eop(eng, lambda q, _o=out, _a=a, _b=b, _u=alu:
                         q.tensor_tensor(_o, _a, _b, _u), sem, waits)

    def tsc(self, out, a, s1, s2, op0, op1=None, sem=None, waits=(), eng="vector"):
        if op1 is None:
            return self._eop(eng, lambda q, _o=out, _a=a, _s=s1, _u=op0:
                             q.tensor_scalar(_o, _a, _s, None, _u), sem, waits)
        return self._eop(eng, lambda q, _o=out, _a=a, _s=s1, _t=s2, _u=op0, _v=op1:
                         q.tensor_scalar(_o, _a, _s, _t, _u, _v), sem, waits)

    def stt(self, out, in0, scalar, in1, op0, op1, sem=None, waits=(), eng="vector"):
        return self._eop(eng, lambda q, _o=out, _a=in0, _s=scalar, _b=in1, _u=op0, _v=op1:
                         q.scalar_tensor_tensor(_o, _a, _s, _b, _u, _v), sem, waits)

    def copy(self, out, a, sem=None, waits=(), eng="vector"):
        return self._eop(eng, lambda q, _o=out, _a=a: q.tensor_copy(_o, _a), sem, waits)

    def memset(self, out, v, sem=None, waits=(), eng="vector"):
        return self._eop(eng, lambda q, _o=out, _v=v: q.memset(_o, _v), sem, waits)

    def reduce(self, out, a, alu, negate=False, sem=None, waits=(), eng="vector"):
        return self._eop(eng, lambda q, _o=out, _a=a, _u=alu, _n=negate:
                         q.tensor_reduce(_o, _a, AXL.X, _u, negate=_n), sem, waits)

    def recip(self, out, a, sem=None, waits=()):
        return self._eop("vector", lambda q, _o=out, _a=a: q.reciprocal(_o, _a), sem, waits)

    def select(self, out, mask, t_, f_, sem=None, waits=()):
        return self._eop("vector", lambda q, _o=out, _m=mask, _t=t_, _f=f_:
                         q.select(_o, _m, _t, _f), sem, waits)

    def scan(self, out, d0, d1, init, op0, op1, sem=None, waits=()):
        return self._eop("vector", lambda q, _o=out, _a=d0, _b=d1, _i=init, _u=op0, _v=op1:
                         q.tensor_tensor_scan(_o, _a, _b, _i, _u, _v), sem, waits)

    def act(self, out, a, fn, bias=0.0, scale=1.0, sem=None, waits=(), accum_out=None):
        if isinstance(bias, float):
            self.fconst(bias)
        if isinstance(scale, float):
            self.fconst(scale)
        if accum_out is None:
            return self._eop("scalar", lambda q, _o=out, _a=a, _f=fn, _b=bias, _s=scale:
                             q.activation(_o, _a, _f, bias=_b, scale=_s), sem, waits)
        return self._eop("scalar", lambda q, _o=out, _a=a, _f=fn, _b=bias, _s=scale, _x=accum_out:
                         q.activation(_o, _a, _f, bias=_b, scale=_s, accum_out=_x), sem, waits)

    def fconst(self, value):
        if value not in self._const_cols:
            assert not self._finalized
            self._const_cols[value] = len(self._const_cols)
        return value

    def init_consts(self, *values):
        for v in values:
            self.fconst(v)

    def finalize(self):
        assert not self._finalized
        self._finalized = True
        nc = self.nc
        pre = {e: [] for e in ENGINES}
        if self._const_cols:
            vals = sorted(self._const_cols, key=self._const_cols.get)
            arr = np.tile(np.array([vals], dtype=np.float32), (128, 1))
            th = nc.inline_tensor(arr.astype(np.float32), name="constvals")
            tile = nc.alloc_sbuf_tensor("constsb", [128, len(vals)], F32)
            for v in vals:
                c = self._const_cols[v]
                nc.const_aps.register(F32, float(v), tile[:, c:c + 1])
            cs = self.sem("cinit")
            pre["sync"].append(lambda q, _t=tile, _a=th.ap(), _s=cs:
                               q.dma_start(out=_t[:, :], in_=_a).then_inc(_s, 16))
            for e in ("scalar",):
                pre[e].append(lambda q, _s=cs: q.wait_ge(_s, 16))
        with nc.Block() as block:
            for eng in ENGINES:
                lst = pre[eng] + self.ops[eng]
                if lst:
                    getattr(block, eng)(lambda q, _l=tuple(lst): [f(q) for f in _l])
        return nc


def tsl(i, n=128):
    return slice(i * n, (i + 1) * n)


"""Section emitters for the megakernel (dev copies; final versions get pasted
into kernel.py)."""

T = 1024
D = 2048
DK = D // 128


def glu_env(P, *, icmax=2, w2slot=6144, hmax=16, acc=None, xnb=None):
    nc = P.nc
    env = {}
    env["xnb"] = xnb if xnb is not None else nc.alloc_sbuf_tensor("xnb", [128, DK * T], BF16)
    env["ICMAX"] = icmax
    env["W1SLOT"] = 16 * icmax * 128
    env["W2SLOT"] = w2slot
    env["w1b"] = nc.alloc_sbuf_tensor("w1b", [128, 2 * env["W1SLOT"]], BF16)
    env["w3b"] = nc.alloc_sbuf_tensor("w3b", [128, 2 * env["W1SLOT"]], BF16)
    env["w2b"] = nc.alloc_sbuf_tensor("w2b", [128, 2 * env["W2SLOT"]], BF16)
    env["h"] = nc.alloc_sbuf_tensor("hbuf", [128, hmax * T], BF16)
    env["sil"] = nc.alloc_sbuf_tensor("sil", [128, 2 * 512], F32)
    env["tmp"] = nc.alloc_sbuf_tensor("glutmp", [128, 2 * 512], F32)
    env["acc"] = acc if acc is not None else nc.alloc_sbuf_tensor("accb", [128, DK * T], F32)
    env["psum"] = [nc.alloc_psum_tensor(f"pb{i}", [128, 512], F32) for i in range(6)]
    env["slotuse"] = {}
    env["bankfree"] = {}
    env["silfree"] = {}
    return env


def emit_glu_ffn(P, env, *, w1_ap, w3_ap, w2_ap, n_i, c_col=None, tag,
                 i_chunk=2, d_chunk=4, first_into_acc=False):
    xnb = env["xnb"]; w1b = env["w1b"]; w3b = env["w3b"]; w2b = env["w2b"]
    hbuf = env["h"]; sil = env["sil"]; tmp = env["tmp"]; acc = env["acc"]
    pb = env["psum"]; slotuse = env["slotuse"]; bankfree = env["bankfree"]
    silfree = env["silfree"]
    W1SLOT = env["W1SLOT"]; W2SLOT = env["W2SLOT"]
    assert i_chunk <= env["ICMAX"] and n_i * T <= hbuf.shape[1]

    chunks = [list(range(c0, min(n_i, c0 + i_chunk))) for c0 in range(0, n_i, i_chunk)]
    dchunks = [list(range(c0, min(DK, c0 + d_chunk))) for c0 in range(0, DK, d_chunk)]
    assert n_i * d_chunk * 128 <= W2SLOT

    gidx = 0
    for ci, chunk in enumerate(chunks):
        cs = len(chunk)
        slot = ci % 2
        dcnt = {}
        for name, wb, src in (("w1", w1b, w1_ap), ("w3", w3b, w3_ap)):
            key = ("A" + name, slot)
            wl = []
            if key in slotuse:
                wl.append(("pe", slotuse[key]))
            dcnt[name] = P.dma(
                "gpsimd",
                wb[:, slot * W1SLOT: slot * W1SLOT + 16 * cs * 128].rearrange(
                    "p (k c) -> p k c", k=16),
                src[:, chunk[0] * 128:(chunk[0] + cs) * 128].rearrange(
                    "(k p) c -> p k c", p=128),
                "w", waits=wl)
        for li, gi in enumerate(chunk):
            for half in range(2):
                par = gidx % 2
                bA, bB = pb[2 * par], pb[2 * par + 1]
                iA, iB = 2 * par, 2 * par + 1
                gidx += 1
                wl = [dcnt["w1"], dcnt["w3"]]
                for ib in (iA, iB):
                    if ib in bankfree:
                        wl.append(bankfree.pop(ib))
                first = True
                for k in range(DK):
                    off = slot * W1SLOT + k * cs * 128 + li * 128
                    rhs = xnb[:, k * T + half * 512: k * T + (half + 1) * 512]
                    P.mm(bA[:, :], w1b[:, off:off + 128], rhs,
                         start=(k == 0), stop=(k == DK - 1),
                         waits=wl if first else ())
                    first = False
                    P.mm(bB[:, :], w3b[:, off:off + 128], rhs,
                         start=(k == 0), stop=(k == DK - 1),
                         sem="pe" if k == DK - 1 else None)
                pecnt = P.cnt["pe"]
                slotuse[("Aw1", slot)] = pecnt
                slotuse[("Aw3", slot)] = pecnt
                ssl = gidx % 2
                swl = [("pe", pecnt)]
                if ssl in silfree:
                    swl.append(silfree.pop(ssl))
                acnt = P.act(sil[:, ssl * 512:(ssl + 1) * 512], bA[:, :], AF.Silu,
                             sem="act", waits=swl)
                hdst = hbuf[:, gi * T + half * 512: gi * T + (half + 1) * 512]
                hg = [env["h_guard"]] if "h_guard" in env else []
                if c_col is None:
                    dcnt2 = P.tt(hdst, sil[:, ssl * 512:(ssl + 1) * 512], bB[:, :],
                                 ALU.mult, sem="dve", waits=[("act", acnt)] + hg)
                else:
                    t1 = P.tt(tmp[:, ssl * 512:(ssl + 1) * 512], bB[:, :],
                              c_col[:, half * 512:(half + 1) * 512], ALU.mult,
                              sem="dve", waits=[("pe", pecnt)])
                    dcnt2 = P.tt(hdst, sil[:, ssl * 512:(ssl + 1) * 512],
                                 tmp[:, ssl * 512:(ssl + 1) * 512],
                                 ALU.mult, sem="dve",
                                 waits=[("act", acnt), ("dve", t1)] + hg)
                bankfree[iA] = ("act", acnt)
                bankfree[iB] = ("dve", dcnt2)
                silfree[ssl] = ("dve", dcnt2)
    env.pop("h_guard", None)
    h_ready = P.cnt["dve"]

    emit_phaseB(P, env, w2_ap=w2_ap, n_i=n_i, rhs_buf=hbuf,
                d_chunk=d_chunk, first_into_acc=first_into_acc,
                hready=("dve", h_ready))


def emit_moe_router(P, env, *, wg_ap, call_tile, ident_sb, tag="rt"):
    nc = P.nc
    xnb = env["xnb"]
    pb = env["psum"]
    wgt = nc.alloc_sbuf_tensor(f"wgt{tag}", [128, DK * 16], BF16)
    sc = nc.alloc_sbuf_tensor(f"sc{tag}", [128, 16], F32)
    msk = nc.alloc_sbuf_tensor(f"msk{tag}", [128, 16], F32)
    comb = nc.alloc_sbuf_tensor(f"comb{tag}", [128, 16], F32)
    eqt = nc.alloc_sbuf_tensor(f"eq{tag}", [128, 16], F32)
    gs = nc.alloc_sbuf_tensor(f"gs{tag}", [128, 4], F32)
    gmb = nc.alloc_sbuf_tensor(f"gmb{tag}", [128, 4], F32)
    red = nc.alloc_sbuf_tensor(f"red{tag}", [128, 6], F32)

    wcnt = P.dma("gpsimd", wgt[:, :].rearrange("p (k c) -> p k c", k=DK),
                 wg_ap.rearrange("(k p) c -> p k c", p=128), "w")
    for tt in range(8):
        wl = [wcnt]
        if "rtbank0" in env:
            wl.append(env.pop("rtbank0"))
        for k in range(DK):
            P.mm(pb[0][:, 0:16], xnb[:, k * T + tt * 128: k * T + (tt + 1) * 128],
                 wgt[:, k * 16:(k + 1) * 16],
                 start=(k == 0), stop=(k == DK - 1),
                 waits=wl if k == 0 else (), sem="pe" if k == DK - 1 else None)
        pc = P.cnt["pe"]
        a1 = P.act(sc[:, :], pb[0][:, 0:16], AF.Exp, sem="act", waits=[("pe", pc)])
        env["rtbank0"] = ("act", a1)
        d1 = P.reduce(red[:, 0:1], sc[:, :], ALU.add, sem="dve", waits=[("act", a1)])
        d2 = P.recip(red[:, 1:2], red[:, 0:1], sem="dve", waits=[("dve", d1)])
        for g in range(4):
            P.reduce(gs[:, g:g + 1], sc[:, 4 * g:4 * (g + 1)], ALU.max,
                     sem="dve", waits=[("act", a1)])
        d3 = P.cnt["dve"]
        d4 = P.reduce(red[:, 2:3], gs[:, :], ALU.max, sem="dve", waits=[("dve", d3)])
        e1 = P.tsc(gmb[:, :], gs[:, :], red[:, 2:3], None, ALU.is_equal,
                   sem="dve", waits=[("dve", d4)])
        f1 = P.stt(gs[:, :], gmb[:, :], -1e30, gs[:, :], ALU.mult, ALU.add,
                   sem="dve", waits=[("dve", e1)])
        f2 = P.reduce(red[:, 3:4], gs[:, :], ALU.max, sem="dve", waits=[("dve", f1)])
        f3 = P.tsc(gs[:, :], gs[:, :], red[:, 3:4], None, ALU.is_equal,
                   sem="dve", waits=[("dve", f2)])
        f4 = P.tt(gmb[:, :], gmb[:, :], gs[:, :], ALU.add, sem="dve", waits=[("dve", f3)])
        f5 = P.tsc(gmb[:, :], gmb[:, :], 1e30, -1e30, ALU.mult, ALU.add,
                   sem="dve", waits=[("dve", f4)])
        for g in range(4):
            P.tsc(msk[:, 4 * g:4 * (g + 1)], sc[:, 4 * g:4 * (g + 1)],
                  gmb[:, g:g + 1], None, ALU.add, sem="dve", waits=[("dve", f5)])
        for r in range(4):
            r1 = P.reduce(red[:, 4:5], msk[:, :], ALU.max, sem="dve",
                          waits=[("dve", P.cnt["dve"])])
            r3 = P.tsc(eqt[:, :], msk[:, :], red[:, 4:5], None, ALU.is_equal,
                       sem="dve", waits=[("dve", r1)])
            if r == 0:
                r4 = P.tsc(comb[:, :], eqt[:, :], red[:, 4:5], None, ALU.mult,
                           sem="dve", waits=[("dve", r3)])
            else:
                r4 = P.stt(comb[:, :], eqt[:, :], red[:, 4:5], comb[:, :],
                           ALU.mult, ALU.add, sem="dve", waits=[("dve", r3)])
            P.stt(msk[:, :], eqt[:, :], -1e30, msk[:, :],
                  ALU.mult, ALU.add, sem="dve", waits=[("dve", r4)])
        s1 = P.tsc(comb[:, :], comb[:, :], red[:, 1:2], None, ALU.mult,
                   sem="dve", waits=[("dve", P.cnt["dve"])])
        wl = [("dve", s1)]
        if "rtbank1" in env:
            wl.append(env.pop("rtbank1"))
        tcnt = P.transpose(pb[1][0:16, 0:128], comb[:, 0:16], ident_sb[:, :],
                           sem="pe", waits=wl)
        c1 = P.copy(call_tile[0:16, tt * 128:(tt + 1) * 128], pb[1][0:16, 0:128],
                    sem="dve", waits=[("pe", tcnt)])
        env["rtbank1"] = ("dve", c1)
    return P.cnt["dve"]


def emit_launchA(P, *, xs_ap, wf_ap, wi_ap, wv_ap, xt_out, xn_out, f_out, g_out,
                 ap_out, wvt_ap=None, rw_ap=None, waux_ap=None, wvr_out=None):
    nc = P.nc
    xnat = nc.alloc_sbuf_tensor("xnat", [128, 4 * D], F32)
    xntok = nc.alloc_sbuf_tensor("xntok", [128, 4 * D], BF16)
    sqt = nc.alloc_sbuf_tensor("sqt", [128, D], F32)
    stat = nc.alloc_sbuf_tensor("stat", [128, 4], F32)
    xTa = nc.alloc_sbuf_tensor("xTa", [128, DK * 512], F32)
    xnb = nc.alloc_sbuf_tensor("xnbA", [128, DK * T], BF16)
    wfb = nc.alloc_sbuf_tensor("wfb", [128, 2 * 16 * 2 * 128], BF16)
    wib = nc.alloc_sbuf_tensor("wib", [128, 2 * 16 * 2 * 128], BF16)
    wvb = nc.alloc_sbuf_tensor("wvb", [128, 2 * 16 * 2 * 128], BF16)
    ftile = nc.alloc_sbuf_tensor("ftile", [128, 2 * T], F32)
    gtile = nc.alloc_sbuf_tensor("gtile", [128, 2 * T], BF16)
    sg1 = nc.alloc_sbuf_tensor("sg1", [128, 2 * 512], F32)
    sg2 = nc.alloc_sbuf_tensor("sg2", [128, 2 * 512], F32)
    t1b = nc.alloc_sbuf_tensor("t1b", [128, 2 * 512], F32)
    scr = nc.alloc_sbuf_tensor("scr", [128, 512], F32)
    zt = nc.alloc_sbuf_tensor("zt", [128, 512], F32)
    apb = nc.alloc_sbuf_tensor("apb", [128, 64], F32)
    identnp = nc.inline_tensor(np.eye(128, dtype=np.float32), name="ident128")
    ident = nc.alloc_sbuf_tensor("ident", [128, 128], F32)
    identbnp = nc.inline_tensor(np.eye(128, dtype=np.float32).astype(ml_dtypes.bfloat16), name="identb128")
    identb = nc.alloc_sbuf_tensor("identb", [128, 128], BF16)
    pb = [nc.alloc_psum_tensor(f"pa{i}", [128, 512], F32) for i in range(6)]
    pbb = [nc.alloc_psum_tensor(f"pab{i}", [128, 128], BF16) for i in range(2)]

    hcnt = P.dma("sync", ident[:, :], identnp.ap(), "hx")
    hcnt2 = P.dma("sync", identb[:, :], identbnp.ap(), "hx")
    P.memset(zt[:, :], 0.0, sem="dve")
    ztc = P.cnt["dve"]

    bankuse = {}
    xtstore = []
    for rnd in range(2):
        lcnt = []
        for j in range(4):
            tt = rnd * 4 + j
            b, sr = tt // 4, (tt % 4) * 128
            lw = []
            if rnd == 1:
                lw = xtstore
            lcnt.append(P.dma("sync", xnat[:, j * D:(j + 1) * D],
                              xs_ap[b, sr:sr + 128, :], "hx", waits=lw))
        if rnd == 1:
            xtstore = []
        for j in range(4):
            tt = rnd * 4 + j
            wl = [lcnt[j]]
            s0 = P.tt(sqt[:, :], xnat[:, j * D:(j + 1) * D],
                      xnat[:, j * D:(j + 1) * D], ALU.mult, sem="dve", waits=wl)
            s1 = P.reduce(stat[:, 0:1], xnat[:, j * D:(j + 1) * D], ALU.add,
                          sem="dve", waits=[("dve", s0)])
            s2 = P.reduce(stat[:, 2:3], sqt[:, :], ALU.add, sem="dve",
                          waits=[("dve", s0)])
            m1 = P.tsc(stat[:, 0:1], stat[:, 0:1], 1.0 / D, None, ALU.mult,
                       sem="dve", waits=[("dve", s1)])
            m2 = P.tsc(stat[:, 2:3], stat[:, 2:3], 1.0 / D, None, ALU.mult,
                       sem="dve", waits=[("dve", s2)])
            m3 = P.stt(stat[:, 3:4], stat[:, 0:1], stat[:, 0:1], stat[:, 2:3],
                       ALU.mult, ALU.subtract, sem="dve", waits=[("dve", m2)])
            a1 = P.act(stat[:, 1:2], stat[:, 3:4], AF.Ln, bias=1e-5, scale=-1.0,
                       sem="act", waits=[("dve", m3)])
            a2 = P.act(stat[:, 1:2], stat[:, 1:2], AF.Exp, scale=-0.5,
                       sem="act", waits=[("act", a1)])
            x1 = P.tsc(xntok[:, j * D:(j + 1) * D], xnat[:, j * D:(j + 1) * D],
                       stat[:, 0:1], stat[:, 1:2], ALU.subtract, ALU.mult,
                       sem="dve", waits=[("act", a2)])
            P.wait("tensor", "dve", x1)
            for dk in range(DK):
                bk = dk % 2
                wl = [hcnt]
                if bk in bankuse:
                    wl.append(bankuse.pop(bk))
                tc = P.transpose(pb[bk][:, 0:128],
                                 xnat[:, j * D + dk * 128: j * D + (dk + 1) * 128],
                                 ident[:, :], sem="pe", waits=wl)
                cc = P.copy(xTa[:, dk * 512 + j * 128: dk * 512 + (j + 1) * 128],
                            pb[bk][:, 0:128], sem="dve", waits=[("pe", tc)])
                bankuse[bk] = ("dve", cc)
                bk2 = 2 + dk % 2
                wl = [hcnt2]
                if bk2 in bankuse:
                    wl.append(bankuse.pop(bk2))
                tc2 = P.transpose(pbb[dk % 2][:, 0:128],
                                  xntok[:, j * D + dk * 128: j * D + (dk + 1) * 128],
                                  identb[:, :], sem="pe", waits=wl)
                cc2 = P.copy(xnb[:, dk * T + rnd * 512 + j * 128: dk * T + rnd * 512 + (j + 1) * 128],
                             pbb[dk % 2][:, 0:128], sem="dve", waits=[("pe", tc2)])
                bankuse[bk2] = ("dve", cc2)
        xc = P.cnt["dve"]
        xtstore.append(P.dma("sync",
                             xt_out[:, :, rnd * 512:(rnd + 1) * 512].rearrange(
                                 "k p c -> p k c"),
                             xTa[:, :].rearrange("p (k c) -> p k c", k=DK), "st",
                             waits=[("dve", xc)]))
    xnc = P.cnt["dve"]
    P.dma("sync", xn_out.rearrange("k p c -> p k c"),
          xnb[:, :].rearrange("p (k c) -> p k c", k=DK), "st", waits=[("dve", xnc)])

    WSLOT = 16 * 2 * 128
    slotuse = {}
    bankuse2 = {}
    sttok = {}
    for ci in range(8):
        chunk = [2 * ci, 2 * ci + 1]
        slot = ci % 2
        dc = {}
        for nm, wb, src in (("wf", wfb, wf_ap), ("wi", wib, wi_ap), ("wv", wvb, wv_ap)):
            wl = [("pe", slotuse[(nm, slot)])] if (nm, slot) in slotuse else []
            dc[nm] = P.dma("gpsimd",
                           wb[:, slot * WSLOT: slot * WSLOT + 16 * 2 * 128].rearrange(
                               "p (k c) -> p k c", k=16),
                           src[:, chunk[0] * 128:(chunk[0] + 2) * 128].rearrange(
                               "(k p) c -> p k c", p=128),
                           "w", waits=wl)
        for li, rt in enumerate(chunk):
            par = rt % 2
            bF, bI, bV = pb[3 * par], pb[3 * par + 1], pb[3 * par + 2]
            for half in range(2):
                wl = [dc["wf"], dc["wi"], dc["wv"], ("dve", xnc)]
                for ib in (3 * par, 3 * par + 1, 3 * par + 2):
                    if ib in bankuse2:
                        wl.append(bankuse2.pop(ib))
                first = True
                for k in range(DK):
                    off = slot * WSLOT + k * 2 * 128 + li * 128
                    rhs = xnb[:, k * T + half * 512: k * T + (half + 1) * 512]
                    P.mm(bF[:, :], wfb[:, off:off + 128], rhs, start=(k == 0),
                         stop=(k == DK - 1), waits=wl if first else ())
                    first = False
                    P.mm(bI[:, :], wib[:, off:off + 128], rhs, start=(k == 0),
                         stop=(k == DK - 1))
                    P.mm(bV[:, :], wvb[:, off:off + 128], rhs, start=(k == 0),
                         stop=(k == DK - 1), sem="pe" if k == DK - 1 else None)
                pc = P.cnt["pe"]
                for nm in ("wf", "wi", "wv"):
                    slotuse[(nm, slot)] = pc
                ssl = half
                fwl = [("pe", pc)]
                if ("ft", par, half) in sttok:
                    fwl.extend(sttok.pop(("ft", par, half)))
                af = P.act(ftile[:, par * T + half * 512: par * T + (half + 1) * 512],
                           bF[:, :], AF.Tanh, sem="act", waits=fwl)
                ai = P.act(sg1[:, ssl * 512:(ssl + 1) * 512], bI[:, :], AF.Sigmoid,
                           sem="act", waits=[("pe", pc)])
                av = P.act(sg2[:, ssl * 512:(ssl + 1) * 512], bV[:, :], AF.Sigmoid,
                           sem="act", waits=[("pe", pc)])
                v1 = P.tt(t1b[:, ssl * 512:(ssl + 1) * 512], bV[:, :],
                          sg2[:, ssl * 512:(ssl + 1) * 512], ALU.mult,
                          sem="dve", waits=[("act", av)])
                gwl = [("act", ai), ("dve", v1)]
                if ("gt", par, half) in sttok:
                    gwl.extend(sttok.pop(("gt", par, half)))
                v2 = P.tt(gtile[:, par * T + half * 512: par * T + (half + 1) * 512],
                          sg1[:, ssl * 512:(ssl + 1) * 512],
                          t1b[:, ssl * 512:(ssl + 1) * 512], ALU.mult,
                          sem="dve", waits=gwl)
                bankuse2[3 * par] = ("act", af)
                bankuse2[3 * par + 1] = ("act", ai)
                bankuse2[3 * par + 2] = ("dve", v2)
            fv = P.cnt["act"]
            gv = P.cnt["dve"]
            for b in range(2):
                s1 = P.scan(scr[:, :], ftile[:, par * T + b * 512: par * T + (b + 1) * 512],
                            gtile[:, par * T + b * 512: par * T + (b + 1) * 512],
                            0.0, ALU.mult, ALU.add, sem="dve",
                            waits=[("act", fv), ("dve", P.cnt["dve"])])
                P.copy(apb[:, rt * 2 + b: rt * 2 + b + 1], scr[:, 511:512],
                       sem="dve", waits=[("dve", s1)])
                s2 = P.scan(scr[:, :], ftile[:, par * T + b * 512: par * T + (b + 1) * 512],
                            zt[:, :], 1.0, ALU.mult, ALU.add, sem="dve",
                            waits=[("dve", P.cnt["dve"]), ("dve", ztc)])
                P.copy(apb[:, 32 + rt * 2 + b: 32 + rt * 2 + b + 1], scr[:, 511:512],
                       sem="dve", waits=[("dve", s2)])
            sc = P.cnt["dve"]
            stf = P.dma("sync", f_out[rt], ftile[:, par * T:(par + 1) * T], "st",
                        waits=[("act", fv)])
            stg = P.dma("sync", g_out[rt], gtile[:, par * T:(par + 1) * T], "st",
                        waits=[("dve", sc)])
            sttok[("ft", par, 0)] = [stf, ("dve", sc)]
            sttok[("ft", par, 1)] = [stf, ("dve", sc)]
            sttok[("gt", par, 0)] = [stg, ("dve", sc)]
            sttok[("gt", par, 1)] = [stg, ("dve", sc)]
    apc = P.cnt["dve"]
    P.dma("sync", ap_out, apb[:, :], "st", waits=[("dve", apc)])

    if wvt_ap is not None:
        slabw = nc.alloc_sbuf_tensor("slabw", [128, 2 * 1024], F32)
        rwA = nc.alloc_sbuf_tensor("rwA", [128, 32], F32)
        wvrA = nc.alloc_sbuf_tensor("wvrA", [128, 1024], BF16)
        lrwA = P.dma("sync", rwA[:, :], rw_ap, "hx")
        mzA = P.memset(wvrA[:, :], 0.0, sem="dve")
        slab_free = {}
        wvb_free = {}
        for h in range(16):
            for sl in range(2):
                b = (h * 2 + sl) % 2
                wl = []
                if b in slab_free:
                    wl.append(slab_free.pop(b))
                ls = P.dma("scalar", slabw[:, b * 1024:(b + 1) * 1024],
                           wvt_ap[128 * h:128 * (h + 1),
                                  sl * 1024:(sl + 1) * 1024],
                           "wvt", waits=wl)
                for j in range(8):
                    dcx = sl * 8 + j
                    pbk = 4 + (h * DK + dcx) % 2
                    mlw = [ls, lrwA]
                    if pbk in bankuse2:
                        mlw.append(bankuse2.pop(pbk))
                    if ("wv", pbk) in wvb_free:
                        mlw.append(wvb_free.pop(("wv", pbk)))
                    mc = P.mm(pb[pbk][:, 0:2],
                              slabw[:, b * 1024 + j * 128: b * 1024 + (j + 1) * 128],
                              rwA[:, 2 * h:2 * h + 2], start=True, stop=True,
                              sem="pe", waits=mlw)
                    cc = P.copy(wvrA[:, dcx * 64 + h: dcx * 64 + h + 33: 32],
                                pb[pbk][:, 0:2], sem="dve",
                                waits=[("pe", mc), ("dve", mzA)])
                    wvb_free[("wv", pbk)] = ("dve", cc)
                slab_free[b] = ("pe", P.cnt["pe"])
        P.dma("sync", wvr_out, wvrA[:, :], "st", waits=[("dve", P.cnt["dve"])])
    if waux_ap is not None:
        P.dma("sync", xnat[0:1, 0:D], waux_ap, "hx", waits=[("dve", apc)])
    P.wait_stream("sync", "st")
    P.wait_stream("sync", "hx")
    P.wait_stream("scalar", "wvt")


def emit_phaseB(P, env, *, w2_ap, n_i, rhs_buf, d_chunk, first_into_acc,
                hready):
    w2b = env["w2b"]; acc = env["acc"]; pb = env["psum"]
    slotuse = env["slotuse"]; bankfree = env["bankfree"]
    W2SLOT = env["W2SLOT"]
    dchunks = [list(range(c0, min(DK, c0 + d_chunk))) for c0 in range(0, DK, d_chunk)]
    assert n_i * d_chunk * 128 <= W2SLOT
    bidx = env.get("bidx", 0)
    for dci, dchunk in enumerate(dchunks):
        dcs = len(dchunk)
        slot = dci % 2
        key = ("Bw2", slot)
        wl = []
        if key in slotuse:
            wl.append(("pe", slotuse[key]))
        w2cnt = P.dma(
            "gpsimd",
            w2b[:, slot * W2SLOT: slot * W2SLOT + n_i * dcs * 128].rearrange(
                "p (k c) -> p k c", k=n_i),
            w2_ap[:, dchunk[0] * 128:(dchunk[0] + dcs) * 128].rearrange(
                "(k p) c -> p k c", p=128),
            "w", waits=wl)
        for mi, m2 in enumerate(dchunk):
            for half in range(2):
                par = bidx % 2
                bY = pb[4 + par]; iY = 4 + par
                bidx += 1
                wl = [w2cnt, hready]
                if iY in bankfree:
                    wl.append(bankfree.pop(iY))
                for k2 in range(n_i):
                    off = slot * W2SLOT + k2 * dcs * 128 + mi * 128
                    if isinstance(rhs_buf, list):
                        ent = rhs_buf[k2]
                        if isinstance(ent, list):
                            rhs = ent[half]
                        else:
                            rhs = ent[:, half * 512:(half + 1) * 512]
                    else:
                        rhs = rhs_buf[:, k2 * T + half * 512: k2 * T + (half + 1) * 512]
                    P.mm(bY[:, :], w2b[:, off:off + 128], rhs,
                         start=(k2 == 0), stop=(k2 == n_i - 1),
                         waits=wl if k2 == 0 else (),
                         sem="pe" if k2 == n_i - 1 else None)
                pecnt = P.cnt["pe"]
                slotuse[("Bw2", slot)] = pecnt
                adst = acc[:, m2 * T + half * 512: m2 * T + (half + 1) * 512]
                if first_into_acc:
                    dc = P.copy(adst, bY[:, :], sem="dve", waits=[("pe", pecnt)])
                else:
                    dc = P.tt(adst, adst, bY[:, :], ALU.add, sem="dve",
                              waits=[("pe", pecnt)])
                bankfree[iY] = ("dve", dc)
    env["bidx"] = bidx


def emit_projA(P, env, *, w_ap, n_m, evac_cb, tagkey, chunk=2, banks=(0, 1),
               rhs_ready=None, wbuf=None, wslot=None):
    xnb = env["xnb"]; pb = env["psum"]
    slotuse = env["slotuse"]; bankfree = env["bankfree"]
    wbuf = wbuf if wbuf is not None else env["w1b"]
    WSLOT = wslot if wslot is not None else env["W1SLOT"]
    chunks = [list(range(c0, min(n_m, c0 + chunk))) for c0 in range(0, n_m, chunk)]
    gidx = env.get(("gidx", tagkey), 0)
    for ci, ch in enumerate(chunks):
        cs = len(ch)
        slot = ci % 2
        key = (tagkey, slot)
        wl = []
        if key in slotuse:
            wl.append(("pe", slotuse[key]))
        dcnt = P.dma("gpsimd",
                     wbuf[:, slot * WSLOT: slot * WSLOT + 16 * cs * 128].rearrange(
                         "p (k c) -> p k c", k=16),
                     w_ap[:, ch[0] * 128:(ch[0] + cs) * 128].rearrange(
                         "(k p) c -> p k c", p=128),
                     "w", waits=wl)
        for li, mt in enumerate(ch):
            for half in range(2):
                par = gidx % 2
                bk = banks[par]
                gidx += 1
                wl = [dcnt]
                if rhs_ready is not None:
                    wl.append(rhs_ready)
                if bk in bankfree:
                    wl.append(bankfree.pop(bk))
                for k in range(DK):
                    off = slot * WSLOT + k * cs * 128 + li * 128
                    rhs = xnb[:, k * T + half * 512: k * T + (half + 1) * 512]
                    P.mm(pb[bk][:, :], wbuf[:, off:off + 128], rhs,
                         start=(k == 0), stop=(k == DK - 1),
                         waits=wl if k == 0 else (),
                         sem="pe" if k == DK - 1 else None)
                pecnt = P.cnt["pe"]
                slotuse[(tagkey, slot)] = pecnt
                tok = evac_cb(mt, half, pb[bk], pecnt)
                bankfree[bk] = tok


def emit_launchB0(P, aps):
    nc = P.nc
    env = glu_env(P, icmax=1, w2slot=2048, hmax=16)
    xTb = env["acc"]
    xnb = env["xnb"]
    hbuf = env["h"]

    carry = nc.alloc_sbuf_tensor("carry", [128, 32], F32)
    apsb = nc.alloc_sbuf_tensor("apsb", [128, 8 * 64], F32)
    initsb = nc.alloc_sbuf_tensor("initsb", [128, 16], F32)
    selb = nc.alloc_sbuf_tensor("selb", [128, 8], F32)
    selrow = nc.alloc_sbuf_tensor("selrow", [1, 8], BF16)
    scr8 = nc.alloc_sbuf_tensor("scr8", [128, 8], F32)
    cat8 = nc.alloc_sbuf_tensor("cat8", [128, 8], F32)
    ftl = nc.alloc_sbuf_tensor("ftl", [128, 2 * T], F32)
    gtl = nc.alloc_sbuf_tensor("gtl", [128, 2 * T], BF16)
    stt_ = nc.alloc_sbuf_tensor("sttl", [128, 2 * 512], F32)
    tq = env["sil"]
    sgq = env["tmp"]
    callb = nc.alloc_sbuf_tensor("callb", [16, T], BF16)
    selnp = np.zeros((16, 16 * 128), np.float32)
    for hh in range(16):
        selnp[hh, 128 * hh:128 * (hh + 1)] = 1.0
    sel16c = nc.inline_tensor(selnp.astype(ml_dtypes.bfloat16), name="sel16c")
    sel16 = nc.alloc_sbuf_tensor("sel16", [16, 16 * 128], BF16)
    identnp = nc.inline_tensor(np.eye(128, dtype=np.float32), name="identB128")
    ident = nc.alloc_sbuf_tensor("identB", [128, 128], F32)
    ones1 = nc.inline_tensor(np.ones((1, 128), np.float32).astype(ml_dtypes.bfloat16),
                             name="ones1B")
    ones_sb = nc.alloc_sbuf_tensor("ones_sbB", [1, 128], BF16)
    env["ident"] = ident
    env["ones_sb"] = ones_sb
    env["sel16"] = sel16
    env["callb"] = callb

    lxt = P.dma("sync", xTb[:, :].rearrange("p (k c) -> p k c", k=DK),
                aps["xt_in"].rearrange("k p c -> p k c"), "hx")
    lxn = P.dma("sync", xnb[:, :].rearrange("p (k c) -> p k c", k=DK),
                aps["xn_in"].rearrange("k p c -> p k c"), "hx")
    lap = P.dma("sync", apsb[:, :].rearrange("p (c k) -> p c k", c=8),
                aps["apall"], "hx")
    lin = P.dma("sync", initsb[:, :], aps["initsb_in"], "hx")
    lid = P.dma("sync", ident[:, :], identnp.ap(), "hx")
    lsel = P.dma("gpsimd", sel16[:, :], sel16c.ap(), "x")
    lone = P.dma("gpsimd", ones_sb[:, :], ones1.ap(), "x")
    lsv = P.dma("gpsimd", selrow[0:1, 0:8], aps["selvec"], "x")
    bc = P.mm(env["psum"][0][:, 0:8], ones_sb[0:1, :], selrow[0:1, 0:8],
              start=True, stop=True, sem="pe", waits=[lone, lsv])
    c0 = P.copy(selb[:, :], env["psum"][0][:, 0:8], sem="dve", waits=[("pe", bc)])
    env["bankfree"][0] = ("dve", c0)

    for rt in range(DK):
        for b in range(2):
            col = rt * 2 + b
            s1 = P.scan(scr8[:, :], apsb[:, :].rearrange("p (c k) -> p c k", c=8)[:, :, 32 + col],
                        apsb[:, :].rearrange("p (c k) -> p c k", c=8)[:, :, col],
                        initsb[:, rt:rt + 1], ALU.mult, ALU.add, sem="dve",
                        waits=[lap, lin, ("dve", P.cnt["dve"])])
            c1 = P.copy(cat8[:, 0:1], initsb[:, rt:rt + 1], sem="dve",
                        waits=[("dve", s1)])
            c2 = P.copy(cat8[:, 1:8], scr8[:, 0:7], sem="dve", waits=[("dve", c1)])
            c3 = P.tt(cat8[:, :], cat8[:, :], selb[:, :], ALU.mult, sem="dve",
                      waits=[("dve", c2)])
            P.reduce(carry[:, col:col + 1], cat8[:, :], ALU.add, sem="dve",
                     waits=[("dve", c3)])
    carry_done = P.cnt["dve"]

    fgtok = {}
    def q_evac(rt, b, bank, pecnt):
        swl = [("pe", pecnt), fgtok[("f", rt % 2)], fgtok[("g", rt % 2)],
               ("dve", carry_done), ("dve", P.cnt["dve"])]
        ssl = b
        s1 = P.scan(stt_[:, ssl * 512:(ssl + 1) * 512],
                    ftl[:, (rt % 2) * T + b * 512: (rt % 2) * T + (b + 1) * 512],
                    gtl[:, (rt % 2) * T + b * 512: (rt % 2) * T + (b + 1) * 512],
                    carry[:, rt * 2 + b: rt * 2 + b + 1], ALU.mult, ALU.add,
                    sem="dve", waits=swl)
        t1 = P.tt(tq[:, ssl * 512:(ssl + 1) * 512], bank[:, :],
                  stt_[:, ssl * 512:(ssl + 1) * 512], ALU.mult, sem="dve",
                  waits=[("dve", s1)])
        a1 = P.act(sgq[:, ssl * 512:(ssl + 1) * 512],
                   tq[:, ssl * 512:(ssl + 1) * 512], AF.Sigmoid, sem="act",
                   waits=[("dve", t1)])
        r1 = P.tt(hbuf[:, rt * T + b * 512: rt * T + (b + 1) * 512],
                  tq[:, ssl * 512:(ssl + 1) * 512],
                  sgq[:, ssl * 512:(ssl + 1) * 512], ALU.mult, sem="dve",
                  waits=[("act", a1)])
        return ("dve", r1)

    def fg_load(rt):
        wl = []
        if ("fw", rt % 2) in fgtok:
            wl = fgtok.pop(("fw", rt % 2))
        f1 = P.dma("sync", ftl[:, (rt % 2) * T:(rt % 2 + 1) * T], aps["f_in"][rt],
                   "hx", waits=wl)
        g1 = P.dma("sync", gtl[:, (rt % 2) * T:(rt % 2 + 1) * T], aps["g_in"][rt],
                   "hx", waits=wl)
        fgtok[("f", rt % 2)] = f1
        fgtok[("g", rt % 2)] = g1

    fg_load(0)
    fg_load(1)
    def q_evac_outer(mt, half, bank, pecnt):
        tok = q_evac(mt, half, bank, pecnt)
        if half == 1:
            if mt + 2 < DK:
                fgtok[("fw", mt % 2)] = [tok]
                fg_load(mt + 2)
        return tok

    emit_projA(P, env, w_ap=aps["wq"], n_m=DK, evac_cb=q_evac_outer, tagkey="wqs",
               rhs_ready=lxn, banks=(0, 1), chunk=1)
    ro_ready = P.cnt["dve"]

    emit_phaseB(P, env, w2_ap=aps["wro"], n_i=DK, rhs_buf=hbuf, d_chunk=1,
                first_into_acc=False, hready=("dve", ro_ready))
    ro_used = P.cnt["pe"]

    env["h_guard"] = ("pe", ro_used)
    emit_glu_ffn(P, env, w1_ap=aps["wgate"], w3_ap=aps["wup"], w2_ap=aps["wdown"],
                 n_i=16, c_col=None, tag="lg", i_chunk=1, d_chunk=1)

    emit_moe_router(P, env, wg_ap=aps["wgmoe"], call_tile=callb, ident_sb=ident)
    env["bankfree"][0] = env.pop("rtbank0")
    env["bankfree"][1] = env.pop("rtbank1")
    cb = nc.alloc_sbuf_tensor("cbB", [128, T], BF16)
    pbc = env["psum"][2]
    for e in range(16):
        for half in range(2):
            wl = [("dve", P.cnt["dve"])]
            if e == 0 and half == 0:
                wl.append(lsel)
            if 2 in env["bankfree"]:
                wl.append(env["bankfree"].pop(2))
            pe1 = P.mm(pbc[:, 0:512], sel16[:, e * 128:(e + 1) * 128],
                       callb[:, half * 512:(half + 1) * 512],
                       start=True, stop=True, sem="pe", waits=wl)
            cc = P.copy(cb[:, half * 512:(half + 1) * 512], pbc[:, 0:512],
                        sem="dve", waits=[("pe", pe1)])
            env["bankfree"][2] = ("dve", cc)
        emit_glu_ffn(P, env, w1_ap=aps["w1t"][e], w3_ap=aps["w3t"][e],
                     w2_ap=aps["w2t"][e], n_i=11, c_col=cb[:, :], tag=f"e{e}",
                     i_chunk=1, d_chunk=1)
    for sh in range(2):
        emit_glu_ffn(P, env, w1_ap=aps["ws1t"][:, sh * 1408:(sh + 1) * 1408],
                     w3_ap=aps["ws3t"][:, sh * 1408:(sh + 1) * 1408],
                     w2_ap=aps["ws2t"][sh * 1408:(sh + 1) * 1408, :],
                     n_i=11, c_col=None, tag=f"sh{sh}", i_chunk=1, d_chunk=1)
    env["xn0_lastuse"] = ("pe", P.cnt["pe"])
    env["apsb"] = apsb
    env["ftl"] = ftl
    env["gtl"] = gtl
    env["sttl"] = stt_
    env["cbB"] = cb
    env["sel16_load"] = lsel
    env["ftl_free"] = ("dve", ro_ready)
    env["gtl_free"] = ("dve", ro_ready)
    return env


def emit_LN1(P, env):
    nc = P.nc
    xTb = env["acc"]; xnb = env["xnb"]; pb = env["psum"]
    sq = env["sil"]
    ftl = env["ftl"]; gtl = env["gtl"]
    meanB = ftl[:, 0:T]; rstdB = ftl[:, T:2 * T]
    m16 = gtl[0:1, 0:T]; r16 = gtl[0:1, T:2 * T]
    onescnp = nc.inline_tensor(np.ones((128, 1), np.float32), name="onescol")
    onesc = nc.alloc_sbuf_tensor("onescol_sb", [128, 1], F32)
    env["onescol_sb"] = onesc
    oc = P.dma("sync", onesc[:, :], onescnp.ap(), "hx")
    x1done = env["x1done"]
    mrow = env["sil"][32:33, 0:T]
    vrow = env["sil"][64:65, 0:T]
    bf_ = env["bankfree"]

    wl0 = [oc, x1done]
    for bkk in range(4):
        if bkk in bf_:
            wl0.append(bf_.pop(bkk))
    sqfree = {}
    for dk in range(DK):
        for half in range(2):
            sq_sl = sq[:, half * 512: half * 512 + 512]
            swl = [x1done]
            if half in sqfree:
                swl.append(sqfree.pop(half))
            s_ = P.tt(sq_sl, xTb[:, dk * T + half * 512: dk * T + (half + 1) * 512],
                      xTb[:, dk * T + half * 512: dk * T + (half + 1) * 512],
                      ALU.mult, sem="dve", waits=swl)
            P.mm(pb[half][0:1, :], onesc[:, :],
                 xTb[:, dk * T + half * 512: dk * T + (half + 1) * 512],
                 start=(dk == 0), stop=(dk == DK - 1),
                 waits=(wl0 if dk == 0 and half == 0 else ()))
            mq_ = P.mm(pb[2 + half][0:1, :], onesc[:, :], sq_sl,
                       start=(dk == 0), stop=(dk == DK - 1),
                       waits=[("dve", s_)], sem="pe")
            sqfree[half] = ("pe", mq_)
    pc = P.cnt["pe"]
    c1 = P.act(mrow[:, 0:512], pb[0][0:1, :], AF.Copy, scale=1.0 / D, sem="act",
               waits=[("pe", pc)])
    c2 = P.act(mrow[:, 512:T], pb[1][0:1, :], AF.Copy, scale=1.0 / D, sem="act",
               waits=[("pe", pc)])
    c3 = P.act(vrow[:, 0:512], pb[2][0:1, :], AF.Copy, scale=1.0 / D, sem="act",
               waits=[("pe", pc)])
    c4 = P.act(vrow[:, 512:T], pb[3][0:1, :], AF.Copy, scale=1.0 / D, sem="act",
               waits=[("pe", pc)])
    for bkk in range(4):
        bf_[bkk] = ("act", [c1, c2, c3, c4][bkk])
    scr_row = env["tmp"][64:65, 0:T]
    d0 = P.tt(scr_row, mrow[:, :], mrow[:, :], ALU.mult, sem="dve",
              waits=[("act", c4)])
    d1 = P.tt(vrow[:, :], scr_row, vrow[:, :], ALU.subtract, sem="dve",
              waits=[("dve", d0)])
    a1 = P.act(vrow[:, :], vrow[:, :], AF.Ln, bias=1e-5, scale=-1.0, sem="act",
               waits=[("dve", d1)])
    a2 = P.act(vrow[:, :], vrow[:, :], AF.Exp, scale=-0.5, sem="act",
               waits=[("act", a1)])
    c5 = P.copy(m16[:, :], mrow[:, :], sem="dve",
                waits=[("act", a2), env["gtl_free"]])
    c6 = P.copy(r16[:, :], vrow[:, :], sem="dve", waits=[("act", a2), ("dve", c5)])
    for half in range(2):
        for j, (dst, srow) in enumerate(((meanB, m16), (rstdB, r16))):
            bk = half * 2 + j
            wl = [("dve", c6)]
            if half == 0 and j == 0:
                wl.append(env["ftl_free"])
            if bk in bf_:
                wl.append(bf_.pop(bk))
            mc = P.mm(pb[bk][:, :], env["ones_sb"][0:1, :],
                      srow[:, half * 512:(half + 1) * 512],
                      start=True, stop=True, sem="pe", waits=wl)
            cc = P.copy(dst[:, half * 512:(half + 1) * 512], pb[bk][:, :],
                        sem="dve", waits=[("pe", mc)])
            bf_[bk] = ("dve", cc)
    bc = P.cnt["dve"]
    for dk in range(DK):
        u1 = P.tt(env["tmp"][:, :], xTb[:, dk * T:(dk + 1) * T], meanB[:, :],
                  ALU.subtract, sem="dve",
                  waits=[("dve", bc), env["xn0_lastuse"], ("dve", P.cnt["dve"])])
        P.tt(xnb[:, dk * T:(dk + 1) * T], env["tmp"][:, :], rstdB[:, :], ALU.mult,
             sem="dve", waits=[("dve", u1)])
    env["xn1_ready"] = ("dve", P.cnt["dve"])


def emit_l1ffn(P, env, aps):
    nc = P.nc
    xnb = env["xnb"]; pb = env["psum"]; bf_ = env["bankfree"]
    slotuse = env["slotuse"]
    xc = env["h"]
    sil = env["sil"]; tmp = env["tmp"]; sttl = env["sttl"]
    ftl = env["ftl"]; gtl = env["gtl"]
    sel16 = env["sel16"]; onesb = env["ones_sb"]
    onesc = env["onescol_sb"]
    W1SLOT = env["W1SLOT"]
    convsb = nc.alloc_sbuf_tensor("convsb", [128, 32], F32)
    rotsb = nc.alloc_sbuf_tensor("rotsb", [128, 2], F32)
    qtl = nc.alloc_sbuf_tensor("qtl", [128, T], BF16)
    ktl = nc.alloc_sbuf_tensor("ktl", [128, T], BF16)
    prodb = nc.alloc_sbuf_tensor("prodb", [128, 512], BF16)
    vgwb = nc.alloc_sbuf_tensor("vgwb", [128, 16 * 128], BF16)
    vtile = nc.alloc_sbuf_tensor("vtile", [128, T], BF16)
    qkrow = nc.alloc_sbuf_tensor("qkrow", [16, 3 * 512], BF16)
    qkb = nc.alloc_sbuf_tensor("qkb", [128, 3 * 512], BF16)
    extf = nc.alloc_sbuf_tensor("extf", [128, T], BF16)
    protb = vgwb[:, 0:128]
    bd64b = vgwb[:, 128:256]
    wvrb = qtl
    rtrD = ftl[0:16, T:2 * T]
    pb6 = nc.alloc_psum_tensor("pb6", [128, 512], F32)
    pb7 = nc.alloc_psum_tensor("pb7", [128, 512], F32)
    onesfnp = nc.inline_tensor(np.ones((1, 128), np.float32), name="onesfrow")
    onesf_row = nc.alloc_sbuf_tensor("onesf_row", [1, 128], F32)
    onesc = env["onescol_sb"]
    outf_tiles = ([env["w1b"][:, i * T:(i + 1) * T] for i in range(4)] +
                  [env["w3b"][:, i * T:(i + 1) * T] for i in range(4)] +
                  [gtl[:, 0:T], gtl[:, T:2 * T], env["cbB"][:, :]] +
                  [qtl[:, :], ktl[:, :]] +
                  [extf[:, 0:T], qkb[:, 0:T],
                   [prodb[:, 0:512], qkb[:, 1024:1536]]])
    rtrA = sil[0:16, :]; rtrB = tmp[0:16, :]; rtrC = sttl[0:16, :]
    mrow = sttl[0:1, 0:T]; vrow = tmp[0:1, 0:T]
    o_t = ftl[:, 0:T]

    lcv = P.dma("sync", convsb[:, :], aps["convw"], "hx")
    lof = P.dma("sync", onesf_row[:, :], onesfnp.ap(), "hx")
    lrt = P.dma("sync", rotsb[:, :], aps["rotcs"], "hx")
    lpr = P.dma("gpsimd", protb[:, :], aps["protc"], "x")
    lbd = P.dma("gpsimd", bd64b[:, :], aps["bd64c"], "x")

    hguard = env["h_guard2"]
    xnr = env["xn1_ready"]
    xc_tok = []
    cscr_war = {}
    for dk in range(DK):
        cw0 = convsb[:, 2 * dk:2 * dk + 1]
        cw1 = convsb[:, 2 * dk + 1:2 * dk + 2]
        S = sttl if dk % 2 == 0 else tmp
        w0 = [xnr, lcv] if dk == 0 else []
        if (dk % 2, 0) in cscr_war:
            w0.append(cscr_war.pop((dk % 2, 0)))
        t0 = P.tsc(S[:, 0:512], xnb[:, dk * T:dk * T + 512], cw1, None,
                   ALU.mult, sem="dve", waits=w0)
        w1 = []
        if (dk % 2, 1) in cscr_war:
            w1.append(cscr_war.pop((dk % 2, 1)))
        t1 = P.tsc(S[:, 512:T], xnb[:, dk * T + 512:(dk + 1) * T], cw1, None,
                   ALU.mult, sem="dve", waits=w1)
        t2 = P.stt(S[:, 512:T], xnb[:, dk * T:dk * T + 512], cw0,
                   S[:, 512:T], ALU.mult, ALU.add, sem="dve",
                   waits=[("dve", t1)])
        aw = [hguard] if dk < 2 else []
        a0 = P.act(xc[:, dk * T:dk * T + 512], S[:, 0:512], AF.Silu, sem="act",
                   waits=aw + [("dve", t0)])
        a1 = P.act(xc[:, dk * T + 512:(dk + 1) * T], S[:, 512:T], AF.Silu,
                   sem="act", waits=[("dve", t2)])
        cscr_war[(dk % 2, 0)] = ("act", a0)
        cscr_war[(dk % 2, 1)] = ("act", a1)
        xc_tok.append(("act", a1))
    xc_ready = xc_tok[-1]

    dpsum = [pb[3][0:16, :], pb[4][0:16, :], pb[5][0:16, :]]
    for bb in (2, 3, 4, 5):
        if bb in bf_:
            pass
    qk_free = {}
    for tt in range(8):
        dc = {}
        for nm, wsrc in (("q", aps["wqf"]), ("k", aps["wkf"])):
            slot = tt % 2
            key = ("l1" + nm, slot)
            wl = [("pe", slotuse[key])] if key in slotuse else []
            wb = env["w1b"] if nm == "q" else env["w3b"]
            dc[nm] = P.dma("gpsimd",
                           wb[:, slot * W1SLOT: slot * W1SLOT + 16 * 128].rearrange(
                               "p (k c) -> p k c", k=16),
                           wsrc[:, tt * 128:(tt + 1) * 128].rearrange(
                               "(k p) c -> p k c", p=128),
                           "w", waits=wl)
        for nm, bk in (("q", 0), ("k", 1)):
            wb = env["w1b"] if nm == "q" else env["w3b"]
            slot = tt % 2
            for half in range(2):
                wl = [dc[nm], xc_ready]
                if (nm, "bank") in qk_free:
                    wl.append(qk_free.pop((nm, "bank")))
                if tt == 0 and half == 0 and bk in bf_:
                    wl.append(bf_.pop(bk))
                for k in range(DK):
                    off = slot * W1SLOT + k * 128
                    P.mm(pb[bk][:, :], wb[:, off:off + 128],
                         xc[:, k * T + half * 512: k * T + (half + 1) * 512],
                         start=(k == 0), stop=(k == DK - 1),
                         waits=wl if k == 0 else (),
                         sem="pe" if k == DK - 1 else None)
                pc = P.cnt["pe"]
                slotuse[("l1" + nm, slot)] = pc
                dst = qtl if nm == "q" else ktl
                dwl = [("pe", pc)]
                if (nm, "tile", half) in qk_free:
                    dwl.append(qk_free.pop((nm, "tile", half)))
                if nm == "q":
                    aq = P.act(dst[:, half * 512:(half + 1) * 512], pb[bk][:, :],
                               AF.Silu, sem="act", waits=dwl)
                    qk_free[(nm, "bank")] = ("act", aq)
                else:
                    dq = P.copy(dst[:, half * 512:(half + 1) * 512], pb[bk][:, :],
                                sem="dve", waits=dwl)
                    qk_free[(nm, "bank")] = ("dve", dq)
            evac_tok = qk_free[(nm, "bank")]
            if nm == "q":
                q_act = evac_tok
            wl2 = [("dve", P.cnt["dve"]), lpr, lrt, evac_tok]
            if ("rot", "bank") in qk_free:
                wl2.append(qk_free.pop(("rot", "bank")))
            rm = P.mm(pb[2][:, :], protb[:, :], dst[:, 512:T], start=True,
                      stop=True, sem="pe", waits=wl2)
            rt1 = P.tsc(sttl[:, 0:512], dst[:, 512:T], rotsb[:, 0:1], None,
                        ALU.mult, sem="dve",
                        waits=[("dve", P.cnt["dve"]), lrt, evac_tok] +
                              ([xc_ready] if tt == 0 else []))
            rt2 = P.stt(dst[:, 512:T], pb[2][:, :], rotsb[:, 1:2],
                        sttl[:, 0:512], ALU.mult, ALU.add, sem="dve",
                        waits=[("pe", rm), ("dve", rt1)])
            qk_free[("rot", "bank")] = ("dve", rt2)
            qk_free[(nm, "tile", 0)] = ("dve", rt2)
            qk_free[(nm, "tile", 1)] = ("dve", rt2)
        qr = ("dve", P.cnt["dve"])
        for pi, (lh, mh) in enumerate(((0, 0), (1, 0), (1, 1))):
            pw = [qr, ("dve", P.cnt["dve"]), lbd, q_act]
            if ("prod",) in qk_free:
                pw.append(qk_free.pop(("prod",)))
            pd = P.tt(prodb[:, :], qtl[:, lh * 512:(lh + 1) * 512],
                      ktl[:, mh * 512:(mh + 1) * 512], ALU.mult, sem="dve",
                      waits=pw)
            dmw = [("dve", pd)]
            if tt == 0 and 3 + pi in bf_:
                dmw.append(bf_.pop(3 + pi))
            dm = P.mm(dpsum[pi], bd64b[:, tt * 16:(tt + 1) * 16], prodb[:, :],
                      start=(tt == 0), stop=(tt == 7), sem="pe", waits=dmw)
            qk_free[("prod",)] = ("pe", dm)
        qk_free[("q", "tile", 0)] = ("pe", P.cnt["pe"])
        qk_free[("q", "tile", 1)] = qk_free[("q", "tile", 0)]
        qk_free[("k", "tile", 0)] = qk_free[("q", "tile", 0)]
        qk_free[("k", "tile", 1)] = qk_free[("q", "tile", 0)]
    dots_pe = P.cnt["pe"]
    if ("rot", "bank") in qk_free:
        bf_[2] = qk_free.pop(("rot", "bank"))

    wvr_ready = P.dma("sync", wvrb[:, :], aps["wvr_in"], "hx",
                      waits=[("pe", dots_pe), ("dve", P.cnt["dve"])])

    lg_pe = []
    for half in range(2):
        bk = half
        wl = [wvr_ready, xc_ready]
        if bk in bf_:
            wl.append(bf_.pop(bk))
        if (("q", "bank") if bk == 0 else ("k", "bank")) in qk_free:
            wl.append(qk_free.pop(("q", "bank") if bk == 0 else ("k", "bank")))
        for dk in range(DK):
            P.mm(pb[bk][0:64, :], wvrb[:, dk * 64:(dk + 1) * 64],
                 xc[:, dk * T + half * 512: dk * T + (half + 1) * 512],
                 start=(dk == 0), stop=(dk == DK - 1),
                 waits=wl if dk == 0 else (),
                 sem="pe" if dk == DK - 1 else None)
        lg_pe.append(P.cnt["pe"])

    e0a = P.act(rtrA[:, 0:512], pb[0][0:16, :], AF.Exp, sem="act",
                waits=[("pe", lg_pe[0])])
    e0b = P.act(rtrA[:, 512:T], pb[1][0:16, :], AF.Exp, sem="act",
                waits=[("pe", lg_pe[1])])
    e1a = P.act(rtrB[:, 0:512], pb[0][32:48, :], AF.Exp, sem="act",
                waits=[("pe", lg_pe[0])])
    e1b = P.act(rtrB[:, 512:T], pb[1][32:48, :], AF.Exp, sem="act",
                waits=[("pe", lg_pe[1])])
    bf_[0] = ("act", e1a)
    bf_[1] = ("act", e1b)
    r1 = P.tt(rtrC[:, :], rtrA[:, :], rtrB[:, :], ALU.add, sem="dve",
              waits=[("act", e1b)])
    r2 = P.recip(rtrC[:, :], rtrC[:, :], sem="dve", waits=[("dve", r1)])
    r3 = P.tt(rtrD[:, :], rtrA[:, :], rtrB[:, :], ALU.is_ge, sem="dve",
              waits=[("dve", r2)])
    r4 = P.tt(rtrA[:, :], rtrA[:, :], rtrB[:, :], ALU.max, sem="dve",
              waits=[("dve", r3)])
    r5 = P.tt(rtrA[:, :], rtrA[:, :], rtrC[:, :], ALU.mult, sem="dve",
              waits=[("dve", r4)])
    r6 = P.tt(rtrB[:, 0:512], rtrD[:, 0:512], rtrD[:, 512:T], ALU.is_equal,
              sem="dve", waits=[("dve", r5)])
    r7 = P.tt(rtrC[:, 0:512], rtrB[:, 0:512], rtrA[:, 512:T], ALU.mult,
              sem="dve", waits=[("dve", r6)])
    r8 = P.tsc(rtrB[:, 512:T], rtrB[:, 0:512], 1.0, None, ALU.add, sem="dve",
               waits=[("dve", r7)])
    r8r = P.recip(rtrB[:, 512:T], rtrB[:, 512:T], sem="dve", waits=[("dve", r8)])
    r9 = P.tt(rtrC[:, 512:T], rtrA[:, 512:T], rtrB[:, 512:T], ALU.mult,
              sem="dve", waits=[("dve", r8r)])
    q1 = P.tt(rtrD[:, 0:512], rtrA[:, 0:512], dpsum[0], ALU.mult, sem="dve",
              waits=[("dve", r9), ("pe", dots_pe)])
    q2 = P.tsc(qkrow[:, 0:512], rtrD[:, 0:512], 0.125, None, ALU.mult,
               sem="dve", waits=[("dve", q1)])
    q3 = P.tt(rtrD[:, 512:T], rtrC[:, 0:512], dpsum[1], ALU.mult, sem="dve",
              waits=[("dve", q2)])
    q4 = P.tsc(qkrow[:, 512:1024], rtrD[:, 512:T], 0.0625, None, ALU.mult,
               sem="dve", waits=[("dve", q3)])
    q5 = P.tt(rtrD[:, 0:512], rtrC[:, 512:T], dpsum[2], ALU.mult, sem="dve",
              waits=[("dve", q4)])
    q6 = P.tsc(qkrow[:, 1024:1536], rtrD[:, 0:512], 0.125, None, ALU.mult,
               sem="dve", waits=[("dve", q5)])
    for bb in (3, 4, 5):
        bf_[bb] = ("dve", q6)
    qk_ready = ("dve", q6)

    vg_free = {}
    outf_done = []
    ldv_tok = [None] * 16
    ldv_tok[0] = P.dma("gpsimd", vgwb[:, :].rearrange("p (k c) -> p k c", k=16),
                       aps["wvf"][:, 0:128].rearrange("(k p) c -> p k c", p=128),
                       "w", waits=[("pe", P.cnt["pe"])])
    W2SLOT = env["W2SLOT"]
    w2b = env["w2b"]
    for h in range(16):
        ldv = ldv_tok[h]
        for half in range(2):
            bk = half
            wl = [ldv, xc_ready]
            if bk in bf_:
                wl.append(bf_.pop(bk))
            for k in range(DK):
                P.mm(pb[bk][:, :], vgwb[:, k * 128:(k + 1) * 128],
                     xc[:, k * T + half * 512: k * T + (half + 1) * 512],
                     start=(k == 0), stop=(k == DK - 1),
                     waits=wl if k == 0 else (),
                     sem="pe" if k == DK - 1 else None)
            pc = P.cnt["pe"]
            vw = [("pe", pc)]
            if ("vtile", half) in vg_free:
                vw.append(vg_free.pop(("vtile", half)))
            cv = P.copy(vtile[:, half * 512:(half + 1) * 512], pb[bk][:, :],
                        sem="dve", waits=vw)
            bf_[bk] = ("dve", cv)
        v_ready = P.cnt["dve"]
        vmm_done = ("pe", P.cnt["pe"])
        gslot = h % 2
        gwl = [("pe", slotuse[("Bw2", gslot)])] if ("Bw2", gslot) in slotuse else []
        ldg = P.dma("gpsimd",
                    w2b[:, gslot * W2SLOT: gslot * W2SLOT + 16 * 128].rearrange(
                        "p (k c) -> p k c", k=16),
                    aps["wgf"][:, 128 * h:(h + 1) * 128].rearrange(
                        "(k p) c -> p k c", p=128), "w", waits=gwl)
        if h < 15:
            ldv_tok[h + 1] = P.dma(
                "gpsimd", vgwb[:, :].rearrange("p (k c) -> p k c", k=16),
                aps["wvf"][:, 128 * (h + 1):128 * (h + 2)].rearrange(
                    "(k p) c -> p k c", p=128), "w", waits=[vmm_done])
        odst = [(o_t[:, 0:512], vtile[:, 0:512]),
                (sttl[:, 0:512], vtile[:, 0:512]),
                (tmp[:, 512:T], vtile[:, 512:T])]
        otoks = []
        for j in range(3):
            bk = 2 + j % 2
            wl = [qk_ready, ("dve", v_ready)]
            if h == 0 and j == 0 and "sel16_load" in env:
                wl.append(env["sel16_load"])
            if bk in bf_:
                wl.append(bf_.pop(bk))
            bm = P.mm(pb[bk][:, :], sel16[:, h * 128:(h + 1) * 128],
                      qkrow[:, j * 512:(j + 1) * 512], start=True, stop=True,
                      sem="pe", waits=wl)
            oj = P.tt(odst[j][0], pb[bk][:, :], odst[j][1], ALU.mult,
                      sem="dve", waits=[("pe", bm), ("dve", P.cnt["dve"])])
            otoks.append(oj)
            bf_[bk] = ("dve", oj)
        o4 = P.tt(o_t[:, 512:T], sttl[:, 0:512], tmp[:, 512:T], ALU.add,
                  sem="dve", waits=[("dve", otoks[2])])
        scr_row = ftl[0:1, T:2 * T]
        for half in range(2):
            base = 0 if half == 0 else 32
            w5 = [("dve", o4)]
            if half == 0:
                for bb in (6, 7):
                    if bb in bf_:
                        w5.append(bf_.pop(bb))
            sq1 = P.tt(sttl[:, half * 512:(half + 1) * 512],
                       o_t[:, half * 512:(half + 1) * 512],
                       o_t[:, half * 512:(half + 1) * 512], ALU.mult,
                       sem="dve", waits=[("dve", o4), ("dve", P.cnt["dve"])])
            P.mm(pb6[base:base + 1, :], onesc[:, :],
                 o_t[:, half * 512:(half + 1) * 512], start=True, stop=True,
                 waits=w5)
            P.mm(pb7[base:base + 1, :], onesc[:, :],
                 sttl[:, half * 512:(half + 1) * 512], start=True, stop=True,
                 waits=[("dve", sq1)], sem="pe")
        lnpc = P.cnt["pe"]
        rw0 = [("pe", lnpc), ("dve", P.cnt["dve"])]
        x1_ = P.act(mrow[:, 0:512], pb6[0:1, :], AF.Copy, scale=1.0 / 128,
                    sem="act", waits=rw0)
        x2_ = P.act(mrow[:, 512:T], pb6[32:33, :], AF.Copy, scale=1.0 / 128,
                    sem="act")
        x3_ = P.act(vrow[:, 0:512], pb7[0:1, :], AF.Copy, scale=1.0 / 128,
                    sem="act")
        x4_ = P.act(vrow[:, 512:T], pb7[32:33, :], AF.Copy, scale=1.0 / 128,
                    sem="act")
        bf_[6] = ("act", x4_)
        bf_[7] = ("act", x4_)
        v1_ = P.tt(scr_row, mrow[:, :], mrow[:, :], ALU.mult, sem="dve",
                   waits=[("act", x4_)])
        v2_ = P.tt(vrow[:, :], scr_row, vrow[:, :], ALU.subtract, sem="dve",
                   waits=[("dve", v1_)])
        a1_ = P.act(vrow[:, :], vrow[:, :], AF.Ln, bias=1e-5, scale=-1.0,
                    sem="act", waits=[("dve", v2_)])
        a2_ = P.act(vrow[:, :], vrow[:, :], AF.Exp, scale=-0.5, sem="act",
                    waits=[("act", a1_)])
        bdone = []
        for j, (srow, dstb) in enumerate(((mrow, sttl), (vrow, tmp))):
            for half in range(2):
                bk = 2 + (2 * j + half) % 2
                wl = [("act", a2_), ("dve", P.cnt["dve"])]
                if bk in bf_:
                    wl.append(bf_.pop(bk))
                if h == 0:
                    wl.append(lof)
                bm = P.mm(pb[bk][:, :], onesf_row[0:1, :],
                          srow[:, half * 512:(half + 1) * 512], start=True,
                          stop=True, sem="pe", waits=wl)
                bc2 = P.copy(dstb[:, half * 512:(half + 1) * 512], pb[bk][:, :],
                             sem="dve", waits=[("pe", bm)])
                bf_[bk] = ("dve", bc2)
                bdone.append(bc2)
        l1_ = P.tt(o_t[:, 0:512], o_t[:, 0:512], sttl[:, 0:512], ALU.subtract,
                   sem="dve", waits=[("dve", bdone[-1])])
        l2_ = P.tt(o_t[:, 512:T], o_t[:, 512:T], sttl[:, 512:T], ALU.subtract,
                   sem="dve", waits=[("dve", l1_)])
        l3_ = P.tt(o_t[:, 0:512], o_t[:, 0:512], tmp[:, 0:512], ALU.mult,
                   sem="dve", waits=[("dve", l2_)])
        l4_ = P.tt(o_t[:, 512:T], o_t[:, 512:T], tmp[:, 512:T], ALU.mult,
                   sem="dve", waits=[("dve", l3_)])
        for half in range(2):
            bk = half
            wl = [ldg]
            if bk in bf_:
                wl.append(bf_.pop(bk))
            for k in range(DK):
                off = gslot * W2SLOT + k * 128
                P.mm(pb[bk][:, :], w2b[:, off:off + 128],
                     xc[:, k * T + half * 512: k * T + (half + 1) * 512],
                     start=(k == 0), stop=(k == DK - 1),
                     waits=wl if k == 0 else (),
                     sem="pe" if k == DK - 1 else None)
            gpc = P.cnt["pe"]
            slotuse[("Bw2", gslot)] = gpc
            ag = P.act(sttl[:, half * 512:(half + 1) * 512], pb[bk][:, :],
                       AF.Silu, sem="act", waits=[("pe", gpc),
                                                  ("dve", l4_ if half == 0 else P.cnt["dve"])])
            tile = outf_tiles[h]
            if isinstance(tile, list):
                tdst = tile[half]
            else:
                tdst = tile[:, half * 512:(half + 1) * 512]
            ow = [("act", ag), ("dve", l4_)]
            if h < 8:
                g_ = max(slotuse.get(("l1q", 0), 0), slotuse.get(("l1q", 1), 0),
                         slotuse.get(("l1k", 0), 0), slotuse.get(("l1k", 1), 0))
                ow.append(("pe", g_))
            elif h in (11, 12, 13, 14, 15):
                ow.append(("pe", dots_pe))
            ot = P.tt(tdst,
                      sttl[:, half * 512:(half + 1) * 512],
                      o_t[:, half * 512:(half + 1) * 512], ALU.mult,
                      sem="dve", waits=ow)
            bf_[bk] = ("act", ag)
        vg_free["wdma"] = ("pe", P.cnt["pe"])
        vg_free[("vtile", 0)] = ("dve", P.cnt["dve"])
        vg_free[("vtile", 1)] = vg_free[("vtile", 0)]
        outf_done.append(P.cnt["dve"])
    outf_ready = ("dve", P.cnt["dve"])

    emit_phaseB(P, env, w2_ap=aps["woutf"], n_i=DK, rhs_buf=outf_tiles,
                d_chunk=1, first_into_acc=False, hready=outf_ready)
    env["l1ffn_done"] = ("pe", P.cnt["pe"])
    env["l1ffn_bufs"] = {"qtl": qtl, "ktl": ktl, "extf": extf, "vgwb": vgwb,
                         "convsb": convsb,
                         "wvrb": wvrb, "vtile": vtile, "qkb": qkb,
                         "prodb": prodb, "protb": protb, "bd64b": bd64b,
                         "onesf_row": onesf_row, "onesf_load": lof,
                         "pb6": pb6, "pb7": pb7}


def emit_l1attn(P, env, aps):
    nc = P.nc
    EPS = 1e-8
    xnb = env["xnb"]; pb = env["psum"]; bf_ = env["bankfree"]
    slotuse = env["slotuse"]
    hbuf = env["h"]; sil = env["sil"]; tmp = env["tmp"]; sttl = env["sttl"]
    ftl = env["ftl"]; gtl = env["gtl"]
    w1b = env["w1b"]; w3b = env["w3b"]; w2b = env["w2b"]
    onesb = env["ones_sb"]; onesc = env["onescol_sb"]; ident = env["ident"]
    l1f = env["l1ffn_done"]
    lf = env["l1ffn_bufs"]
    qtl, ktl, extf, vgwb, wvrb, vtile, qkb, prodb, protb, bd64b, onesf_row = (
        lf["qtl"], lf["ktl"], lf["extf"], lf["vgwb"], lf["wvrb"], lf["vtile"],
        lf["qkb"], lf["prodb"], lf["protb"], lf["bd64b"], lf["onesf_row"])
    pb6, pb7 = lf["pb6"], lf["pb7"]
    convsb = lf["convsb"]
    kbuf = hbuf[:, 0:4 * T]
    vbuf = hbuf[:, 4 * T:8 * T]
    mknbuf = hbuf[:, 8 * T:8 * T + 6144]
    mkf_tiles = [qtl[:, :], ktl[:, :], extf[:, 0:T],
                 env["cbB"][:, :], gtl[:, 0:T], gtl[:, T:2 * T]]
    wmvt = qkb[:, 0:1024]
    roll_slices = [prodb[:, 0:128], prodb[:, 128:256], prodb[:, 256:384],
                   prodb[:, 384:512], qkb[:, 1024:1152], qkb[:, 1152:1280]]
    if "apsb" in env:
        biasb = env["apsb"][:, 0:72]
    else:
        biasb = nc.alloc_sbuf_tensor("biasb", [128, 72], F32)
    normTb = qkb[:, 1280:1280 + 96]
    dnormb = None
    coeftb = qkb[:, 1376:1376 + 32]
    memb = vgwb
    mknrow = ftl[0:1, 0:T]
    mknrow16 = gtl[0:1, T:T + T]
    sqrow = ftl[32:33, 0:T]

    lro_toks = []
    for ri in range(6):
        lro_toks.append(P.dma("gpsimd", roll_slices[ri],
                              aps["rollc"][:, ri * 128:(ri + 1) * 128],
                              "x", waits=[l1f]))
    lbi = P.dma("sync", biasb[:, 0:4], aps["bk_in"], "hx")
    lbv = P.dma("sync", biasb[:, 4:8], aps["bv_in"], "hx")
    lbmb = P.dma("sync", biasb[:, 8:24], aps["bmb_in"], "hx")
    lbq = P.dma("sync", biasb[:, 24:40], aps["bq_in"], "hx")
    lbg = P.dma("sync", biasb[:, 40:56], aps["bg_in"], "hx")
    lnt = P.dma("gpsimd", normTb[:, :].rearrange("p (k c) -> p k c", k=6),
                aps["normT_in"].rearrange("(k p) c -> p k c", p=128), "w")
    nb = P.tsc(biasb[:, 56:72], biasb[:, 24:40], -1.0, None, ALU.mult,
               sem="dve", waits=[lbq, lbg])

    def kv_evac(name, boff):
        btok = lbi if name == "k" else lbv
        def cb(mt, half, bank, pecnt):
            dst = (kbuf if name == "k" else vbuf)
            tok = P.tsc(dst[:, mt * T + half * 512: mt * T + (half + 1) * 512],
                        bank[:, :], biasb[:, boff + mt:boff + mt + 1], None,
                        ALU.add, sem="dve", waits=[("pe", pecnt), btok,
                                                   l1f if mt + half == 0 else ("pe", 0)])
            return ("dve", tok)
        return cb
    slotuse[("wkT", 0)] = max(slotuse.get(("Bw2", 0), 0), env["l1ffn_done"][1])
    slotuse[("wkT", 1)] = slotuse[("wkT", 0)]
    slotuse[("wvT", 0)] = slotuse[("wkT", 0)]
    slotuse[("wvT", 1)] = slotuse[("wkT", 0)]
    slotuse[("knw", 0)] = slotuse[("wkT", 0)]
    slotuse[("knw", 1)] = slotuse[("wkT", 0)]
    emit_projA(P, env, w_ap=aps["wkT"], n_m=4, evac_cb=kv_evac("k", 0),
               tagkey="wkT", chunk=1, banks=(0, 1), rhs_ready=None,
               wbuf=w2b, wslot=2048)
    for s_ in (0, 1):
        slotuse[("wvT", s_)] = max(slotuse.get(("wvT", s_), 0),
                                   slotuse.get(("wkT", 0), 0),
                                   slotuse.get(("wkT", 1), 0))
    emit_projA(P, env, w_ap=aps["wvT"], n_m=4, evac_cb=kv_evac("v", 4),
               tagkey="wvT", chunk=1, banks=(0, 1), rhs_ready=None,
               wbuf=w2b, wslot=2048)

    bkB = sil[:, 0:512]
    wl = [("dve", P.cnt["dve"])]
    if 2 in bf_:
        wl.append(bf_.pop(2))
    lkr = P.dma("sync", sttl[0:1, 0:512], aps["bkrow"], "hx", waits=[l1f])
    bm = P.mm(pb[2][:, :], onesf_row[0:1, :], sttl[0:1, 0:512], start=True,
              stop=True, sem="pe", waits=wl + [lkr, lf["onesf_load"]])
    bkc = P.copy(bkB, pb[2][:, :], sem="dve", waits=[("pe", bm)])
    bf_[2] = ("dve", bkc)
    kn_tiles = [w1b[:, i * 512:(i + 1) * 512] for i in range(8)]
    for grp in range(2):
        wdc = {}
        for k in range(DK):
            slot = k % 2
            wl = []
            if ("knw", slot) in slotuse:
                wl.append(("pe", slotuse[("knw", slot)]))
            wdc[k] = P.dma("gpsimd", w3b[:, slot * 2048:slot * 2048 + 512],
                           aps["wkT"][k * 128:(k + 1) * 128, :], "w", waits=wl)
            for ti in range(4):
                tt = grp * 4 + ti
                bk_ = 2 + ti
                wl2 = [wdc[k]] if ti == 0 else []
                if k == 0 and bk_ in bf_:
                    wl2.append(bf_.pop(bk_))
                if k == 0 and ti == 0 and grp == 0:
                    wl2.append(env["xn1_ready"])
                P.mm(pb[bk_][:, :], xnb[:, k * T + tt * 128:k * T + (tt + 1) * 128],
                     w3b[:, slot * 2048:slot * 2048 + 512],
                     start=(k == 0), stop=(k == DK - 1),
                     waits=wl2,
                     sem="pe" if ti == 3 else None)
            slotuse[("knw", slot)] = P.cnt["pe"]
        pcg = P.cnt["pe"]
        for ti in range(4):
            tt = grp * 4 + ti
            tok = P.tt(kn_tiles[tt], pb[2 + ti][:, :], bkB, ALU.add, sem="dve",
                       waits=[("pe", pcg), ("dve", bkc)])
            bf_[2 + ti] = ("dve", tok)
    kn_ready = P.cnt["dve"]
    bf_[("mbw", 0)] = ("pe", P.cnt["pe"])
    bf_[("mbw", 1)] = ("pe", P.cnt["pe"])
    bf_[("memd", 0)] = env["l1ffn_done"]
    bf_[("memd", 1)] = env["l1ffn_done"]

    for hk in range(4):
        for tt in range(8):
            xcat = w2b[:, 2048:2304]
            xcat2 = w2b[:, 2304:2816]
            src = kn_tiles[tt][:, hk * 128:(hk + 1) * 128]
            awl = [("dve", kn_ready), ("dve", P.cnt["dve"]),
                   ("act", P.cnt["act"]), ("pe", P.cnt["pe"])]
            a1 = P.act(xcat[:, 0:128], src, AF.Relu, sem="act", waits=awl)
            a2 = P.act(xcat[:, 128:256], src, AF.Relu, scale=-1.0, sem="act",
                       waits=[("act", a1)])
            c1 = P.tt(xcat2[:, 0:256], xcat[:, :], xcat[:, :], ALU.max,
                      sem="dve", waits=[("act", a2)])
            c2 = P.tt(xcat2[:, 256:512], xcat[:, :], xcat[:, :], ALU.max,
                      sem="dve", waits=[("act", a2), ("dve", c1)])
            mkr = sil[:, 0:768]
            for j in (1, 2, 3):
                P.tt(mkr[:, (j - 1) * 256:j * 256], xcat[:, :],
                     xcat2[:, 256 - j:512 - j], ALU.mult, sem="dve",
                     waits=[("dve", c2), ("dve", P.cnt["dve"])])
            m1 = P.cnt["dve"]
            sqw = [("dve", m1)]
            if ("tmpd",) in bf_:
                sqw.append(bf_.pop(("tmpd",)))
            sq_ = P.tt(tmp[:, 0:768], mkr[:, :], mkr[:, :], ALU.mult, sem="dve",
                       waits=sqw)
            r1 = P.reduce(sttl[:, 0:1], tmp[:, 0:768], ALU.add, sem="dve",
                          waits=[("dve", sq_)])
            s1 = P.act(sttl[:, 1:2], sttl[:, 0:1], AF.Ln, sem="act",
                       waits=[("dve", r1)])
            s2 = P.act(sttl[:, 1:2], sttl[:, 1:2], AF.Exp, scale=0.5, sem="act",
                       waits=[("act", s1)])
            s3 = P.tsc(sttl[:, 1:2], sttl[:, 1:2], 1e-12, None, ALU.max,
                       sem="dve", waits=[("act", s2)])
            s4 = P.recip(sttl[:, 2:3], sttl[:, 1:2], sem="dve", waits=[("dve", s3)])
            mk_ = P.tsc(mknbuf[:, tt * 768:(tt + 1) * 768], mkr[:, :],
                        sttl[:, 2:3], None, ALU.mult, sem="dve",
                        waits=[("dve", s4)])
            P.copy(sttl[:, 3 + tt:4 + tt], sttl[:, 1:2], sem="dve",
                   waits=[("dve", s4), ("dve", P.cnt["dve"])])
        sq8 = P.cnt["dve"]
        for tt in range(8):
            bk_ = (tt % 2)
            wl = [("dve", sq8)]
            if bk_ in bf_:
                wl.append(bf_.pop(bk_))
            tc = P.transpose(pb[bk_][0:1, 0:128], sttl[:, 3 + tt:4 + tt],
                             ident[:, :], sem="pe", waits=wl)
            cc = P.copy(mknrow[:, tt * 128:(tt + 1) * 128], pb[bk_][0:1, 0:128],
                        sem="dve", waits=[("pe", tc)])
            bf_[bk_] = ("dve", cc)
        mkn_done = P.cnt["dve"]

        c0t = w2b[:, 0:T]; c1t = w2b[:, T:2 * T]
        ra = P.act(c0t, kbuf[:, hk * T:(hk + 1) * T], AF.Relu, sem="act",
                   waits=[("dve", P.cnt["dve"]), ("pe", P.cnt["pe"])])
        rb = P.act(c1t, kbuf[:, hk * T:(hk + 1) * T], AF.Relu, scale=-1.0,
                   sem="act", waits=[("act", ra)])
        for j in (1, 2, 3):
            for half_t in range(2):
                dst = mkf_tiles[(j - 1) * 2 + half_t]
                bk_ = 2 + (j + half_t) % 2
                src_main = c0t if half_t == 0 else c1t
                src_wrap = c1t if half_t == 0 else c0t
                wl = [("act", rb), ("dve", P.cnt["dve"])]
                if hk == 0:
                    wl.extend(lro_toks)
                if bk_ in bf_:
                    wl.append(bf_.pop(bk_))
                tok = None
                for hh in range(2):
                    wlh = wl if hh == 0 else [("dve", tok)]
                    P.mm(pb[bk_][:, 0:512], roll_slices[2 * (j - 1)],
                         src_main[:, hh * 512:(hh + 1) * 512], start=True, stop=False,
                         waits=wlh)
                    P.mm(pb[bk_][:, 0:512], roll_slices[2 * (j - 1) + 1],
                         src_wrap[:, hh * 512:(hh + 1) * 512], start=False, stop=True,
                         sem="pe")
                    pc = P.cnt["pe"]
                    x_rep = c0t if half_t == 0 else c1t
                    tok = P.tt(dst[:, hh * 512:(hh + 1) * 512],
                               x_rep[:, hh * 512:(hh + 1) * 512],
                               pb[bk_][:, 0:512], ALU.mult, sem="dve",
                               waits=[("pe", pc)])
                bf_[bk_] = ("dve", tok)
        mkf_done = ("dve", P.cnt["dve"])

        den4 = sil[32:36, 0:T]
        rec4 = sil[64:68, 0:T]
        coef4 = sil[0:4, 0:T]
        nbc4 = tmp[32:36, 0:T]
        for half in range(2):
            bk_ = 4 + half
            wl = [mkf_done, lnt]
            if bk_ in bf_:
                wl.append(bf_.pop(bk_))
            for fb in range(6):
                P.mm(pb[bk_][0:4, 0:512],
                     normTb[:, fb * 16 + 4 * hk: fb * 16 + 4 * hk + 4],
                     mkf_tiles[fb][:, half * 512:(half + 1) * 512],
                     start=(fb == 0), stop=(fb == 5),
                     waits=wl if fb == 0 else (),
                     sem="pe" if fb == 5 else None)
            dpc = P.cnt["pe"]
            bk2 = 2 + half
            wl2 = [("dve", mkn_done)]
            if bk2 in bf_:
                wl2.append(bf_.pop(bk2))
            bmn = P.mm(pb[bk2][0:4, 0:512], onesf_row[0:1, 0:4],
                       mknrow[:, half * 512:(half + 1) * 512], start=True,
                       stop=True, sem="pe", waits=wl2)
            cnb = P.copy(nbc4[:, half * 512:(half + 1) * 512], pb[bk2][0:4, :],
                         sem="dve", waits=[("pe", bmn), ("dve", P.cnt["dve"])])
            bf_[bk2] = ("dve", cnb)
            da = P.stt(den4[:, half * 512:(half + 1) * 512],
                       nbc4[:, half * 512:(half + 1) * 512], 1e-8,
                       pb[bk_][0:4, 0:512], ALU.mult, ALU.add, sem="dve",
                       waits=[("pe", dpc), ("dve", cnb)])
            bf_[bk_] = ("dve", da)
        dd = P.cnt["dve"]
        r4 = P.recip(rec4[:, :], den4[:, :], sem="dve", waits=[("dve", dd)])
        r4b = P.copy(sttl[0:4, 0:T], rec4[:, :], sem="dve", waits=[("dve", r4)])
        rec16 = mknrow16
        rec16 = vtile[0:4, 0:T]
        rc16 = P.copy(rec16, rec4[:, :], sem="dve", waits=[("dve", r4)])
        cf0 = P.recip(nbc4[:, :], nbc4[:, :], sem="dve",
                      waits=[("dve", dd), ("dve", r4b)])
        cf1 = P.tt(coef4[:, :], den4[:, :], nbc4[:, :], ALU.mult, sem="dve",
                   waits=[("dve", cf0)])
        cf2 = P.tsc(coef4[:, :], coef4[:, :], -1.0, 1.0, ALU.mult, ALU.add,
                    sem="dve", waits=[("dve", cf1)])
        cf3 = P.tsc(coef4[:, :], coef4[:, :], 0.0, 1.0, ALU.max, ALU.min,
                    sem="dve", waits=[("dve", cf2)])
        for tt in range(8):
            bk_ = tt % 2
            wl = [("dve", cf3)]
            if bk_ in bf_:
                wl.append(bf_.pop(bk_))
            tc = P.transpose(pb[bk_][0:128, 0:4], coef4[:, tt * 128:(tt + 1) * 128],
                             ident[0:4, 0:4], sem="pe", waits=wl)
            cc = P.copy(coeftb[:, tt * 4:(tt + 1) * 4], pb[bk_][0:128, 0:4],
                        sem="dve", waits=[("pe", tc)])
            bf_[bk_] = ("dve", cc)
        coeft_done = P.cnt["dve"]
        for fb in range(6):
            bk_ = 6 if fb % 2 == 0 else 7
            pbx = pb6 if fb % 2 == 0 else pb7
            wl = [("dve", coeft_done)]
            if ("d67", bk_) in bf_:
                wl.append(bf_.pop(("d67", bk_)))
            for tt in range(8):
                P.mm(pbx[:, 0:4], mknbuf[:, tt * 768 + fb * 128: tt * 768 + (fb + 1) * 128],
                     coeftb[:, tt * 4:(tt + 1) * 4], start=(tt == 0), stop=(tt == 7),
                     waits=wl if tt == 0 else (),
                     sem="pe" if tt == 7 else None)
            pcd = P.cnt["pe"]
            dnw = [("pe", pcd)]
            if ("dnst",) in bf_ and fb == 0:
                dnw.append(bf_.pop(("dnst",)))
            dn = P.copy(convsb[:, fb * 4:(fb + 1) * 4], pbx[:, 0:4], sem="dve",
                        waits=dnw)
            bf_[("d67", bk_)] = ("dve", dn)
        dnst = P.dma("sync",
                     aps["dnorm_out"][:, hk * 4:(hk + 1) * 4].rearrange(
                         "(k p) g -> p k g", p=128),
                     convsb[:, 0:24].rearrange("p (k g) -> p k g", k=6), "st",
                     waits=[("dve", P.cnt["dve"])])
        bf_[("dnst",)] = dnst

        for hq in range(4):
            h = hk * 4 + hq
            mslot = h % 2
            wl = [bf_.pop(("memd", mslot))] if ("memd", mslot) in bf_ else []
            lmem = P.dma("gpsimd", memb[:, mslot * 768:(mslot + 1) * 768]
                         .rearrange("p (k c) -> p k c", k=6),
                         aps["memC"][h].rearrange("(k p) c -> p k c", p=128),
                         "w", waits=wl)
            wl = [bf_.pop(("mbw", mslot))] if ("mbw", mslot) in bf_ else []
            lmb = P.dma("gpsimd", w3b[:, mslot * 2048: mslot * 2048 + 2048]
                        .rearrange("p (k c) -> p k c", k=16),
                        aps["wmbT"][:, h * 128:(h + 1) * 128].rearrange(
                            "(k p) c -> p k c", p=128), "w", waits=wl)
            for half in range(2):
                bk_ = 2 + half
                wl = [lmem, mkf_done]
                if bk_ in bf_:
                    wl.append(bf_.pop(bk_))
                for fb in range(6):
                    P.mm(pb[bk_][:, :], memb[:, mslot * 768 + fb * 128: mslot * 768 + (fb + 1) * 128],
                         mkf_tiles[fb][:, half * 512:(half + 1) * 512],
                         start=(fb == 0), stop=(fb == 5),
                         waits=wl if fb == 0 else ())
                bk2 = 4 + half
                wl2 = [lmb]
                if bk2 in bf_:
                    wl2.append(bf_.pop(bk2))
                for k in range(DK):
                    P.mm(pb[bk2][:, :], w3b[:, mslot * 2048 + k * 128: mslot * 2048 + (k + 1) * 128],
                         xnb[:, k * T + half * 512: k * T + (half + 1) * 512],
                         start=(k == 0), stop=(k == DK - 1),
                         waits=wl2 if k == 0 else (),
                         sem="pe" if k == DK - 1 else None)
                pcb = P.cnt["pe"]
                bk3 = 0 + half
                wl3 = [("dve", rc16)]
                if bk3 in bf_:
                    wl3.append(bf_.pop(bk3))
                sb_ = P.mm(pb[bk3][:, :], env["sel16"][0:4, hq * 128:(hq + 1) * 128],
                           rec16[:, half * 512:(half + 1) * 512], start=True,
                           stop=True, sem="pe", waits=wl3)
                rb_ = P.copy(sttl[:, half * 512:(half + 1) * 512], pb[bk3][:, :],
                             sem="dve", waits=[("pe", sb_), ("dve", P.cnt["dve"])])
                bf_[bk3] = ("dve", rb_)
                pvw = [("dve", rb_)]
                if ("tmpd",) in bf_:
                    pvw.append(bf_.pop(("tmpd",)))
                pv = P.tt(tmp[:, half * 512:(half + 1) * 512], pb[bk_][:, :],
                          sttl[:, half * 512:(half + 1) * 512], ALU.mult,
                          sem="dve", waits=pvw)
                bf_[bk_] = ("dve", pv)
                mv = P.tt(tmp[:, half * 512:(half + 1) * 512],
                          vbuf[:, hk * T + half * 512: hk * T + (half + 1) * 512],
                          tmp[:, half * 512:(half + 1) * 512], ALU.subtract,
                          sem="dve", waits=[("dve", pv)])
                am = P.act(sil[:, half * 512:(half + 1) * 512], pb[bk2][:, :],
                           AF.Sigmoid, bias=biasb[:, 8 + h:9 + h], sem="act",
                           waits=[("pe", pcb), ("dve", P.cnt["dve"]), lbmb])
                bf_[bk2] = ("act", am)
                wv_ = P.tt(ftl[:, T + half * 512:T + (half + 1) * 512],
                           tmp[:, half * 512:(half + 1) * 512],
                           sil[:, half * 512:(half + 1) * 512], ALU.mult,
                           sem="dve", waits=[("act", am), ("dve", mv)])
            wmv_done = P.cnt["dve"]
            for tt in range(8):
                bk_ = tt % 2
                wl = [("dve", wmv_done)]
                if bk_ in bf_:
                    wl.append(bf_.pop(bk_))
                tc = P.transpose(pb[bk_][:, 0:128],
                                 ftl[:, T + tt * 128:T + (tt + 1) * 128],
                                 ident[:, :], sem="pe", waits=wl)
                cc = P.copy(wmvt[:, tt * 128:(tt + 1) * 128], pb[bk_][:, 0:128],
                            sem="dve", waits=[("pe", tc)])
                bf_[bk_] = ("dve", cc)
            wmvt_done = P.cnt["dve"]
            for fb in range(6):
                bk_ = 2 + fb % 2
                wl = [("dve", wmvt_done)]
                if bk_ in bf_:
                    wl.append(bf_.pop(bk_))
                for tt in range(8):
                    P.mm(pb[bk_][:, 0:128],
                         mknbuf[:, tt * 768 + fb * 128: tt * 768 + (fb + 1) * 128],
                         wmvt[:, tt * 128:(tt + 1) * 128],
                         start=(tt == 0), stop=(tt == 7),
                         waits=wl if tt == 0 else (),
                         sem="pe" if tt == 7 else None)
                pcd = P.cnt["pe"]
                dc_ = P.copy(sil[:, fb * 128:(fb + 1) * 128] if False else
                             tmp[:, fb * 128:(fb + 1) * 128],
                             pb[bk_][:, 0:128], sem="dve", waits=[("pe", pcd)])
                bf_[bk_] = ("dve", dc_)
            dstore = P.dma("sync", aps["dmem_out"][h].rearrange("(k p) c -> p k c", p=128),
                           tmp[:, 0:768].rearrange("p (k c) -> p k c", k=6), "st",
                           waits=[("dve", P.cnt["dve"])])
            bf_[("tmpd",)] = dstore
            bf_[("memd", mslot)] = ("pe", P.cnt["pe"])
            bf_[("mbw", mslot)] = ("pe", P.cnt["pe"])


    for sft in (0, 1):
        if ("mbw", sft) in bf_:
            bf_[("mqw", sft)] = bf_.pop(("mbw", sft))
    for h in range(16):
        mslot = h % 2
        wl = [bf_.pop(("mqw", mslot))] if ("mqw", mslot) in bf_ else []
        lq = P.dma("gpsimd", w3b[:, mslot * 2048: mslot * 2048 + 2048]
                   .rearrange("p (k c) -> p k c", k=16),
                   aps["wqT2"][:, h * 128:(h + 1) * 128].rearrange(
                       "(k p) c -> p k c", p=128), "w", waits=wl)
        c0t = w2b[:, 0:T]; c1t = w2b[:, T:2 * T]
        for half in range(2):
            bk_ = 0 + half
            wl = [lq]
            if bk_ in bf_:
                wl.append(bf_.pop(bk_))
            for k in range(DK):
                P.mm(pb[bk_][:, :], w3b[:, mslot * 2048 + k * 128: mslot * 2048 + (k + 1) * 128],
                     xnb[:, k * T + half * 512: k * T + (half + 1) * 512],
                     start=(k == 0), stop=(k == DK - 1),
                     waits=wl if k == 0 else (),
                     sem="pe" if k == DK - 1 else None)
            pcq = P.cnt["pe"]
            aw = [("pe", pcq), ("dve", P.cnt["dve"]), ("act", P.cnt["act"])]
            a1 = P.act(c0t[:, half * 512:(half + 1) * 512], pb[bk_][:, :],
                       AF.Relu, bias=biasb[:, 24 + h:25 + h], sem="act", waits=aw)
            a2 = P.act(c1t[:, half * 512:(half + 1) * 512], pb[bk_][:, :],
                       AF.Relu, bias=biasb[:, 56 + h:57 + h], scale=-1.0,
                       sem="act", waits=[("act", a1)])
            bf_[bk_] = ("act", a2)
        relu_done = P.cnt["act"]
        for j in (1, 2, 3):
            for half_t in range(2):
                dst = mkf_tiles[(j - 1) * 2 + half_t]
                bk_ = 2 + (j + half_t) % 2
                src_main = c0t if half_t == 0 else c1t
                src_wrap = c1t if half_t == 0 else c0t
                wl = [("act", relu_done), ("dve", P.cnt["dve"])]
                if h == 0:
                    wl.extend(lro_toks)
                if ("mkf_war",) in bf_:
                    wl.extend(bf_.pop(("mkf_war",)))
                if bk_ in bf_:
                    wl.append(bf_.pop(bk_))
                tok = None
                for hh in range(2):
                    wlh = wl if hh == 0 else [("dve", tok)]
                    P.mm(pb[bk_][:, 0:512], roll_slices[2 * (j - 1)],
                         src_main[:, hh * 512:(hh + 1) * 512], start=True,
                         stop=False, waits=wlh)
                    P.mm(pb[bk_][:, 0:512], roll_slices[2 * (j - 1) + 1],
                         src_wrap[:, hh * 512:(hh + 1) * 512], start=False,
                         stop=True, sem="pe")
                    pc = P.cnt["pe"]
                    x_rep = c0t if half_t == 0 else c1t
                    tok = P.tt(dst[:, hh * 512:(hh + 1) * 512],
                               x_rep[:, hh * 512:(hh + 1) * 512],
                               pb[bk_][:, 0:512], ALU.mult, sem="dve",
                               waits=[("pe", pc)])
                bf_[bk_] = ("dve", tok)
        mq_done = P.cnt["dve"]
        for ti in range(6):
            sq_ = P.tt(sil[:, 0:T], mkf_tiles[ti][:, :], mkf_tiles[ti][:, :],
                       ALU.mult, sem="dve",
                       waits=[("dve", mq_done), ("dve", P.cnt["dve"]),
                              ("pe", P.cnt["pe"])])
            for half in range(2):
                bk_ = 4 + half
                wl = [("dve", sq_)]
                if ti == 0 and bk_ in bf_:
                    wl.append(bf_.pop(bk_))
                P.mm(pb[bk_][0:1, :], onesc[:, :],
                     sil[:, half * 512:(half + 1) * 512],
                     start=(ti == 0), stop=(ti == 5),
                     waits=wl, sem="pe" if half == 1 else None)
        pcs = P.cnt["pe"]
        sqw1 = [("pe", pcs)]
        if ("sqst",) in bf_:
            sqw1.append(bf_.pop(("sqst",)))
        sq1 = P.act(sqrow[:, 0:512], pb[4][0:1, :], AF.Copy, sem="act",
                    waits=sqw1)
        sq2 = P.act(sqrow[:, 512:T], pb[5][0:1, :], AF.Copy, sem="act",
                    waits=[("pe", pcs)])
        bf_[4] = ("act", sq2)
        bf_[5] = ("act", sq2)
        sst = P.dma("sync", aps["sqmq_out"][0:1, h * T:(h + 1) * T], sqrow[:, :],
                    "st", waits=[("act", sq2)])
        bf_[("sqst",)] = sst
        mqs = []
        for ti in range(6):
            mqs.append(P.dma("sync", aps["mq_out"][h, ti], mkf_tiles[ti][:, :],
                             "st", waits=[("dve", mq_done)]))
        bf_[("mqw", mslot)] = ("pe", P.cnt["pe"])
        bf_[("mkf_war",)] = mqs

    gstok = {}
    def g_evac(mt, half, bank, pecnt):
        aw = [("pe", pecnt), ("act", P.cnt["act"])]
        if half in gstok:
            aw.append(gstok.pop(half))
        ag = P.act(prodb[:, half * 256:(half + 1) * 256][:, 0:0] if False else
                   prodb[:, 0:512] if half == 0 else qkb[:, 1024:1536],
                   bank[:, :], AF.Sigmoid,
                   bias=biasb[:, 40 + mt:41 + mt], sem="act", waits=aw)
        src = prodb[:, 0:512] if half == 0 else qkb[:, 1024:1536]
        gs = P.dma("sync", aps["gate_out"][mt, :, half * 512:(half + 1) * 512],
                   src, "st", waits=[("act", ag)])
        gstok[half] = gs
        return gs
    slotuse[("wgT2", 0)] = P.cnt["pe"]
    slotuse[("wgT2", 1)] = P.cnt["pe"]
    emit_projA(P, env, w_ap=aps["wgT2"], n_m=DK, evac_cb=g_evac, tagkey="wgT2",
               chunk=1, banks=(0, 1), rhs_ready=None, wbuf=w2b, wslot=2048)

    xTb = env["acc"]
    for dk in range(DK):
        P.tt(xTb[:, dk * T:(dk + 1) * T], xTb[:, dk * T:(dk + 1) * T],
             xnb[:, dk * T:(dk + 1) * T], ALU.add, sem="dve",
             waits=[("dve", P.cnt["dve"]), ("pe", P.cnt["pe"])])
    xp = P.cnt["dve"]
    P.dma("sync", aps["xpart_out"].rearrange("k p c -> p k c"),
          xTb[:, :].rearrange("p (k c) -> p k c", k=DK), "st",
          waits=[("dve", xp)])


def emit_launchC(P, aps):
    nc = P.nc
    EPS = 1e-8
    xT = nc.alloc_sbuf_tensor("xTc", [128, DK * T], F32)
    memnew = nc.alloc_sbuf_tensor("memnew", [128, 2 * 768], BF16)
    dsum = nc.alloc_sbuf_tensor("dsum", [128, 768], F32)
    dload = nc.alloc_sbuf_tensor("dload", [128, 2 * 768], F32)
    nnorm = nc.alloc_sbuf_tensor("nnorm", [128, 96], F32)
    nnorm16 = nc.alloc_sbuf_tensor("nnorm16", [128, 96], BF16)
    mqb = nc.alloc_sbuf_tensor("mqb", [128, 2 * 6 * T], BF16)
    gtb = nc.alloc_sbuf_tensor("gtb", [128, 2 * T], BF16)
    sqr = nc.alloc_sbuf_tensor("sqr", [1, T], F32)
    drow = nc.alloc_sbuf_tensor("drow", [1, T], F32)
    rrow16 = nc.alloc_sbuf_tensor("rrow16", [1, T], BF16)
    rcpB = nc.alloc_sbuf_tensor("rcpB", [128, T], F32)
    memh = nc.alloc_sbuf_tensor("memh", [128, T], F32)
    onat = nc.alloc_sbuf_tensor("onat", [128, D], F32)
    identnp = nc.inline_tensor(np.eye(128, dtype=np.float32), name="identC")
    ident = nc.alloc_sbuf_tensor("identC_sb", [128, 128], F32)
    onesfnp = nc.inline_tensor(np.ones((1, 128), np.float32), name="onesfC")
    onesf = nc.alloc_sbuf_tensor("onesfC_sb", [1, 128], F32)
    onesbnp = nc.inline_tensor(np.ones((1, 128), np.float32).astype(ml_dtypes.bfloat16),
                               name="onesbC")
    onesb = nc.alloc_sbuf_tensor("onesbC_sb", [1, 128], BF16)
    pb = [nc.alloc_psum_tensor(f"pc{i}", [128, 512], F32) for i in range(8)]
    bf_ = {}

    lx = P.dma("sync", xT[:, :].rearrange("p (k c) -> p k c", k=DK),
               aps["xpart_in"].rearrange("k p c -> p k c"), "hx")
    lid = P.dma("sync", ident[:, :], identnp.ap(), "hx")
    lof = P.dma("sync", onesf[:, :], onesfnp.ap(), "hx")
    lob = P.dma("gpsimd", onesb[:, :], onesbnp.ap(), "x")

    t1 = None
    for c in range(8):
        sl = c % 2
        ln_ = P.dma("sync", nnorm16[:, :].rearrange("p (k g) -> p k g", k=6) if False
                    else dload[:, sl * 768:sl * 768 + 96].rearrange(
                        "p (k g) -> p k g", k=6),
                    aps["dnparts"][c].rearrange("(k p) g -> p k g", p=128), "hx",
                    waits=[bf_.pop(("dn", sl))] if ("dn", sl) in bf_ else [])
        if c == 0:
            t1 = P.copy(nnorm[:, :], dload[:, 0:96], sem="dve", waits=[ln_])
        else:
            t1 = P.tt(nnorm[:, :], nnorm[:, :], dload[:, sl * 768:sl * 768 + 96],
                      ALU.add, sem="dve", waits=[ln_, ("dve", t1)])
        bf_[("dn", sl)] = ("dve", t1)
    lnt = P.dma("gpsimd", dload[:, 768:768 + 96].rearrange("p (k g) -> p k g", k=6),
                aps["normT_in"].rearrange("(k p) g -> p k g", p=128), "w",
                waits=[bf_.pop(("dn", 1))])
    t2 = P.tt(nnorm[:, :], nnorm[:, :], dload[:, 768:768 + 96], ALU.add,
              sem="dve", waits=[lnt, ("dve", t1)])
    t3 = P.copy(nnorm16[:, :], nnorm[:, :], sem="dve", waits=[("dve", t2)])
    nn_done = ("dve", t3)
    bf_.pop(("dn", 0), None)

    sqld = P.dma("sync", drow[0:1, :], aps["sqmq_in"][0:1, 0:T], "hx") if False else None
    hstore = []
    for h in range(16):
        sl = h % 2
        tok = None
        for c in range(8):
            wl = [bf_.pop(("dl", sl, c % 2))] if ("dl", sl, c % 2) in bf_ else []
            ld = P.dma("sync", dload[:, (c % 2) * 768:(c % 2) * 768 + 768]
                       .rearrange("p (k c) -> p k c", k=6),
                       aps["dparts"][c, h].rearrange("(k p) c -> p k c", p=128),
                       "hx", waits=wl)
            if c == 0:
                tok = P.copy(dsum[:, :], dload[:, 0:768], sem="dve",
                             waits=[ld, ("dve", P.cnt["dve"])])
            else:
                tok = P.tt(dsum[:, :], dsum[:, :],
                           dload[:, (c % 2) * 768:(c % 2) * 768 + 768], ALU.add,
                           sem="dve", waits=[ld, ("dve", tok)])
            bf_[("dl", sl, c % 2)] = ("dve", tok)
        lm = P.dma("gpsimd", memnew[:, sl * 768:(sl + 1) * 768]
                   .rearrange("p (k c) -> p k c", k=6),
                   aps["memC"][h].rearrange("(k p) c -> p k c", p=128), "w",
                   waits=[bf_.pop(("mem", sl))] if ("mem", sl) in bf_ else [])
        lmf = P.dma("sync", dload[:, (0) * 768:768].rearrange("p (k c) -> p k c", k=6),
                    aps["memC"][h].rearrange("(k p) c -> p k c", p=128), "hx",
                    waits=[("dve", tok)])
        a1 = P.tt(dsum[:, :], dsum[:, :], dload[:, 0:768], ALU.add, sem="dve",
                  waits=[lmf, ("dve", tok)])
        a2 = P.copy(memnew[:, sl * 768:(sl + 1) * 768], dsum[:, :], sem="dve",
                    waits=[("dve", a1), lm])
        bf_[("mem", sl)] = ("dve", a2)
        bf_[("dl", sl, 0)] = ("dve", a1)
        lq = P.dma("sync", mqb[:, sl * 6 * T:(sl + 1) * 6 * T]
                   .rearrange("p (k c) -> p k c", k=6),
                   aps["mq_in"][h].rearrange("k p c -> p k c"), "hx",
                   waits=[bf_.pop(("mq", sl))] if ("mq", sl) in bf_ else [])
        lg = P.dma("sync", gtb[:, sl * T:(sl + 1) * T], aps["gate_in"][h], "hx",
                   waits=[bf_.pop(("gt", sl))] if ("gt", sl) in bf_ else [])
        for half in range(2):
            bk = half
            wl = [lq, ("dve", a2)]
            if bk in bf_:
                wl.append(bf_.pop(bk))
            for fb in range(6):
                P.mm(pb[bk][:, :], memnew[:, sl * 768 + fb * 128: sl * 768 + (fb + 1) * 128],
                     mqb[:, sl * 6 * T + fb * T + half * 512: sl * 6 * T + fb * T + (half + 1) * 512],
                     start=(fb == 0), stop=(fb == 5),
                     waits=wl if fb == 0 else ())
            bk2 = 2 if half == 0 else 7
            wl2 = [nn_done, lq]
            if bk2 in bf_:
                wl2.append(bf_.pop(bk2))
            for fb in range(6):
                P.mm(pb[bk2][0:1, 0:512],
                     nnorm16[:, fb * 16 + h:fb * 16 + h + 1],
                     mqb[:, sl * 6 * T + fb * T + half * 512: sl * 6 * T + fb * T + (half + 1) * 512],
                     start=(fb == 0), stop=(fb == 5),
                     waits=wl2 if fb == 0 else (),
                     sem="pe" if fb == 5 else None)
        pcd = P.cnt["pe"]
        sq1 = P.act(sqr[0:1, :], aps_sq_row(P, aps, h), AF.Ln, sem="act",
                    waits=[("pe", pcd)]) if False else None
        lsq = P.dma("sync", sqr[0:1, :], aps["sqmq_in"][0:1, h * T:(h + 1) * T],
                    "hx", waits=[bf_.pop(("sq",))] if ("sq",) in bf_ else [])
        s2 = P.act(drow[0:1, :], sqr[0:1, :], AF.Sqrt, sem="act",
                   waits=[lsq, ("act", P.cnt.get("act", 0)),
                          ("dve", P.cnt.get("dve", 0))])
        bf_[("sq",)] = ("act", s2)
        d1 = P.stt(drow[0:1, :], drow[0:1, :], EPS, pb[2][0:1, 0:T], ALU.mult,
                   ALU.add, sem="dve", waits=[("act", s2), ("pe", pcd)]) if False else None
        dtoks = []
        for half in range(2):
            bk2 = 2 if half == 0 else 7
            dd = P.stt(drow[0:1, half * 512:(half + 1) * 512],
                       drow[0:1, half * 512:(half + 1) * 512], EPS,
                       pb[bk2][0:1, 0:512], ALU.mult, ALU.add,
                       sem="dve", waits=[("act", s2), ("pe", pcd),
                                         ("dve", P.cnt["dve"])])
            dtoks.append(dd)
            bf_[bk2] = ("dve", dd)
        r2 = P.copy(rrow16[0:1, :], drow[0:1, :], sem="dve",
                    waits=[("dve", dtoks[-1])])
        for half in range(2):
            bk3 = 3 + half % 2
            wl3 = [("dve", r2), lob]
            if bk3 in bf_:
                wl3.append(bf_.pop(bk3))
            bm = P.mm(pb[bk3][:, :], onesb[0:1, :],
                      rrow16[0:1, half * 512:(half + 1) * 512], start=True,
                      stop=True, sem="pe", waits=wl3)
            bc = P.recip(rcpB[:, half * 512:(half + 1) * 512], pb[bk3][:, :],
                         sem="dve", waits=[("pe", bm)])
            bf_[bk3] = ("dve", bc)
            m1 = P.tt(memh[:, half * 512:(half + 1) * 512], pb[half][:, :],
                      rcpB[:, half * 512:(half + 1) * 512], ALU.mult, sem="dve",
                      waits=[("dve", bc)])
            bf_[half] = ("dve", m1)
            m2 = P.tt(memh[:, half * 512:(half + 1) * 512],
                      memh[:, half * 512:(half + 1) * 512],
                      gtb[:, sl * T + half * 512: sl * T + (half + 1) * 512],
                      ALU.mult, sem="dve", waits=[("dve", m1), lg])
            m3 = P.tt(xT[:, h * T + half * 512: h * T + (half + 1) * 512],
                      xT[:, h * T + half * 512: h * T + (half + 1) * 512],
                      memh[:, half * 512:(half + 1) * 512], ALU.add, sem="dve",
                      waits=[("dve", m2), lx])
            hstore.append(m3)
        bf_[("mq", sl)] = ("pe", P.cnt["pe"])
        bf_[("gt", sl)] = ("dve", P.cnt["dve"])
    xdone = P.cnt["dve"]

    for tt in range(8):
        b, sr = tt // 4, (tt % 4) * 128
        for dk in range(DK):
            bk = 5 + (dk % 2)
            wl = [("dve", xdone), lid]
            if bk in bf_:
                wl.append(bf_.pop(bk))
            tc = P.transpose(pb[bk][:, 0:128], xT[:, dk * T + tt * 128: dk * T + (tt + 1) * 128],
                             ident[:, :], sem="pe", waits=wl)
            cc = P.copy(onat[:, dk * 128:(dk + 1) * 128], pb[bk][:, 0:128],
                        sem="dve", waits=[("pe", tc),
                                          bf_.pop(("on", dk % 2)) if ("on", dk % 2) in bf_ else ("dve", 0)])
            bf_[bk] = ("dve", cc)
        oc = P.cnt["dve"]
        st = P.dma("sync", aps["y_out"][b, sr:sr + 128, :], onat[:, :], "st",
                   waits=[("dve", oc)])
        bf_[("on", 0)] = st
        bf_[("on", 1)] = st
    P.wait_stream("sync", "st")
    P.wait_stream("gpsimd", "w")


from concourse.bass_utils import run_bass_kernel_spmd

NC = 8
SC = 512


def _consts():
    c = {}
    inv = 1.0 / 10000.0 ** (np.arange(0, 64, 2, np.float32) / 64)
    cs, sn = np.cos(inv), np.sin(inv)
    coscol = np.zeros(128, np.float32)
    sgnsin = np.zeros(128, np.float32)
    for p in range(128):
        i = p % 64
        j = i if i < 32 else i - 32
        coscol[p] = cs[j]
        sgnsin[p] = -sn[j] if i < 32 else sn[j]
    c["rotcs"] = np.ascontiguousarray(np.stack([coscol, sgnsin], 1))
    prot = np.zeros((128, 128), np.float32)
    for i in range(128):
        ii = i % 64
        prot[i, (i + 32) if ii < 32 else (i - 32)] = 1.0
    c["protc"] = prot.astype(ml_dtypes.bfloat16)
    bd = np.zeros((128, 8 * 16), np.float32)
    for tt in range(8):
        bd[0:64, tt * 16 + 2 * tt] = 1.0
        bd[64:128, tt * 16 + 2 * tt + 1] = 1.0
    c["bd64c"] = bd.astype(ml_dtypes.bfloat16)
    rollc = np.zeros((128, 6 * 128), np.float32)
    for j in (1, 2, 3):
        S = np.zeros((128, 128), np.float32)
        Tm = np.zeros((128, 128), np.float32)
        for i in range(128):
            if i >= j:
                S[i, i - j] = 1.0
            else:
                Tm[i, 128 - j + i] = 1.0
        rollc[:, (2 * (j - 1)) * 128:(2 * (j - 1) + 1) * 128] = S.T
        rollc[:, (2 * (j - 1) + 1) * 128:(2 * (j - 1) + 2) * 128] = Tm.T
    c["rollc"] = rollc.astype(ml_dtypes.bfloat16)
    return c


def _build_A():
    P = Prog()
    nc = P.nc
    aps = {}
    def di(name, shape, dt=F32):
        aps[name] = nc.dram_tensor(name, shape, dt, kind="ExternalInput").ap()
    def do(name, shape, dt=F32):
        aps[name] = nc.dram_tensor(name, shape, dt, kind="ExternalOutput").ap()
    di("xs", [2, SC, D]); di("wf", [D, D]); di("wi", [D, D]); di("wv", [D, D])
    di("wvt", [D, D]); di("rwA_in", [128, 32]); di("waux", [1, D])
    do("xt_out", [DK, 128, T]); do("xn_out", [DK, 128, T], BF16)
    do("f_out", [DK, 128, T]); do("g_out", [DK, 128, T], BF16)
    do("ap_out", [128, 64]); do("wvr_out", [128, 1024], BF16)
    emit_launchA(P, xs_ap=aps["xs"], wf_ap=aps["wf"], wi_ap=aps["wi"],
                 wv_ap=aps["wv"], xt_out=aps["xt_out"], xn_out=aps["xn_out"],
                 f_out=aps["f_out"], g_out=aps["g_out"], ap_out=aps["ap_out"],
                 wvt_ap=aps["wvt"], rw_ap=aps["rwA_in"], waux_ap=aps["waux"],
                 wvr_out=aps["wvr_out"])
    return P.finalize()


def _build_B():
    P = Prog()
    nc = P.nc
    aps = {}
    def di(name, shape, dt=F32):
        aps[name] = nc.dram_tensor(name, shape, dt, kind="ExternalInput").ap()
    def do(name, shape, dt=F32):
        aps[name] = nc.dram_tensor(name, shape, dt, kind="ExternalOutput").ap()
    di("xt_in", [DK, 128, T]); di("xn_in", [DK, 128, T], BF16)
    di("f_in", [DK, 128, T]); di("g_in", [DK, 128, T], BF16)
    di("apall", [128, 8, 64]); di("initsb_in", [128, 16]); di("selvec", [1, 8])
    for w in ("wq", "wro", "wup", "wgate", "wdown"):
        di(w, [D, D])
    di("wgmoe", [D, 16])
    di("w1t", [16, D, 1408]); di("w3t", [16, D, 1408]); di("w2t", [16, 1408, D])
    di("ws1t", [D, 2816]); di("ws3t", [D, 2816]); di("ws2t", [2816, D])
    di("convw", [128, 32]); di("rotcs", [128, 2])
    di("wqf", [D, 1024]); di("wkf", [D, 1024]); di("wvf", [D, D]); di("wgf", [D, D])
    di("wvr_in", [128, 1024], BF16); di("woutf", [D, D])
    di("protc", [128, 128], BF16); di("bd64c", [128, 8 * 16], BF16)
    di("wkT", [D, 512]); di("wvT", [D, 512]); di("wmbT", [D, D])
    di("wqT2", [D, D]); di("wgT2", [D, D])
    di("bk_in", [128, 4]); di("bv_in", [128, 4]); di("bmb_in", [128, 16])
    di("bq_in", [128, 16]); di("bg_in", [128, 16]); di("bkrow", [1, 512])
    di("memC", [16, 768, 128]); di("normT_in", [768, 16])
    di("rollc", [128, 6 * 128], BF16)
    do("dmem_out", [16, 768, 128]); do("dnorm_out", [768, 16])
    do("mq_out", [16, 6, 128, T], BF16); do("sqmq_out", [1, 16 * T])
    do("gate_out", [DK, 128, T], BF16); do("xpart_out", [DK, 128, T])

    env = emit_launchB0(P, aps)
    env["x1done"] = ("dve", P.cnt["dve"])
    env["h_guard2"] = ("pe", P.cnt["pe"])
    emit_LN1(P, env)
    emit_l1ffn(P, env, aps)
    emit_l1attn(P, env, aps)
    P.wait("sync", "dve", P.cnt["dve"])
    P.wait("sync", "pe", P.cnt["pe"])
    P.wait("sync", "act", P.cnt["act"])
    P.wait_stream("sync", "st")
    P.wait_stream("sync", "w")
    P.wait_stream("sync", "x")
    P.wait_stream("sync", "hx")
    return P.finalize()


def _build_C():
    P = Prog()
    nc = P.nc
    aps = {}
    def di(name, shape, dt=F32):
        aps[name] = nc.dram_tensor(name, shape, dt, kind="ExternalInput").ap()
    di("xpart_in", [DK, 128, T]); di("mq_in", [16, 6, 128, T], BF16)
    di("sqmq_in", [1, 16 * T]); di("gate_in", [DK, 128, T], BF16)
    di("dparts", [8, 16, 768, 128]); di("dnparts", [8, 768, 16])
    di("memC", [16, 768, 128]); di("normT_in", [768, 16])
    aps["y_out"] = nc.dram_tensor("y_out", [2, SC, D], F32,
                                  kind="ExternalOutput").ap()
    emit_launchC(P, aps)
    return P.finalize()


def kernel(**inp):
    co = _consts()
    ac = np.ascontiguousarray
    x = inp["L0_attn_x"]

    ncA = _build_A()
    wvt_h = ac(inp["L1_ffn_Wv"].T)
    rw_h = ac(inp["L1_ffn_router_w"].transpose(1, 0, 2).reshape(128, 32))
    waux_h = ac(inp["L0_attn_w_aux"][:1])
    mapsA = []
    for c in range(NC):
        mapsA.append({
            "xs": ac(x[:, SC * c:SC * (c + 1), :]),
            "wf": inp["L0_attn_w_f"], "wi": inp["L0_attn_w_i"],
            "wv": inp["L0_attn_w_v"],
            "wvt": wvt_h, "rwA_in": rw_h, "waux": waux_h,
        })
    rA = run_bass_kernel_spmd(ncA, mapsA, core_ids=list(range(NC)))
    oA = rA.results

    ncB = _build_B()
    apall = ac(np.stack([oA[c]["ap_out"] for c in range(NC)], axis=1))
    initsb = ac(inp["L0_attn_init_state"].reshape(16, 128).T)
    shared = {
        "apall": apall, "initsb_in": initsb,
        "wq": inp["L0_attn_w_q"], "wro": inp["L0_attn_w_ro"],
        "wup": inp["L0_attn_w_up"], "wgate": inp["L0_attn_w_gate"],
        "wdown": inp["L0_attn_w_down"],
        "wgmoe": ac(inp["L0_ffn_Wg"].T),
        "w1t": ac(inp["L0_ffn_W1"].transpose(0, 2, 1)),
        "w3t": ac(inp["L0_ffn_W3"].transpose(0, 2, 1)),
        "w2t": ac(inp["L0_ffn_W2"].transpose(0, 2, 1)),
        "ws1t": ac(inp["L0_ffn_Ws1"].T), "ws3t": ac(inp["L0_ffn_Ws3"].T),
        "ws2t": ac(inp["L0_ffn_Ws2"].T),
        "convw": ac(inp["L1_ffn_conv_w"].reshape(DK, 128, 2)
                    .transpose(1, 0, 2).reshape(128, 32)),
        "rotcs": co["rotcs"], "protc": co["protc"], "bd64c": co["bd64c"],
        "wqf": inp["L1_ffn_Wq"], "wkf": inp["L1_ffn_Wk"],
        "wvf": inp["L1_ffn_Wv"], "wgf": inp["L1_ffn_Wg"],
        "woutf": inp["L1_ffn_Wout"],
        "wkT": ac(inp["L1_attn_Wk"].T), "wvT": ac(inp["L1_attn_Wv"].T),
        "wmbT": ac(inp["L1_attn_Wmb"].T), "wqT2": ac(inp["L1_attn_Wq"].T),
        "wgT2": ac(inp["L1_attn_Wg"].T),
        "bk_in": ac(inp["L1_attn_bk"].reshape(4, 128).T),
        "bv_in": ac(inp["L1_attn_bv"].reshape(4, 128).T),
        "bmb_in": ac(inp["L1_attn_bmb"].reshape(16, 128).T),
        "bq_in": ac(inp["L1_attn_bq"].reshape(16, 128).T),
        "bg_in": ac(inp["L1_attn_bg"].reshape(16, 128).T),
        "bkrow": ac(inp["L1_attn_bk"][None, :]),
        "memC": inp["L1_attn_memory"],
        "normT_in": ac(inp["L1_attn_norm"].T),
        "rollc": co["rollc"],
    }
    mapsB = []
    for c in range(NC):
        sv = np.zeros((1, 8), np.float32)
        sv[0, c] = 1.0
        m = dict(shared)
        m.update({
            "xt_in": oA[c]["xt_out"], "xn_in": oA[c]["xn_out"],
            "f_in": oA[c]["f_out"], "g_in": oA[c]["g_out"],
            "wvr_in": oA[c]["wvr_out"], "selvec": sv,
        })
        mapsB.append(m)
    rB = run_bass_kernel_spmd(ncB, mapsB, core_ids=list(range(NC)))
    oB = rB.results

    ncC = _build_C()
    dparts = ac(np.stack([oB[c]["dmem_out"] for c in range(NC)], axis=0))
    dnparts = ac(np.stack([oB[c]["dnorm_out"] for c in range(NC)], axis=0))
    sharedC = {"dparts": dparts, "dnparts": dnparts,
               "memC": inp["L1_attn_memory"],
               "normT_in": ac(inp["L1_attn_norm"].T)}
    mapsC = []
    for c in range(NC):
        m = dict(sharedC)
        m.update({"xpart_in": oB[c]["xpart_out"], "mq_in": oB[c]["mq_out"],
                  "sqmq_in": oB[c]["sqmq_out"], "gate_in": oB[c]["gate_out"]})
        mapsC.append(m)
    rC = run_bass_kernel_spmd(ncC, mapsC, core_ids=list(range(NC)))
    oC = rC.results

    y = np.empty((2, 4096, D), np.float32)
    for c in range(NC):
        y[:, SC * c:SC * (c + 1), :] = oC[c]["y_out"]
    return y
